# version 18
# baseline (speedup 1.0000x reference)
import numpy as np
import ml_dtypes

import concourse.bass as bass
import concourse.mybir as mybir
from concourse.bass_utils import run_bass_kernel_spmd

F32 = mybir.dt.float32
BF16 = mybir.dt.bfloat16
OP = mybir.AluOpType
AF = mybir.ActivationFunctionType

T = 2048
D = 1024
NT = 16
NSC = 2
EPS_LN = 1e-5
EPS_RMS = 1.1920929e-07

ENGINES = ("sync", "gpsimd", "tensor", "vector", "scalar")

A_XNT = 0
A_ABCD = 4096
A_PK1 = 12288
A_PK2 = 16384
A_PK3 = 18432
A_Q = 19456
A_R = 19968
A_SIG = 20992
A_SCALEB = 21760
A_END = 23168
A_WPX = 4096
A_TL1 = 16384
A_TL2 = 18432
A_TL3 = 19456
A_TL4 = 19968
A_TDB = 20992
B_WF = 0
B_LN2T = 16384
B_H1T = 18432
B_H2T = 20480
B_TMP = 12288

COMB = (("a", "a", "a", "b", "c"), ("b", "a", "b", "b", "d"),
        ("c", "c", "a", "d", "c"), ("d", "c", "b", "d", "d"))
NAMES = ("a", "b", "c", "d")


class Plan:
    def __init__(self):
        self.ops = {e: [] for e in ENGINES}
        self.val = {}
        self.last_wait = {}
        self.since_drain = {e: 0 for e in ENGINES}

    def op(self, eng, fn, inc=None, amt=1):
        self.since_drain[eng] += 1
        if inc is not None:
            self.val[inc] = self.val.get(inc, 0) + amt
            v = self.val[inc]

            def wrapped(e, sems, fn=fn, inc=inc, amt=amt):
                fn(e, sems).then_inc(sems[inc], amt)

            self.ops[eng].append(wrapped)
            return (inc, v, eng)
        self.ops[eng].append(fn)
        return None

    def drain(self, eng):
        if self.since_drain[eng] == 0:
            return
        self.since_drain[eng] = 0
        self.ops[eng].append(lambda e, sems: e.drain())

    def wait(self, eng, tok):
        if tok is None:
            return
        sem, v, producer = tok
        if producer == eng:
            self.drain(eng)
            return
        key = (eng, sem)
        if self.last_wait.get(key, -1) >= v:
            return
        self.last_wait[key] = v

        def w(e, sems, sem=sem, v=v):
            e.wait_ge(sems[sem], v)

        self.ops[eng].append(w)

    def cur(self, sem, eng="?"):
        return (sem, self.val.get(sem, 0), eng)


def build_nc():
    nc = bass.Bass("TRN2", target_bir_lowering=False)

    x_d = nc.dram_tensor("x", [T, D], F32, kind="ExternalInput").ap()
    y_d = nc.dram_tensor("y", [T, D], F32, kind="ExternalOutput").ap()
    ws_d = nc.dram_tensor("w_scan", [128, 8, 8, 128], F32, kind="ExternalInput").ap()
    wp_d = nc.dram_tensor("w_proj", [128, 4, 1024], F32, kind="ExternalInput").ap()
    wf_d = nc.dram_tensor("w_ffn", [128, 3, 8, 1024], F32, kind="ExternalInput").ap()
    sb_d = nc.dram_tensor("scale_b", [128, 1024], F32, kind="ExternalInput").ap()
    bb_d = nc.dram_tensor("bias_b", [128, 1024], F32, kind="ExternalInput").ap()
    vi_d = nc.dram_tensor("v_init", [128, 2, 2], F32, kind="ExternalInput").ap()
    id_d = nc.dram_tensor("ident", [128, 128], BF16, kind="ExternalInput").ap()

    p = Plan()
    import contextlib
    stack = contextlib.ExitStack()
    with stack:
        arena_base = (nc._sbuf_addr_for_side("left") + 31) // 32 * 32
        arena = stack.enter_context(nc.sbuf_tensor("arena", [128, A_END], F32))
        x_sb = stack.enter_context(nc.sbuf_tensor("x_sb", [128, NT, D], F32))
        xn_sb = stack.enter_context(nc.sbuf_tensor("xn_sb", [128, 4, D], BF16))
        weh_base = (nc._sbuf_addr_for_side("left") + 31) // 32 * 32
        weh = stack.enter_context(nc.sbuf_tensor("weh", [128, 8, 8, 128], BF16))
        ln2T0 = nc.alloc_sbuf_tensor_at(
            "ln2T0", [128, 8, 512], BF16, offset=weh_base).ap()
        h1T0 = nc.alloc_sbuf_tensor_at(
            "h1T0", [128, 8, 512], BF16, offset=weh_base + 8192).ap()
        h2T0 = nc.alloc_sbuf_tensor_at(
            "h2T0", [128, 8, 512], BF16, offset=weh_base).ap()
        wpb = stack.enter_context(nc.sbuf_tensor("wpb", [128, 4, D], BF16))
        biasb = stack.enter_context(nc.sbuf_tensor("biasb", [128, D], F32))
        vinit = stack.enter_context(nc.sbuf_tensor("vinit", [128, 2, 2], F32))
        id_sb = stack.enter_context(nc.sbuf_tensor("id_sb", [128, 128], BF16))
        stats = stack.enter_context(nc.sbuf_tensor("stats", [128, NT, 2], F32))
        bn6 = stack.enter_context(nc.sbuf_tensor("bn6", [128, 8, 12], F32))
        st_t0 = stack.enter_context(nc.sbuf_tensor("st_t0", [128, NT], F32))
        st_s1 = stack.enter_context(nc.sbuf_tensor("st_s1", [128, NT], F32))
        st_s2 = stack.enter_context(nc.sbuf_tensor("st_s2", [128, NT], F32))
        st_r1 = stack.enter_context(nc.sbuf_tensor("st_r1", [128, NT], F32))
        st_rn = stack.enter_context(nc.sbuf_tensor("st_rn", [128, NT], F32))
        st_bxn = stack.enter_context(nc.sbuf_tensor("st_bxn", [128, NT], F32))
        st_sr = stack.enter_context(nc.sbuf_tensor("st_sr", [128, NT], F32))
        st_br = stack.enter_context(nc.sbuf_tensor("st_br", [128, NT], F32))
        hxybf_base = (nc._sbuf_addr_for_side("left") + 31) // 32 * 32
        hxybf = stack.enter_context(nc.sbuf_tensor("hxybf", [128, 2, 4, D], BF16))
        btmpv = nc.alloc_sbuf_tensor_at(
            "btmpv", [128, 4, 512], F32, offset=hxybf_base).ap()
        wf1v = None
        psum = [stack.enter_context(nc.psum_tensor(f"ps{i}", [128, 512], F32))
                for i in range(8)]
        sems = {}
        for s in ("sx", "sx2", "sx3", "sx4", "sw", "swf", "swf2", "swf3", "sy",
                  "t", "v", "s", "g"):
            sems[s] = stack.enter_context(nc.semaphore(f"sem_{s}"))

        def av(off, n):
            return arena[:, off:off + n]

        xnT = nc.alloc_sbuf_tensor_at(
            "xnT", [128, 8, 1024], BF16, offset=arena_base + A_XNT * 4).ap()
        abcd = av(A_ABCD, 8192).rearrange("p (i t) -> p i t", i=8)
        ntmp = av(A_PK1, 4096).rearrange("p (i t) -> p i t", i=4)
        PK1 = av(A_PK1, 4096).rearrange("p (i h t) -> p i h t", i=4, h=2)
        PK2 = av(A_PK2, 2048).rearrange("p (i h t) -> p i h t", i=4, h=2)
        PK3 = av(A_PK3, 1024).rearrange("p (i h t) -> p i h t", i=4, h=2)
        Qv = av(A_Q, 512).rearrange("p (i h t) -> p i h t", i=4, h=2)
        Rv = av(A_R, 1024).rearrange("p (b i h t) -> p b i h t", b=2, i=4, h=2)
        SIG = av(A_SIG, 768).rearrange("p (i h t) -> p i h t", i=6, h=2)
        TL1 = av(A_TL1, 4096).rearrange("p (i h t) -> p i h t", i=4, h=2)
        TL2 = av(A_TL2, 2048).rearrange("p (i h t) -> p i h t", i=4, h=2)
        TL3 = av(A_TL3, 1024).rearrange("p (i h t) -> p i h t", i=4, h=2)
        TL4 = av(A_TL4, 512).rearrange("p (i h t) -> p i h t", i=4, h=2)
        TDB = av(A_TDB, 504).rearrange("p (i t) -> p i t", i=4)
        hxy = av(A_PK1, 4096).rearrange("p (i t) -> p i t", i=4)
        wpx = av(A_WPX, 4096).rearrange("p (i t) -> p i t", i=4)
        scaleb = av(A_SCALEB, 1024)
        wf1v = nc.alloc_sbuf_tensor_at(
            "wf1v", [128, 8, 1024], BF16, offset=arena_base + A_PK1 * 4).ap()
        wfv = nc.alloc_sbuf_tensor_at(
            "wfv", [128, 3, 8, 1024], BF16, offset=arena_base + B_WF * 4).ap()
        ln2T = nc.alloc_sbuf_tensor_at(
            "ln2T", [128, 8, 512], BF16, offset=arena_base + B_LN2T * 4).ap()
        h1T = nc.alloc_sbuf_tensor_at(
            "h1T", [128, 8, 512], BF16, offset=arena_base + B_H1T * 4).ap()
        h2T = nc.alloc_sbuf_tensor_at(
            "h2T", [128, 8, 512], BF16, offset=arena_base + B_H2T * 4).ap()
        btmp = btmpv

        def vop(fn, inc=None):
            return p.op("vector", fn, inc=inc)

        def sop(fn, inc=None):
            return p.op("scalar", fn, inc=inc)

        TT = OP

        bank_state = [{"evac": None, "group": None} for _ in range(8)]
        bank_rr = [0]

        def mm_group(mms, evac_eng, evac_fn, deps=(), evac_deps=()):
            b = bank_rr[0] % 8
            bank_rr[0] += 1
            st = bank_state[b]
            if st["evac"] is not None:
                p.wait("tensor", st["evac"])
            for d in deps:
                p.wait("tensor", d)
            n = len(mms)
            for i, m in enumerate(mms):
                if isinstance(m, tuple):
                    m, mdep = m
                    p.wait("tensor", mdep)
                if i == n - 1:
                    tok = p.op("tensor", lambda e, sems, m=m, b=b: m(e, sems, psum[b]),
                               inc="t")
                else:
                    p.op("tensor", lambda e, sems, m=m, b=b: m(e, sems, psum[b]))
            st["group"] = tok
            p.wait(evac_eng, tok)
            for d in evac_deps:
                p.wait(evac_eng, d)
            esem = {"scalar": "s", "vector": "v"}[evac_eng]
            if not isinstance(evac_fn, (list, tuple)):
                evac_fn = [evac_fn]
            for fi, ef in enumerate(evac_fn):
                if fi == len(evac_fn) - 1:
                    etok = p.op(evac_eng, lambda e, sems, b=b, ef=ef:
                                ef(e, sems, psum[b]), inc=esem)
                else:
                    p.op(evac_eng, lambda e, sems, b=b, ef=ef: ef(e, sems, psum[b]))
                    p.drain(evac_eng)
            st["evac"] = etok
            return etok

        x_src = x_d.rearrange("(n p) d -> p n d", p=128)
        tok_x_q = []
        for q in range(4):
            tok_x_q.append(p.op("sync", lambda e, sems, q=q: e.dma_start(
                out=x_sb[:, 4 * q:4 * (q + 1), :], in_=x_src[:, 4 * q:4 * (q + 1), :]),
                inc=("sx", "sx2", "sx3", "sx4")[q], amt=16))

        def wdma(out_ap, in_ap):
            return p.op("gpsimd", lambda e, sems: e.dma_start(out=out_ap, in_=in_ap),
                        inc="sw", amt=16)

        wdma(weh[:, :, :, :], ws_d)
        wdma(wpx[:, :, :], wp_d)
        wdma(scaleb, sb_d)
        wdma(biasb[:, :], bb_d)
        wdma(vinit[:, :, :], vi_d)
        wdma(id_sb[:, :], id_d)
        tok_w = p.cur("sw", "gpsimd")

        p.wait("vector", tok_w)
        for i in range(4):
            tok_fold = vop(lambda e, sems, i=i: e.tensor_tensor(
                wpb[:, i, :], wpx[:, i, :], scaleb, TT.mult), inc="v")

        xn_slot_tok = [None] * 4
        half_tok = [None]
        pk1_free = [None]
        proj_tok = [None, None]
        ln2t_tok = [None]
        h1t_tok = [None]
        h2t_tok = [None]

        def combine(dst, L, R, tmps, extra_layer1=()):
            for (od, f1, f2, g1, g2) in COMB:
                vop(lambda e, sems, od=od, f1=f1, f2=f2: e.tensor_tensor(
                    dst[od], L[f1], R[f2], TT.mult))
                vop(lambda e, sems, od=od, g1=g1, g2=g2: e.tensor_tensor(
                    tmps[od], L[g1], R[g2], TT.mult))
            for fn in extra_layer1:
                vop(fn)
            p.drain("vector")
            for od in NAMES:
                vop(lambda e, sems, od=od: e.tensor_tensor(
                    dst[od], dst[od], tmps[od], TT.add))
            p.drain("vector")

        def pairs(viewmap):
            return {nm: viewmap[nm].rearrange("p (u two) -> p u two", two=2)
                    for nm in NAMES}

        stat_toks = []
        stat_toks2 = []
        apply_tok_j = [None] * NT
        for sc in range(NSC):
            j0 = sc * 8
            p.drain("vector")
            for jj in range(8):
                j = j0 + jj
                p.wait("vector", tok_x_q[j // 4])
                vop(lambda e, sems, j=j, jj=jj: e.bn_stats(
                    bn6[:, jj, 0:6], x_sb[:, j, 0:512]))
                vop(lambda e, sems, j=j, jj=jj: e.bn_stats(
                    bn6[:, jj, 6:12], x_sb[:, j, 512:1024]))
            p.drain("vector")
            for jj in range(8):
                vop(lambda e, sems, j=j0 + jj, jj=jj: e.bn_aggr(
                    stats[:, j, :], bn6[:, jj, 0:12]))
            p.drain("vector")
            sl = slice(j0, j0 + 8)
            mean = stats[:, sl, 0]
            var = stats[:, sl, 1]
            vop(lambda e, sems, sl=sl, var=var: e.tensor_scalar(
                st_t0[:, sl], var, EPS_LN, None, TT.add))
            tok_v0 = vop(lambda e, sems, sl=sl, var=var: e.tensor_scalar(
                st_s2[:, sl], var, 1.0 + EPS_RMS, EPS_LN * EPS_RMS, TT.mult, TT.add),
                inc="v")
            p.wait("scalar", tok_v0)
            sop(lambda e, sems, sl=sl: e.activation(
                st_s1[:, sl], st_t0[:, sl], AF.Ln))
            sop(lambda e, sems, sl=sl: e.activation(
                st_s2[:, sl], st_s2[:, sl], AF.Ln))
            p.drain("scalar")
            sop(lambda e, sems, sl=sl: e.activation(
                st_r1[:, sl], st_s1[:, sl], AF.Exp, scale=-0.5))
            tok_e = sop(lambda e, sems, sl=sl: e.activation(
                st_rn[:, sl], st_s2[:, sl], AF.Exp, scale=-0.5), inc="s")
            p.wait("vector", tok_e)
            tok_stat = vop(lambda e, sems, sl=sl, mean=mean: e.scalar_tensor_tensor(
                st_bxn[:, sl], mean, -1.0, st_rn[:, sl], TT.mult, TT.mult), inc="v")
            vop(lambda e, sems, sl=sl: e.tensor_scalar(
                st_sr[:, sl], st_r1[:, sl], 1.0, None, TT.add))
            tok_stat2 = vop(lambda e, sems, sl=sl, mean=mean: e.scalar_tensor_tensor(
                st_br[:, sl], mean, -1.0, st_r1[:, sl], TT.mult, TT.mult), inc="v")
            stat_toks.append(tok_stat)
            stat_toks2.append(tok_stat2)

        def abcd_chain(sc, extra_evac_deps, mid=None):
            j0 = sc * 8
            tr_tok = {}
            for cc in range(2):
                for k in range(4):
                    j = j0 + cc * 4 + k
                    p.wait("scalar", stat_toks[sc])
                    if xn_slot_tok[k] is not None:
                        p.wait("scalar", xn_slot_tok[k])
                    atok = sop(lambda e, sems, j=j, k=k: e.activation(
                        xn_sb[:, k, :], x_sb[:, j, :], AF.Identity,
                        bias=st_bxn[:, j:j + 1], scale=st_rn[:, j:j + 1]), inc="s")
                    apply_tok_j[j] = atok
                    for gh in range(2):
                        def mk_mm(k, g):
                            def m(e, sems, out):
                                return e.matmul(
                                    out[:, 128 * (g % 4):128 * (g % 4 + 1)],
                                    xn_sb[:, k, 128 * g:128 * (g + 1)],
                                    id_sb[:, :],
                                    start=(g % 4 == 0), stop=(g % 4 == 3))
                            return m
                        tok = mm_group(
                            [mk_mm(k, g) for g in range(4 * gh, 4 * gh + 4)],
                            "scalar",
                            lambda e, sems, bank, k=k, cc=cc, gh=gh: e.activation(
                                xnT[:, 4 * gh:4 * gh + 4,
                                    128 * (4 * cc + k):128 * (4 * cc + k + 1)],
                                bank[:, :], AF.Copy),
                            deps=[atok, tok_w])
                        tr_tok[(cc, gh)] = tok
                    xn_slot_tok[k] = bank_state[(bank_rr[0] - 1) % 8]["group"]

            def mm_i(i):
                for cc in range(2):
                    def mk_mm(g, i, cc):
                        def m(e, sems, out):
                            return e.matmul(out[:, :], weh[:, i, g, :],
                                            xnT[:, g, 512 * cc:512 * (cc + 1)],
                                            start=(g == 0), stop=(g == 7))
                        return m
                    mm_group(
                        [(mk_mm(g, i, cc), tr_tok[(cc, g // 4)]) for g in range(8)],
                        "scalar",
                        lambda e, sems, bank, i=i, cc=cc: e.activation(
                            abcd[:, i, 512 * cc:512 * (cc + 1)], bank[:, :], AF.Copy),
                        deps=[tok_w], evac_deps=list(extra_evac_deps))

            for i in (0, 2, 4, 6):
                mm_i(i)
            if mid is not None:
                mid()
            for i in (1, 3, 5, 7):
                mm_i(i)

        norm_mul_tok = [None]

        def norm_half(h, gate):
            A, B, C, Dt = (abcd[:, 0 + h, :], abcd[:, 2 + h, :],
                           abcd[:, 4 + h, :], abcd[:, 6 + h, :])
            p.drain("scalar")
            if gate is not None:
                p.wait("scalar", gate)
            for ti, M_ in enumerate((A, B, C, Dt)):
                tok_sq = sop(lambda e, sems, M_=M_, ti=ti: e.activation(
                    ntmp[:, ti, :], M_, AF.Square), inc="s")
            p.wait("vector", tok_sq)
            vop(lambda e, sems: e.tensor_tensor(
                ntmp[:, 0, :], ntmp[:, 0, :], ntmp[:, 1, :], TT.add))
            vop(lambda e, sems: e.tensor_tensor(
                ntmp[:, 2, :], ntmp[:, 2, :], ntmp[:, 3, :], TT.add))
            p.drain("vector")
            vop(lambda e, sems: e.tensor_tensor(
                ntmp[:, 0, :], ntmp[:, 0, :], ntmp[:, 2, :], TT.add))
            p.drain("vector")
            tok_clip = vop(lambda e, sems: e.tensor_scalar(
                ntmp[:, 0, :], ntmp[:, 0, :], 1e-12, None, TT.max), inc="v")
            p.wait("scalar", tok_clip)
            sop(lambda e, sems: e.activation(
                ntmp[:, 1, :], ntmp[:, 0, :], AF.Ln))
            p.drain("scalar")
            tok_s = sop(lambda e, sems: e.activation(
                ntmp[:, 0, :], ntmp[:, 1, :], AF.Exp, scale=-0.5), inc="s")
            p.wait("vector", tok_s)
            for mi, M_ in enumerate((A, B, C, Dt)):
                if mi == 3:
                    norm_mul_tok[0] = vop(lambda e, sems, M_=M_: e.tensor_tensor(
                        M_, M_, ntmp[:, 0, :], TT.mult), inc="v")
                else:
                    vop(lambda e, sems, M_=M_: e.tensor_tensor(
                        M_, M_, ntmp[:, 0, :], TT.mult))
            p.drain("vector")

        def scan_core(sc):
            def epack(n):
                out = {}
                for ei, nm in enumerate(NAMES):
                    out[nm] = abcd[:, 2 * ei:2 * ei + 2, :].rearrange(
                        "p h (u two) -> p h u two", two=2)
                return out

            pr = epack(512)
            combine({nm: PK1[:, ei, :, :] for ei, nm in enumerate(NAMES)},
                    {nm: pr[nm][:, :, :, 1] for nm in NAMES},
                    {nm: pr[nm][:, :, :, 0] for nm in NAMES},
                    {nm: TL1[:, ei, :, :] for ei, nm in enumerate(NAMES)})
            pr = {nm: PK1[:, ei, :, :].rearrange("p h (u two) -> p h u two", two=2)
                  for ei, nm in enumerate(NAMES)}
            combine({nm: PK2[:, ei, :, :] for ei, nm in enumerate(NAMES)},
                    {nm: pr[nm][:, :, :, 1] for nm in NAMES},
                    {nm: pr[nm][:, :, :, 0] for nm in NAMES},
                    {nm: TL2[:, ei, :, :] for ei, nm in enumerate(NAMES)})
            pr = {nm: PK2[:, ei, :, :].rearrange("p h (u two) -> p h u two", two=2)
                  for ei, nm in enumerate(NAMES)}
            combine({nm: PK3[:, ei, :, :] for ei, nm in enumerate(NAMES)},
                    {nm: pr[nm][:, :, :, 1] for nm in NAMES},
                    {nm: pr[nm][:, :, :, 0] for nm in NAMES},
                    {nm: TL3[:, ei, :, :] for ei, nm in enumerate(NAMES)})
            pr = {nm: PK3[:, ei, :, :].rearrange("p h (u two) -> p h u two", two=2)
                  for ei, nm in enumerate(NAMES)}
            combine({nm: Qv[:, ei, :, :] for ei, nm in enumerate(NAMES)},
                    {nm: pr[nm][:, :, :, 1] for nm in NAMES},
                    {nm: pr[nm][:, :, :, 0] for nm in NAMES},
                    {nm: TL4[:, ei, :, :] for ei, nm in enumerate(NAMES)})

            cur, nxt = 0, 1
            first = True
            s = 1
            while s < 64:
                srcv = {nm: (Qv[:, ei, :, :] if first else Rv[:, cur, ei, :, :])
                        for ei, nm in enumerate(NAMES)}
                dstv = {nm: Rv[:, nxt, ei, :, :] for ei, nm in enumerate(NAMES)}
                heads = [
                    (lambda e, sems, d=dstv[nm], sv=srcv[nm], s=s:
                     e.tensor_copy(d[:, :, 0:s], sv[:, :, 0:s])) for nm in NAMES]
                combine({nm: dstv[nm][:, :, s:64] for nm in NAMES},
                        {nm: srcv[nm][:, :, s:64] for nm in NAMES},
                        {nm: srcv[nm][:, :, 0:64 - s] for nm in NAMES},
                        {nm: TDB[:, ei, 0:2 * (64 - s)].rearrange(
                            "p (h t) -> p h t", h=2)
                         for ei, nm in enumerate(NAMES)},
                        extra_layer1=heads)
                cur, nxt = nxt, cur
                first = False
                s *= 2
            Rf = {nm: Rv[:, cur, ei, :, :] for ei, nm in enumerate(NAMES)}

            cxb = vinit[:, :, 0:1].broadcast_to([128, 2, 64])
            cyb = vinit[:, :, 1:2].broadcast_to([128, 2, 64])
            vop(lambda e, sems, Rf=Rf, cxb=cxb: e.tensor_tensor(
                SIG[:, 0, :, :], Rf["a"], cxb, TT.mult))
            vop(lambda e, sems, Rf=Rf, cyb=cyb: e.tensor_tensor(
                SIG[:, 4, :, :], Rf["b"], cyb, TT.mult))
            vop(lambda e, sems, Rf=Rf, cxb=cxb: e.tensor_tensor(
                SIG[:, 1, :, :], Rf["c"], cxb, TT.mult))
            vop(lambda e, sems, Rf=Rf, cyb=cyb: e.tensor_tensor(
                SIG[:, 5, :, :], Rf["d"], cyb, TT.mult))
            p.drain("vector")
            vop(lambda e, sems: e.tensor_tensor(
                SIG[:, 0, :, :], SIG[:, 0, :, :], SIG[:, 4, :, :], TT.add))
            vop(lambda e, sems: e.tensor_tensor(
                SIG[:, 1, :, :], SIG[:, 1, :, :], SIG[:, 5, :, :], TT.add))
            p.drain("vector")
            vop(lambda e, sems: e.tensor_copy(SIG[:, 2, :, 1:64], SIG[:, 0, :, 0:63]))
            vop(lambda e, sems: e.tensor_copy(SIG[:, 2, :, 0:1], vinit[:, :, 0:1]))
            vop(lambda e, sems: e.tensor_copy(SIG[:, 3, :, 1:64], SIG[:, 1, :, 0:63]))
            vop(lambda e, sems: e.tensor_copy(SIG[:, 3, :, 0:1], vinit[:, :, 1:2]))
            p.drain("vector")
            vop(lambda e, sems: e.tensor_copy(vinit[:, :, 0:1], SIG[:, 0, :, 63:64]))
            vop(lambda e, sems: e.tensor_copy(vinit[:, :, 1:2], SIG[:, 1, :, 63:64]))
            p.drain("vector")

            archx = av(A_PK2, 1152).rearrange("p (j h u) -> p j h u", j=9, h=2)
            archy = av(A_PK2 + 1152, 1152).rearrange("p (j h u) -> p j h u",
                                                     j=9, h=2)
            etmp1 = av(A_R, 1024)
            etmp2 = av(18688, 1024)
            PRV = {nm: PK1[:, ei, :, :].rearrange("p h (u j) -> p h u j", j=8)
                   for ei, nm in enumerate(NAMES)}
            vop(lambda e, sems: e.tensor_copy(archx[:, 0, :, :], SIG[:, 2, :, :]))
            vop(lambda e, sems: e.tensor_copy(archy[:, 0, :, :], SIG[:, 3, :, :]))
            p.drain("vector")
            for j in range(8):
                px = archx[:, j, :, :]
                py = archy[:, j, :, :]
                vop(lambda e, sems, j=j, px=px: e.tensor_tensor(
                    SIG[:, 4, :, :], PRV["a"][:, :, :, j], px, TT.mult))
                vop(lambda e, sems, j=j, px=px: e.tensor_tensor(
                    SIG[:, 5, :, :], PRV["c"][:, :, :, j], px, TT.mult))
                vop(lambda e, sems, j=j, py=py: e.tensor_tensor(
                    archx[:, j + 1, :, :], PRV["b"][:, :, :, j], py, TT.mult))
                vop(lambda e, sems, j=j, py=py: e.tensor_tensor(
                    archy[:, j + 1, :, :], PRV["d"][:, :, :, j], py, TT.mult))
                p.drain("vector")
                vop(lambda e, sems, j=j: e.tensor_tensor(
                    archx[:, j + 1, :, :], archx[:, j + 1, :, :],
                    SIG[:, 4, :, :], TT.add))
                if j == 7:
                    pk1_free[0] = vop(lambda e, sems, j=j: e.tensor_tensor(
                        archy[:, j + 1, :, :], archy[:, j + 1, :, :],
                        SIG[:, 5, :, :], TT.add), inc="v")
                else:
                    vop(lambda e, sems, j=j: e.tensor_tensor(
                        archy[:, j + 1, :, :], archy[:, j + 1, :, :],
                        SIG[:, 5, :, :], TT.add))
                p.drain("vector")
            bk = sc % 2
            HBx = hxybf[:, bk, 0:2, :].rearrange("p h (u r) -> p h u r", r=16)
            HBy = hxybf[:, bk, 2:4, :].rearrange("p h (u r) -> p h u r", r=16)
            arx = archx[:, 1:9, :, :].rearrange("p j h u -> p h u j")
            ary = archy[:, 1:9, :, :].rearrange("p j h u -> p h u j")
            vop(lambda e, sems: e.tensor_copy(HBx[:, :, :, 1:16:2], arx))
            vop(lambda e, sems: e.tensor_copy(HBy[:, :, :, 1:16:2], ary))
            Aev = {nm: abcd[:, 2 * ei:2 * ei + 2, :].rearrange(
                "p h (u r) -> p h u r", r=16)[:, :, :, 0:16:2]
                for ei, nm in enumerate(NAMES)}
            hpx = archx[:, 0:8, :, :].rearrange("p j h u -> p h u j")
            hpy = archy[:, 0:8, :, :].rearrange("p j h u -> p h u j")
            e1v = etmp1.rearrange("p (h u j) -> p h u j", h=2, u=64)
            e2v = etmp2.rearrange("p (h u j) -> p h u j", h=2, u=64)
            vop(lambda e, sems: e.tensor_tensor(e1v, Aev["a"], hpx, TT.mult))
            vop(lambda e, sems: e.tensor_tensor(e2v, Aev["b"], hpy, TT.mult))
            p.drain("vector")
            vop(lambda e, sems: e.tensor_tensor(
                HBx[:, :, :, 0:16:2], e1v, e2v, TT.add))
            p.drain("vector")
            vop(lambda e, sems: e.tensor_tensor(e1v, Aev["c"], hpx, TT.mult))
            vop(lambda e, sems: e.tensor_tensor(e2v, Aev["d"], hpy, TT.mult))
            p.drain("vector")
            half_tok[0] = vop(lambda e, sems: e.tensor_tensor(
                HBy[:, :, :, 0:16:2], e1v, e2v, TT.add), inc="v")
            p.drain("vector")

        def r_apply(sc):
            for jj in range(8):
                j = sc * 8 + jj
                p.wait("scalar", stat_toks2[sc])
                p.wait("scalar", apply_tok_j[j])
                apply_tok_j[j] = sop(lambda e, sems, j=j: e.activation(
                    x_sb[:, j, :], x_sb[:, j, :], AF.Identity,
                    bias=st_br[:, j:j + 1], scale=st_sr[:, j:j + 1]), inc="s")

        def proj_x2(sc, tok_dc, jlo=0, jhi=8):
            j0 = sc * 8
            for jj in range(jlo, jhi):
                j = j0 + jj
                for ec in range(2):
                    def mk_mm(i, jj, ec, bk=sc % 2):
                        def m(e, sems, out):
                            return e.matmul(out[:, :],
                                            hxybf[:, bk, i, 128 * jj:128 * (jj + 1)],
                                            wpb[:, i, 512 * ec:512 * (ec + 1)],
                                            start=(i == 0), stop=(i == 3))
                        return m

                    def ev2(e, sems, bank, j=j, ec=ec):
                        return e.tensor_tensor(
                            x_sb[:, j, 512 * ec:512 * (ec + 1)],
                            x_sb[:, j, 512 * ec:512 * (ec + 1)],
                            bank[:, :], TT.add)
                    mm_group([mk_mm(i, jj, ec) for i in range(2 * 2)],
                             "vector", ev2,
                             deps=[tok_dc, tok_fold],
                             evac_deps=[apply_tok_j[j]])
            proj_tok[sc] = bank_state[(bank_rr[0] - 1) % 8]["group"]

        btmp_tok = [None, None]
        bstat_toks = [None] * 4

        def b_stats(c):
            j0 = c * 4
            sl = slice(j0, j0 + 4)
            p.drain("vector")
            for k in range(4):
                j = j0 + k
                vop(lambda e, sems, j=j, k=k: e.bn_stats(
                    bn6[:, k, 0:6], x_sb[:, j, 0:512]))
                vop(lambda e, sems, j=j, k=k: e.bn_stats(
                    bn6[:, k, 6:12], x_sb[:, j, 512:1024]))
            p.drain("vector")
            for k in range(4):
                vop(lambda e, sems, j=j0 + k, k=k: e.bn_aggr(
                    stats[:, j, :], bn6[:, k, 0:12]))
            p.drain("vector")
            mean = stats[:, sl, 0]
            var = stats[:, sl, 1]
            tok_v0 = vop(lambda e, sems, sl=sl, var=var: e.tensor_scalar(
                st_t0[:, sl], var, EPS_LN, None, TT.add), inc="v")
            p.wait("scalar", tok_v0)
            sop(lambda e, sems, sl=sl: e.activation(
                st_s1[:, sl], st_t0[:, sl], AF.Ln))
            p.drain("scalar")
            tok_s0 = sop(lambda e, sems, sl=sl: e.activation(
                st_r1[:, sl], st_s1[:, sl], AF.Exp, scale=-0.5), inc="s")
            p.wait("vector", tok_s0)
            bstat_toks[c] = vop(lambda e, sems, sl=sl, mean=mean:
                                e.scalar_tensor_tensor(
                st_bxn[:, sl], mean, -1.0, st_r1[:, sl], TT.mult, TT.mult), inc="v")

        btr = {}
        bf1 = {}
        bf2 = {}

        def b_front(c):
            j0 = c * 4
            ln2Tv = ln2T0 if c == 0 else ln2T
            h1Tv = h1T0 if c == 0 else h1T
            front_gate = tok_mm1 if c == 0 else scan1_end
            apply_toks = []
            for k in range(4):
                j = j0 + k
                p.wait("scalar", bstat_toks[c])
                if xn_slot_tok[k] is not None:
                    p.wait("scalar", xn_slot_tok[k])
                tok = sop(lambda e, sems, j=j, k=k: e.activation(
                    xn_sb[:, k, :], x_sb[:, j, :], AF.Identity,
                    bias=st_bxn[:, j:j + 1], scale=st_r1[:, j:j + 1]), inc="s")
                apply_toks.append(tok)
            tr_toks = []
            for g in range(8):
                def mk_mm(k, g):
                    def m(e, sems, out):
                        return e.matmul(out[:, 128 * k:128 * (k + 1)],
                                        xn_sb[:, k, 128 * g:128 * (g + 1)],
                                        id_sb[:, :], start=(k == 0), stop=(k == 3))
                    return m
                evac_deps = [front_gate]
                if ln2t_tok[0] is not None:
                    evac_deps.append(ln2t_tok[0])
                tok = mm_group([mk_mm(k, g) for k in range(4)], "scalar",
                               lambda e, sems, bank, g=g, ln2Tv=ln2Tv:
                               e.activation(
                                   ln2Tv[:, g, :], bank[:, :], AF.Copy),
                               deps=apply_toks, evac_deps=evac_deps)
                tr_toks.append(tok)
            last_tr_group = bank_state[(bank_rr[0] - 1) % 8]["group"]
            for k in range(4):
                xn_slot_tok[k] = last_tr_group
            btr[c] = tr_toks

            f1_toks = []
            for E in range(8):
                def mk_mm(g, E, ln2Tv=ln2Tv):
                    def m(e, sems, out):
                        return e.matmul(out[:, :],
                                        wfv[:, 0, g, 128 * E:128 * (E + 1)],
                                        ln2Tv[:, g, :], start=(g == 0),
                                        stop=(g == 7))
                    return m
                evac_deps = [front_gate]
                if h1t_tok[0] is not None:
                    evac_deps.append(h1t_tok[0])
                tok = mm_group([(mk_mm(g, E), tr_toks[g]) for g in range(8)],
                               "scalar",
                               lambda e, sems, bank, E=E, h1Tv=h1Tv:
                               e.activation(
                                   h1Tv[:, E, :], bank[:, :], AF.Copy),
                               deps=[tok_wf1], evac_deps=evac_deps)
                f1_toks.append(tok)
            ln2t_tok[0] = bank_state[(bank_rr[0] - 1) % 8]["group"]
            bf1[c] = f1_toks

        def b_f2(c):
            h1Tv = h1T0 if c == 0 else h1T
            h2Tv = h2T0 if c == 0 else h2T
            f2_toks = []
            for E in range(8):
                def mk_mm(g, E, h1Tv=h1Tv):
                    def m(e, sems, out):
                        return e.matmul(out[:, :],
                                        wf1v[:, g, 128 * E:128 * (E + 1)],
                                        h1Tv[:, g, :], start=(g == 0),
                                        stop=(g == 7))
                    return m
                if c == 0:
                    evac_deps = [ln2t_tok[0]]
                else:
                    evac_deps = [scan1_end]
                if h2t_tok[0] is not None:
                    evac_deps.append(h2t_tok[0])
                tok = mm_group([(mk_mm(g, E), bf1[c][g]) for g in range(8)],
                               "scalar",
                               lambda e, sems, bank, E=E, h2Tv=h2Tv:
                               e.activation(
                                   h2Tv[:, E, :], bank[:, :], AF.Copy),
                               deps=[("swf2", 16, "gpsimd")],
                               evac_deps=evac_deps)
                f2_toks.append(tok)
            h1t_tok[0] = bank_state[(bank_rr[0] - 1) % 8]["group"]
            bf2[c] = f2_toks

        def b_f3(c):
            j0 = c * 4
            h2Tv = h2T0 if c == 0 else h2T
            for k in range(4):
                j = j0 + k
                y_toks = []
                for ec in range(2):
                    def mk_mm(g, k, ec, h2Tv=h2Tv):
                        def m(e, sems, out):
                            return e.matmul(out[:, :],
                                            h2Tv[:, g, 128 * k:128 * (k + 1)],
                                            wfv[:, 2, g, 512 * ec:512 * (ec + 1)],
                                            start=(g == 0), stop=(g == 7))
                        return m

                    def evac(e, sems, bank, ec=ec):
                        return e.tensor_tensor(
                            btmp[:, 2 * ec, :], bank[:, :],
                            biasb[:, 512 * ec:512 * (ec + 1)], TT.add)
                    if btmp_tok[ec] is not None:
                        evac_deps_b = [btmp_tok[ec]]
                    else:
                        evac_deps_b = [proj_tok[0]]
                    tok_b = mm_group([(mk_mm(g, k, ec), bf2[c][g])
                                      for g in range(8)],
                                     "vector", evac,
                                     deps=[("swf3", 16, "gpsimd")],
                                     evac_deps=evac_deps_b)
                    p.wait("scalar", tok_b)
                    tok_tanh = sop(lambda e, sems, ec=ec: e.activation(
                        btmp[:, 2 * ec + 1, :], btmp[:, 2 * ec, :], AF.Tanh),
                        inc="s")
                    btmp_tok[ec] = tok_tanh
                    p.wait("vector", tok_tanh)
                    tok_y = vop(lambda e, sems, j=j, ec=ec: e.tensor_tensor(
                        x_sb[:, j, 512 * ec:512 * (ec + 1)],
                        x_sb[:, j, 512 * ec:512 * (ec + 1)],
                        btmp[:, 2 * ec + 1, :], TT.add), inc="v")
                    y_toks.append(tok_y)
                    p.wait("scalar", tok_y)
                for tk in y_toks:
                    p.wait("sync", tk)
                p.op("sync", lambda e, sems, j=j: e.dma_start(
                    out=y_d[128 * j:128 * (j + 1), :], in_=x_sb[:, j, :]),
                    inc="sy", amt=16)
            h2t_tok[0] = bank_state[(bank_rr[0] - 1) % 8]["group"]


        abcd_chain(0, [tok_fold], mid=lambda: norm_half(0, None))
        norm_half(1, norm_mul_tok[0])
        scan_core(0)
        r_apply(0)
        scan0_end = half_tok[0]
        tok_dc0 = scan0_end
        abcd_chain(1, [scan0_end], mid=lambda: norm_half(0, tok_dc0))
        tok_mm1 = p.cur("t", "tensor")
        tok_abcd1_s = p.cur("s", "scalar")
        norm_half(1, norm_mul_tok[0])
        r_apply(1)
        p.wait("gpsimd", tok_mm1)
        p.wait("gpsimd", tok_abcd1_s)
        tok_wf1 = p.op("gpsimd", lambda e, sems: e.dma_start(
            out=wfv[:, 0, :, :], in_=wf_d[:, 0, :, :]), inc="swf", amt=16)
        proj_x2(0, tok_dc0, 0, 4)
        b_stats(0)
        proj_x2(0, tok_dc0, 4, 8)
        scan1_end = None
        b_front(0)
        scan_core(1)
        scan1_end = half_tok[0]
        p.wait("gpsimd", pk1_free[0])
        p.op("gpsimd", lambda e, sems: e.dma_start(
            out=wf1v[:, :, :], in_=wf_d[:, 1, :, :]), inc="swf2", amt=16)
        p.wait("gpsimd", scan1_end)
        tok_wf = p.op("gpsimd", lambda e, sems: e.dma_start(
            out=wfv[:, 2, :, :], in_=wf_d[:, 2, :, :]), inc="swf3", amt=16)

        tok_dc1 = scan1_end
        proj_x2(1, tok_dc1)
        b_stats(1)
        b_stats(2)
        b_stats(3)
        b_f2(0)
        b_front(1)
        b_f3(0)
        b_f2(1)
        b_front(2)
        b_f3(1)
        b_f2(2)
        b_front(3)
        b_f3(2)
        b_f2(3)
        b_f3(3)

        p.op("sync", lambda e, sems: e.wait_ge(sems["sy"], p.val["sy"]))

        with nc.Block() as block:
            @block.sync
            def _(e):
                for f in p.ops["sync"]:
                    f(e, sems)

            @block.gpsimd
            def _(e):
                for f in p.ops["gpsimd"]:
                    f(e, sems)

            @block.tensor
            def _(e):
                for f in p.ops["tensor"]:
                    f(e, sems)

            @block.vector
            def _(e):
                for f in p.ops["vector"]:
                    f(e, sems)

            @block.scalar
            def _(e):
                for f in p.ops["scalar"]:
                    f(e, sems)

    return nc


def make_weights(L0_attn_W_m, L0_attn_v_vec, L0_attn_W_proj, L0_attn_out_scale,
                 L0_ffn_weight, L0_ffn_aux_weight1, L0_ffn_aux_weight2,
                 L0_ffn_bias):
    f32 = np.float32
    Wm = np.asarray(L0_attn_W_m, f32)
    Wp = np.asarray(L0_attn_W_proj, f32)
    vv = np.asarray(L0_attn_v_vec, f32)
    osc = np.asarray(L0_attn_out_scale, f32)
    Wf = [np.asarray(w, f32) for w in
          (L0_ffn_weight, L0_ffn_aux_weight1, L0_ffn_aux_weight2)]
    bias = np.asarray(L0_ffn_bias, f32)

    Wm4 = Wm.reshape(256, 4, 1024)
    ws = np.empty((128, 8, 8, 128), f32)
    for e in range(4):
        for h in range(2):
            blk = Wm4[128 * h:128 * (h + 1), e, :]
            ws[:, e * 2 + h, :, :] = blk.T.reshape(8, 128, 128).transpose(1, 0, 2)
    wp = np.empty((128, 4, 1024), f32)
    for h in range(2):
        for comp in range(2):
            cols = 2 * (128 * h + np.arange(128)) + comp
            wp[:, 2 * comp + h, :] = Wp[:, cols].T
    wf = np.stack([w.reshape(8, 128, 1024).transpose(1, 0, 2) for w in Wf], axis=1)
    sb = np.ascontiguousarray(np.broadcast_to(osc, (128, 1024)))
    bb = np.ascontiguousarray(np.broadcast_to(bias, (128, 1024)))
    vi = np.empty((128, 2, 2), f32)
    for h in range(2):
        vi[:, h, 0] = vv[128 * h:128 * (h + 1), 0, 0]
        vi[:, h, 1] = vv[128 * h:128 * (h + 1), 1, 0]
    idm = np.eye(128, dtype=ml_dtypes.bfloat16)
    return {"w_scan": np.ascontiguousarray(ws), "w_proj": np.ascontiguousarray(wp),
            "w_ffn": np.ascontiguousarray(wf), "scale_b": sb, "bias_b": bb,
            "v_init": vi, "ident": idm}


def kernel(L0_attn_x, L0_attn_W_m, L0_attn_v_vec, L0_attn_W_proj,
           L0_attn_out_scale, L0_ffn_weight, L0_ffn_aux_weight1,
           L0_ffn_aux_weight2, L0_ffn_bias):
    x = np.asarray(L0_attn_x, np.float32)
    wmap = make_weights(L0_attn_W_m, L0_attn_v_vec, L0_attn_W_proj,
                        L0_attn_out_scale, L0_ffn_weight, L0_ffn_aux_weight1,
                        L0_ffn_aux_weight2, L0_ffn_bias)
    nc = build_nc()
    in_maps = []
    for i in range(8):
        m = dict(wmap)
        m["x"] = np.ascontiguousarray(x[i])
        in_maps.append(m)
    res = run_bass_kernel_spmd(nc, in_maps, core_ids=list(range(8)))
    out = np.stack([np.asarray(r["y"], np.float32) for r in res.results], axis=0)
    return out


# version 19
# speedup vs baseline: 1.2032x; 1.2032x over previous
import numpy as np
import ml_dtypes

import concourse.bass as bass
import concourse.mybir as mybir
from concourse.bass_utils import run_bass_kernel_spmd

F32 = mybir.dt.float32
BF16 = mybir.dt.bfloat16
OP = mybir.AluOpType
AF = mybir.ActivationFunctionType

T = 2048
D = 1024
NT = 16
NSC = 2
EPS_LN = 1e-5
EPS_RMS = 1.1920929e-07

ENGINES = ("sync", "gpsimd", "tensor", "vector", "scalar")

A_XNT = 0
A_ABCD = 4096
A_PK1 = 12288
A_PK2 = 16384
A_PK3 = 18432
A_Q = 19456
A_R = 19968
A_SIG = 20992
A_SCALEB = 21760
A_END = 23168
A_WPX = 4096
A_TL1 = 16384
A_TL2 = 18432
A_TL3 = 19456
A_TL4 = 19968
A_TDB = 20992
B_WF = 0
B_LN2T = 16384
B_H1T = 18432
B_H2T = 20480
B_TMP = 12288

COMB = (("a", "a", "a", "b", "c"), ("b", "a", "b", "b", "d"),
        ("c", "c", "a", "d", "c"), ("d", "c", "b", "d", "d"))
NAMES = ("a", "b", "c", "d")


class Plan:
    def __init__(self):
        self.ops = {e: [] for e in ENGINES}
        self.val = {}
        self.last_wait = {}
        self.since_drain = {e: 0 for e in ENGINES}

    def op(self, eng, fn, inc=None, amt=1):
        self.since_drain[eng] += 1
        if inc is not None:
            self.val[inc] = self.val.get(inc, 0) + amt
            v = self.val[inc]

            def wrapped(e, sems, fn=fn, inc=inc, amt=amt):
                fn(e, sems).then_inc(sems[inc], amt)

            self.ops[eng].append(wrapped)
            return (inc, v, eng)
        self.ops[eng].append(fn)
        return None

    def drain(self, eng):
        if self.since_drain[eng] == 0:
            return
        self.since_drain[eng] = 0
        self.ops[eng].append(lambda e, sems: e.drain())

    def wait(self, eng, tok):
        if tok is None:
            return
        sem, v, producer = tok
        if producer == eng:
            self.drain(eng)
            return
        key = (eng, sem)
        if self.last_wait.get(key, -1) >= v:
            return
        self.last_wait[key] = v

        def w(e, sems, sem=sem, v=v):
            e.wait_ge(sems[sem], v)

        self.ops[eng].append(w)

    def cur(self, sem, eng="?"):
        return (sem, self.val.get(sem, 0), eng)


def build_nc():
    nc = bass.Bass("TRN2", target_bir_lowering=False)

    x_d = nc.dram_tensor("x", [T, D], F32, kind="ExternalInput").ap()
    y_d = nc.dram_tensor("y", [T, D], F32, kind="ExternalOutput").ap()
    ws_d = nc.dram_tensor("w_scan", [128, 8, 8, 128], F32, kind="ExternalInput").ap()
    wp_d = nc.dram_tensor("w_proj", [128, 4, 1024], F32, kind="ExternalInput").ap()
    wf_d = nc.dram_tensor("w_ffn", [128, 3, 8, 1024], F32, kind="ExternalInput").ap()
    sb_d = nc.dram_tensor("scale_b", [128, 1024], F32, kind="ExternalInput").ap()
    bb_d = nc.dram_tensor("bias_b", [128, 1024], F32, kind="ExternalInput").ap()
    vi_d = nc.dram_tensor("v_init", [128, 2, 2], F32, kind="ExternalInput").ap()
    id_d = nc.dram_tensor("ident", [128, 128], BF16, kind="ExternalInput").ap()

    p = Plan()
    import contextlib
    stack = contextlib.ExitStack()
    with stack:
        arena_base = (nc._sbuf_addr_for_side("left") + 31) // 32 * 32
        arena = stack.enter_context(nc.sbuf_tensor("arena", [128, A_END], F32))
        x_sb = stack.enter_context(nc.sbuf_tensor("x_sb", [128, NT, D], F32))
        xn_sb = stack.enter_context(nc.sbuf_tensor("xn_sb", [128, 4, D], BF16))
        weh_base = (nc._sbuf_addr_for_side("left") + 31) // 32 * 32
        weh = stack.enter_context(nc.sbuf_tensor("weh", [128, 8, 8, 128], BF16))
        ln2T0 = nc.alloc_sbuf_tensor_at(
            "ln2T0", [128, 8, 512], BF16, offset=weh_base).ap()
        h1T0 = nc.alloc_sbuf_tensor_at(
            "h1T0", [128, 8, 512], BF16, offset=weh_base + 8192).ap()
        h2T0 = nc.alloc_sbuf_tensor_at(
            "h2T0", [128, 8, 512], BF16, offset=weh_base).ap()
        wpb = stack.enter_context(nc.sbuf_tensor("wpb", [128, 4, D], BF16))
        biasb = stack.enter_context(nc.sbuf_tensor("biasb", [128, D], F32))
        vinit = stack.enter_context(nc.sbuf_tensor("vinit", [128, 2, 2], F32))
        id_sb = stack.enter_context(nc.sbuf_tensor("id_sb", [128, 128], BF16))
        stats = stack.enter_context(nc.sbuf_tensor("stats", [128, NT, 2], F32))
        bn6 = stack.enter_context(nc.sbuf_tensor("bn6", [128, 8, 12], F32))
        st_t0 = stack.enter_context(nc.sbuf_tensor("st_t0", [128, NT], F32))
        st_s1 = stack.enter_context(nc.sbuf_tensor("st_s1", [128, NT], F32))
        st_s2 = stack.enter_context(nc.sbuf_tensor("st_s2", [128, NT], F32))
        st_r1 = stack.enter_context(nc.sbuf_tensor("st_r1", [128, NT], F32))
        st_rn = stack.enter_context(nc.sbuf_tensor("st_rn", [128, NT], F32))
        st_bxn = stack.enter_context(nc.sbuf_tensor("st_bxn", [128, NT], F32))
        st_sr = stack.enter_context(nc.sbuf_tensor("st_sr", [128, NT], F32))
        st_br = stack.enter_context(nc.sbuf_tensor("st_br", [128, NT], F32))
        hxybf_base = (nc._sbuf_addr_for_side("left") + 31) // 32 * 32
        hxybf = stack.enter_context(nc.sbuf_tensor("hxybf", [128, 2, 4, D], BF16))
        btmpv = nc.alloc_sbuf_tensor_at(
            "btmpv", [128, 4, 512], F32, offset=hxybf_base).ap()
        wf1v = None
        psum = [stack.enter_context(nc.psum_tensor(f"ps{i}", [128, 512], F32))
                for i in range(8)]
        sems = {}
        for s in ("sx", "sx2", "sx3", "sx4", "sw", "swf", "swf2", "swf3", "sy",
                  "t", "v", "s", "g"):
            sems[s] = stack.enter_context(nc.semaphore(f"sem_{s}"))

        def av(off, n):
            return arena[:, off:off + n]

        xnT = nc.alloc_sbuf_tensor_at(
            "xnT", [128, 8, 1024], BF16, offset=arena_base + A_XNT * 4).ap()
        abcd = av(A_ABCD, 8192).rearrange("p (i t) -> p i t", i=8)
        ntmp = av(A_PK1, 4096).rearrange("p (i t) -> p i t", i=4)
        PK1 = av(A_PK1, 4096).rearrange("p (i h t) -> p i h t", i=4, h=2)
        PK2 = av(A_PK2, 2048).rearrange("p (i h t) -> p i h t", i=4, h=2)
        PK3 = av(A_PK3, 1024).rearrange("p (i h t) -> p i h t", i=4, h=2)
        Qv = av(A_Q, 512).rearrange("p (i h t) -> p i h t", i=4, h=2)
        Rv = av(A_R, 1024).rearrange("p (b i h t) -> p b i h t", b=2, i=4, h=2)
        SIG = av(A_SIG, 768).rearrange("p (i h t) -> p i h t", i=6, h=2)
        TL1 = av(A_TL1, 4096).rearrange("p (i h t) -> p i h t", i=4, h=2)
        TL2 = av(A_TL2, 2048).rearrange("p (i h t) -> p i h t", i=4, h=2)
        TL3 = av(A_TL3, 1024).rearrange("p (i h t) -> p i h t", i=4, h=2)
        TL4 = av(A_TL4, 512).rearrange("p (i h t) -> p i h t", i=4, h=2)
        TDB = av(A_TDB, 504).rearrange("p (i t) -> p i t", i=4)
        hxy = av(A_PK1, 4096).rearrange("p (i t) -> p i t", i=4)
        wpx = av(A_WPX, 4096).rearrange("p (i t) -> p i t", i=4)
        scaleb = av(A_SCALEB, 1024)
        wf1v = nc.alloc_sbuf_tensor_at(
            "wf1v", [128, 8, 1024], BF16, offset=arena_base + A_PK1 * 4).ap()
        wfv = nc.alloc_sbuf_tensor_at(
            "wfv", [128, 3, 8, 1024], BF16, offset=arena_base + B_WF * 4).ap()
        ln2T = nc.alloc_sbuf_tensor_at(
            "ln2T", [128, 8, 512], BF16, offset=arena_base + B_LN2T * 4).ap()
        h1T = nc.alloc_sbuf_tensor_at(
            "h1T", [128, 8, 512], BF16, offset=arena_base + B_H1T * 4).ap()
        h2T = nc.alloc_sbuf_tensor_at(
            "h2T", [128, 8, 512], BF16, offset=arena_base + B_H2T * 4).ap()
        btmp = btmpv

        def vop(fn, inc=None):
            return p.op("vector", fn, inc=inc)

        def sop(fn, inc=None):
            return p.op("scalar", fn, inc=inc)

        TT = OP

        bank_state = [{"evac": None, "group": None} for _ in range(8)]
        bank_rr = [0]

        def mm_group(mms, evac_eng, evac_fn, deps=(), evac_deps=()):
            b = bank_rr[0] % 8
            bank_rr[0] += 1
            st = bank_state[b]
            if st["evac"] is not None:
                p.wait("tensor", st["evac"])
            for d in deps:
                p.wait("tensor", d)
            n = len(mms)
            for i, m in enumerate(mms):
                if isinstance(m, tuple):
                    m, mdep = m
                    p.wait("tensor", mdep)
                if i == n - 1:
                    tok = p.op("tensor", lambda e, sems, m=m, b=b: m(e, sems, psum[b]),
                               inc="t")
                else:
                    p.op("tensor", lambda e, sems, m=m, b=b: m(e, sems, psum[b]))
            st["group"] = tok
            p.wait(evac_eng, tok)
            for d in evac_deps:
                p.wait(evac_eng, d)
            esem = {"scalar": "s", "vector": "v"}[evac_eng]
            if not isinstance(evac_fn, (list, tuple)):
                evac_fn = [evac_fn]
            for fi, ef in enumerate(evac_fn):
                if fi == len(evac_fn) - 1:
                    etok = p.op(evac_eng, lambda e, sems, b=b, ef=ef:
                                ef(e, sems, psum[b]), inc=esem)
                else:
                    p.op(evac_eng, lambda e, sems, b=b, ef=ef: ef(e, sems, psum[b]))
                    p.drain(evac_eng)
            st["evac"] = etok
            return etok

        x_src = x_d.rearrange("(n p) d -> p n d", p=128)
        tok_x_q = []
        for q in range(4):
            tok_x_q.append(p.op("sync", lambda e, sems, q=q: e.dma_start(
                out=x_sb[:, 4 * q:4 * (q + 1), :], in_=x_src[:, 4 * q:4 * (q + 1), :]),
                inc=("sx", "sx2", "sx3", "sx4")[q], amt=16))

        def wdma(out_ap, in_ap):
            return p.op("gpsimd", lambda e, sems: e.dma_start(out=out_ap, in_=in_ap),
                        inc="sw", amt=16)

        wdma(weh[:, :, :, :], ws_d)
        wdma(wpx[:, :, :], wp_d)
        wdma(scaleb, sb_d)
        wdma(biasb[:, :], bb_d)
        wdma(vinit[:, :, :], vi_d)
        wdma(id_sb[:, :], id_d)
        tok_w = p.cur("sw", "gpsimd")

        p.wait("vector", tok_w)
        for i in range(4):
            tok_fold = vop(lambda e, sems, i=i: e.tensor_tensor(
                wpb[:, i, :], wpx[:, i, :], scaleb, TT.mult), inc="v")

        xn_slot_tok = [None] * 4
        half_tok = [None]
        pk1_free = [None]
        proj_tok = [None, None]
        ln2t_tok = [None]
        h1t_tok = [None]
        h2t_tok = [None]

        def combine(dst, L, R, tmps, extra_layer1=()):
            for (od, f1, f2, g1, g2) in COMB:
                vop(lambda e, sems, od=od, f1=f1, f2=f2: e.tensor_tensor(
                    dst[od], L[f1], R[f2], TT.mult))
                vop(lambda e, sems, od=od, g1=g1, g2=g2: e.tensor_tensor(
                    tmps[od], L[g1], R[g2], TT.mult))
            for fn in extra_layer1:
                vop(fn)
            p.drain("vector")
            for od in NAMES:
                vop(lambda e, sems, od=od: e.tensor_tensor(
                    dst[od], dst[od], tmps[od], TT.add))
            p.drain("vector")

        def pairs(viewmap):
            return {nm: viewmap[nm].rearrange("p (u two) -> p u two", two=2)
                    for nm in NAMES}

        stat_toks = []
        stat_toks2 = []
        apply_tok_j = [None] * NT
        for sc in range(NSC):
            j0 = sc * 8
            p.drain("vector")
            for jj in range(8):
                j = j0 + jj
                p.wait("vector", tok_x_q[j // 4])
                vop(lambda e, sems, j=j, jj=jj: e.bn_stats(
                    bn6[:, jj, 0:6], x_sb[:, j, 0:512]))
                vop(lambda e, sems, j=j, jj=jj: e.bn_stats(
                    bn6[:, jj, 6:12], x_sb[:, j, 512:1024]))
            p.drain("vector")
            for jj in range(8):
                vop(lambda e, sems, j=j0 + jj, jj=jj: e.bn_aggr(
                    stats[:, j, :], bn6[:, jj, 0:12]))
            p.drain("vector")
            sl = slice(j0, j0 + 8)
            mean = stats[:, sl, 0]
            var = stats[:, sl, 1]
            vop(lambda e, sems, sl=sl, var=var: e.tensor_scalar(
                st_t0[:, sl], var, EPS_LN, None, TT.add))
            tok_v0 = vop(lambda e, sems, sl=sl, var=var: e.tensor_scalar(
                st_s2[:, sl], var, 1.0 + EPS_RMS, EPS_LN * EPS_RMS, TT.mult, TT.add),
                inc="v")
            p.wait("scalar", tok_v0)
            sop(lambda e, sems, sl=sl: e.activation(
                st_s1[:, sl], st_t0[:, sl], AF.Ln))
            sop(lambda e, sems, sl=sl: e.activation(
                st_s2[:, sl], st_s2[:, sl], AF.Ln))
            p.drain("scalar")
            sop(lambda e, sems, sl=sl: e.activation(
                st_r1[:, sl], st_s1[:, sl], AF.Exp, scale=-0.5))
            tok_e = sop(lambda e, sems, sl=sl: e.activation(
                st_rn[:, sl], st_s2[:, sl], AF.Exp, scale=-0.5), inc="s")
            p.wait("vector", tok_e)
            tok_stat = vop(lambda e, sems, sl=sl, mean=mean: e.scalar_tensor_tensor(
                st_bxn[:, sl], mean, -1.0, st_rn[:, sl], TT.mult, TT.mult), inc="v")
            vop(lambda e, sems, sl=sl: e.tensor_scalar(
                st_sr[:, sl], st_r1[:, sl], 1.0, None, TT.add))
            tok_stat2 = vop(lambda e, sems, sl=sl, mean=mean: e.scalar_tensor_tensor(
                st_br[:, sl], mean, -1.0, st_r1[:, sl], TT.mult, TT.mult), inc="v")
            stat_toks.append(tok_stat)
            stat_toks2.append(tok_stat2)

        def abcd_chain(sc, extra_evac_deps, mid=None):
            j0 = sc * 8
            ee = "vector" if sc == 0 else "scalar"
            tr_tok = {}
            for cc in range(2):
                for k in range(4):
                    j = j0 + cc * 4 + k
                    p.wait("scalar", stat_toks[sc])
                    if xn_slot_tok[k] is not None:
                        p.wait("scalar", xn_slot_tok[k])
                    atok = sop(lambda e, sems, j=j, k=k: e.activation(
                        xn_sb[:, k, :], x_sb[:, j, :], AF.Identity,
                        bias=st_bxn[:, j:j + 1], scale=st_rn[:, j:j + 1]), inc="s")
                    apply_tok_j[j] = atok
                    for gh in range(2):
                        def mk_mm(k, g):
                            def m(e, sems, out):
                                return e.matmul(
                                    out[:, 128 * (g % 4):128 * (g % 4 + 1)],
                                    xn_sb[:, k, 128 * g:128 * (g + 1)],
                                    id_sb[:, :],
                                    start=(g % 4 == 0), stop=(g % 4 == 3))
                            return m
                        def tr_evac(e, sems, bank, k=k, cc=cc, gh=gh):
                            dst = xnT[:, 4 * gh:4 * gh + 4,
                                      128 * (4 * cc + k):128 * (4 * cc + k + 1)]
                            if sc == 0:
                                return e.tensor_copy(dst, bank[:, :])
                            return e.activation(dst, bank[:, :], AF.Copy)
                        tok = mm_group(
                            [mk_mm(k, g) for g in range(4 * gh, 4 * gh + 4)],
                            ee, tr_evac, deps=[atok, tok_w])
                        tr_tok[(cc, gh)] = tok
                    xn_slot_tok[k] = bank_state[(bank_rr[0] - 1) % 8]["group"]

            def mm_i(i):
                for cc in range(2):
                    def mk_mm(g, i, cc):
                        def m(e, sems, out):
                            return e.matmul(out[:, :], weh[:, i, g, :],
                                            xnT[:, g, 512 * cc:512 * (cc + 1)],
                                            start=(g == 0), stop=(g == 7))
                        return m
                    def ab_evac(e, sems, bank, i=i, cc=cc):
                        dst = abcd[:, i, 512 * cc:512 * (cc + 1)]
                        if sc == 0:
                            return e.tensor_copy(dst, bank[:, :])
                        return e.activation(dst, bank[:, :], AF.Copy)
                    mm_group(
                        [(mk_mm(g, i, cc), tr_tok[(cc, g // 4)]) for g in range(8)],
                        ee, ab_evac,
                        deps=[tok_w], evac_deps=list(extra_evac_deps))

            for i in (0, 2, 4, 6):
                mm_i(i)
            if mid is not None:
                mid()
            for i in (1, 3, 5, 7):
                mm_i(i)

        norm_mul_tok = [None]

        def norm_half(h, gate):
            A, B, C, Dt = (abcd[:, 0 + h, :], abcd[:, 2 + h, :],
                           abcd[:, 4 + h, :], abcd[:, 6 + h, :])
            p.drain("scalar")
            if gate is not None:
                p.wait("scalar", gate)
            for ti, M_ in enumerate((A, B, C, Dt)):
                tok_sq = sop(lambda e, sems, M_=M_, ti=ti: e.activation(
                    ntmp[:, ti, :], M_, AF.Square), inc="s")
            p.wait("vector", tok_sq)
            vop(lambda e, sems: e.tensor_tensor(
                ntmp[:, 0, :], ntmp[:, 0, :], ntmp[:, 1, :], TT.add))
            vop(lambda e, sems: e.tensor_tensor(
                ntmp[:, 2, :], ntmp[:, 2, :], ntmp[:, 3, :], TT.add))
            p.drain("vector")
            vop(lambda e, sems: e.tensor_tensor(
                ntmp[:, 0, :], ntmp[:, 0, :], ntmp[:, 2, :], TT.add))
            p.drain("vector")
            tok_clip = vop(lambda e, sems: e.tensor_scalar(
                ntmp[:, 0, :], ntmp[:, 0, :], 1e-12, None, TT.max), inc="v")
            p.wait("scalar", tok_clip)
            sop(lambda e, sems: e.activation(
                ntmp[:, 1, :], ntmp[:, 0, :], AF.Ln))
            p.drain("scalar")
            tok_s = sop(lambda e, sems: e.activation(
                ntmp[:, 0, :], ntmp[:, 1, :], AF.Exp, scale=-0.5), inc="s")
            p.wait("vector", tok_s)
            for mi, M_ in enumerate((A, B, C, Dt)):
                if mi == 3:
                    norm_mul_tok[0] = vop(lambda e, sems, M_=M_: e.tensor_tensor(
                        M_, M_, ntmp[:, 0, :], TT.mult), inc="v")
                else:
                    vop(lambda e, sems, M_=M_: e.tensor_tensor(
                        M_, M_, ntmp[:, 0, :], TT.mult))
            p.drain("vector")

        def scan_core(sc):
            def epack(n):
                out = {}
                for ei, nm in enumerate(NAMES):
                    out[nm] = abcd[:, 2 * ei:2 * ei + 2, :].rearrange(
                        "p h (u two) -> p h u two", two=2)
                return out

            pr = epack(512)
            combine({nm: PK1[:, ei, :, :] for ei, nm in enumerate(NAMES)},
                    {nm: pr[nm][:, :, :, 1] for nm in NAMES},
                    {nm: pr[nm][:, :, :, 0] for nm in NAMES},
                    {nm: TL1[:, ei, :, :] for ei, nm in enumerate(NAMES)})
            pr = {nm: PK1[:, ei, :, :].rearrange("p h (u two) -> p h u two", two=2)
                  for ei, nm in enumerate(NAMES)}
            combine({nm: PK2[:, ei, :, :] for ei, nm in enumerate(NAMES)},
                    {nm: pr[nm][:, :, :, 1] for nm in NAMES},
                    {nm: pr[nm][:, :, :, 0] for nm in NAMES},
                    {nm: TL2[:, ei, :, :] for ei, nm in enumerate(NAMES)})
            pr = {nm: PK2[:, ei, :, :].rearrange("p h (u two) -> p h u two", two=2)
                  for ei, nm in enumerate(NAMES)}
            combine({nm: PK3[:, ei, :, :] for ei, nm in enumerate(NAMES)},
                    {nm: pr[nm][:, :, :, 1] for nm in NAMES},
                    {nm: pr[nm][:, :, :, 0] for nm in NAMES},
                    {nm: TL3[:, ei, :, :] for ei, nm in enumerate(NAMES)})
            pr = {nm: PK3[:, ei, :, :].rearrange("p h (u two) -> p h u two", two=2)
                  for ei, nm in enumerate(NAMES)}
            combine({nm: Qv[:, ei, :, :] for ei, nm in enumerate(NAMES)},
                    {nm: pr[nm][:, :, :, 1] for nm in NAMES},
                    {nm: pr[nm][:, :, :, 0] for nm in NAMES},
                    {nm: TL4[:, ei, :, :] for ei, nm in enumerate(NAMES)})

            cur, nxt = 0, 1
            first = True
            s = 1
            while s < 64:
                srcv = {nm: (Qv[:, ei, :, :] if first else Rv[:, cur, ei, :, :])
                        for ei, nm in enumerate(NAMES)}
                dstv = {nm: Rv[:, nxt, ei, :, :] for ei, nm in enumerate(NAMES)}
                heads = [
                    (lambda e, sems, d=dstv[nm], sv=srcv[nm], s=s:
                     e.tensor_copy(d[:, :, 0:s], sv[:, :, 0:s])) for nm in NAMES]
                combine({nm: dstv[nm][:, :, s:64] for nm in NAMES},
                        {nm: srcv[nm][:, :, s:64] for nm in NAMES},
                        {nm: srcv[nm][:, :, 0:64 - s] for nm in NAMES},
                        {nm: TDB[:, ei, 0:2 * (64 - s)].rearrange(
                            "p (h t) -> p h t", h=2)
                         for ei, nm in enumerate(NAMES)},
                        extra_layer1=heads)
                cur, nxt = nxt, cur
                first = False
                s *= 2
            Rf = {nm: Rv[:, cur, ei, :, :] for ei, nm in enumerate(NAMES)}

            cxb = vinit[:, :, 0:1].broadcast_to([128, 2, 64])
            cyb = vinit[:, :, 1:2].broadcast_to([128, 2, 64])
            vop(lambda e, sems, Rf=Rf, cxb=cxb: e.tensor_tensor(
                SIG[:, 0, :, :], Rf["a"], cxb, TT.mult))
            vop(lambda e, sems, Rf=Rf, cyb=cyb: e.tensor_tensor(
                SIG[:, 4, :, :], Rf["b"], cyb, TT.mult))
            vop(lambda e, sems, Rf=Rf, cxb=cxb: e.tensor_tensor(
                SIG[:, 1, :, :], Rf["c"], cxb, TT.mult))
            vop(lambda e, sems, Rf=Rf, cyb=cyb: e.tensor_tensor(
                SIG[:, 5, :, :], Rf["d"], cyb, TT.mult))
            p.drain("vector")
            vop(lambda e, sems: e.tensor_tensor(
                SIG[:, 0, :, :], SIG[:, 0, :, :], SIG[:, 4, :, :], TT.add))
            vop(lambda e, sems: e.tensor_tensor(
                SIG[:, 1, :, :], SIG[:, 1, :, :], SIG[:, 5, :, :], TT.add))
            p.drain("vector")
            vop(lambda e, sems: e.tensor_copy(SIG[:, 2, :, 1:64], SIG[:, 0, :, 0:63]))
            vop(lambda e, sems: e.tensor_copy(SIG[:, 2, :, 0:1], vinit[:, :, 0:1]))
            vop(lambda e, sems: e.tensor_copy(SIG[:, 3, :, 1:64], SIG[:, 1, :, 0:63]))
            vop(lambda e, sems: e.tensor_copy(SIG[:, 3, :, 0:1], vinit[:, :, 1:2]))
            p.drain("vector")
            vop(lambda e, sems: e.tensor_copy(vinit[:, :, 0:1], SIG[:, 0, :, 63:64]))
            vop(lambda e, sems: e.tensor_copy(vinit[:, :, 1:2], SIG[:, 1, :, 63:64]))
            p.drain("vector")

            archx = av(A_PK2, 1152).rearrange("p (j h u) -> p j h u", j=9, h=2)
            archy = av(A_PK2 + 1152, 1152).rearrange("p (j h u) -> p j h u",
                                                     j=9, h=2)
            etmp1 = av(A_R, 1024)
            etmp2 = av(18688, 1024)
            PRV = {nm: PK1[:, ei, :, :].rearrange("p h (u j) -> p h u j", j=8)
                   for ei, nm in enumerate(NAMES)}
            vop(lambda e, sems: e.tensor_copy(archx[:, 0, :, :], SIG[:, 2, :, :]))
            vop(lambda e, sems: e.tensor_copy(archy[:, 0, :, :], SIG[:, 3, :, :]))
            p.drain("vector")
            for j in range(8):
                px = archx[:, j, :, :]
                py = archy[:, j, :, :]
                vop(lambda e, sems, j=j, px=px: e.tensor_tensor(
                    SIG[:, 4, :, :], PRV["a"][:, :, :, j], px, TT.mult))
                vop(lambda e, sems, j=j, px=px: e.tensor_tensor(
                    SIG[:, 5, :, :], PRV["c"][:, :, :, j], px, TT.mult))
                vop(lambda e, sems, j=j, py=py: e.tensor_tensor(
                    archx[:, j + 1, :, :], PRV["b"][:, :, :, j], py, TT.mult))
                vop(lambda e, sems, j=j, py=py: e.tensor_tensor(
                    archy[:, j + 1, :, :], PRV["d"][:, :, :, j], py, TT.mult))
                p.drain("vector")
                vop(lambda e, sems, j=j: e.tensor_tensor(
                    archx[:, j + 1, :, :], archx[:, j + 1, :, :],
                    SIG[:, 4, :, :], TT.add))
                if j == 7:
                    pk1_free[0] = vop(lambda e, sems, j=j: e.tensor_tensor(
                        archy[:, j + 1, :, :], archy[:, j + 1, :, :],
                        SIG[:, 5, :, :], TT.add), inc="v")
                else:
                    vop(lambda e, sems, j=j: e.tensor_tensor(
                        archy[:, j + 1, :, :], archy[:, j + 1, :, :],
                        SIG[:, 5, :, :], TT.add))
                p.drain("vector")
            bk = sc % 2
            HBx = hxybf[:, bk, 0:2, :].rearrange("p h (u r) -> p h u r", r=16)
            HBy = hxybf[:, bk, 2:4, :].rearrange("p h (u r) -> p h u r", r=16)
            arx = archx[:, 1:9, :, :].rearrange("p j h u -> p h u j")
            ary = archy[:, 1:9, :, :].rearrange("p j h u -> p h u j")
            vop(lambda e, sems: e.tensor_copy(HBx[:, :, :, 1:16:2], arx))
            vop(lambda e, sems: e.tensor_copy(HBy[:, :, :, 1:16:2], ary))
            Aev = {nm: abcd[:, 2 * ei:2 * ei + 2, :].rearrange(
                "p h (u r) -> p h u r", r=16)[:, :, :, 0:16:2]
                for ei, nm in enumerate(NAMES)}
            hpx = archx[:, 0:8, :, :].rearrange("p j h u -> p h u j")
            hpy = archy[:, 0:8, :, :].rearrange("p j h u -> p h u j")
            e1v = etmp1.rearrange("p (h u j) -> p h u j", h=2, u=64)
            e2v = etmp2.rearrange("p (h u j) -> p h u j", h=2, u=64)
            vop(lambda e, sems: e.tensor_tensor(e1v, Aev["a"], hpx, TT.mult))
            vop(lambda e, sems: e.tensor_tensor(e2v, Aev["b"], hpy, TT.mult))
            p.drain("vector")
            vop(lambda e, sems: e.tensor_tensor(
                HBx[:, :, :, 0:16:2], e1v, e2v, TT.add))
            p.drain("vector")
            vop(lambda e, sems: e.tensor_tensor(e1v, Aev["c"], hpx, TT.mult))
            vop(lambda e, sems: e.tensor_tensor(e2v, Aev["d"], hpy, TT.mult))
            p.drain("vector")
            half_tok[0] = vop(lambda e, sems: e.tensor_tensor(
                HBy[:, :, :, 0:16:2], e1v, e2v, TT.add), inc="v")
            p.drain("vector")

        def r_apply(sc):
            for jj in range(8):
                j = sc * 8 + jj
                p.wait("scalar", stat_toks2[sc])
                p.wait("scalar", apply_tok_j[j])
                apply_tok_j[j] = sop(lambda e, sems, j=j: e.activation(
                    x_sb[:, j, :], x_sb[:, j, :], AF.Identity,
                    bias=st_br[:, j:j + 1], scale=st_sr[:, j:j + 1]), inc="s")

        def proj_x2(sc, tok_dc, jlo=0, jhi=8):
            j0 = sc * 8
            for jj in range(jlo, jhi):
                j = j0 + jj
                for ec in range(2):
                    def mk_mm(i, jj, ec, bk=sc % 2):
                        def m(e, sems, out):
                            return e.matmul(out[:, :],
                                            hxybf[:, bk, i, 128 * jj:128 * (jj + 1)],
                                            wpb[:, i, 512 * ec:512 * (ec + 1)],
                                            start=(i == 0), stop=(i == 3))
                        return m

                    def ev2(e, sems, bank, j=j, ec=ec):
                        return e.tensor_tensor(
                            x_sb[:, j, 512 * ec:512 * (ec + 1)],
                            x_sb[:, j, 512 * ec:512 * (ec + 1)],
                            bank[:, :], TT.add)
                    mm_group([mk_mm(i, jj, ec) for i in range(2 * 2)],
                             "vector", ev2,
                             deps=[tok_dc, tok_fold],
                             evac_deps=[apply_tok_j[j]])
            proj_tok[sc] = bank_state[(bank_rr[0] - 1) % 8]["group"]

        btmp_tok = [None, None]
        bstat_toks = [None] * 4

        def b_stats(c):
            j0 = c * 4
            sl = slice(j0, j0 + 4)
            p.drain("vector")
            for k in range(4):
                j = j0 + k
                vop(lambda e, sems, j=j, k=k: e.bn_stats(
                    bn6[:, k, 0:6], x_sb[:, j, 0:512]))
                vop(lambda e, sems, j=j, k=k: e.bn_stats(
                    bn6[:, k, 6:12], x_sb[:, j, 512:1024]))
            p.drain("vector")
            for k in range(4):
                vop(lambda e, sems, j=j0 + k, k=k: e.bn_aggr(
                    stats[:, j, :], bn6[:, k, 0:12]))
            p.drain("vector")
            mean = stats[:, sl, 0]
            var = stats[:, sl, 1]
            tok_v0 = vop(lambda e, sems, sl=sl, var=var: e.tensor_scalar(
                st_t0[:, sl], var, EPS_LN, None, TT.add), inc="v")
            p.wait("scalar", tok_v0)
            sop(lambda e, sems, sl=sl: e.activation(
                st_s1[:, sl], st_t0[:, sl], AF.Ln))
            p.drain("scalar")
            tok_s0 = sop(lambda e, sems, sl=sl: e.activation(
                st_r1[:, sl], st_s1[:, sl], AF.Exp, scale=-0.5), inc="s")
            p.wait("vector", tok_s0)
            bstat_toks[c] = vop(lambda e, sems, sl=sl, mean=mean:
                                e.scalar_tensor_tensor(
                st_bxn[:, sl], mean, -1.0, st_r1[:, sl], TT.mult, TT.mult), inc="v")

        btr = {}
        bf1 = {}
        bf2 = {}

        def b_front(c):
            j0 = c * 4
            ln2Tv = ln2T0 if c == 0 else ln2T
            h1Tv = h1T0 if c == 0 else h1T
            front_gate = tok_mm1 if c == 0 else scan1_end
            apply_toks = []
            for k in range(4):
                j = j0 + k
                p.wait("scalar", bstat_toks[c])
                if xn_slot_tok[k] is not None:
                    p.wait("scalar", xn_slot_tok[k])
                tok = sop(lambda e, sems, j=j, k=k: e.activation(
                    xn_sb[:, k, :], x_sb[:, j, :], AF.Identity,
                    bias=st_bxn[:, j:j + 1], scale=st_r1[:, j:j + 1]), inc="s")
                apply_toks.append(tok)
            tr_toks = []
            for g in range(8):
                def mk_mm(k, g):
                    def m(e, sems, out):
                        return e.matmul(out[:, 128 * k:128 * (k + 1)],
                                        xn_sb[:, k, 128 * g:128 * (g + 1)],
                                        id_sb[:, :], start=(k == 0), stop=(k == 3))
                    return m
                evac_deps = [front_gate]
                if ln2t_tok[0] is not None:
                    evac_deps.append(ln2t_tok[0])
                tok = mm_group([mk_mm(k, g) for k in range(4)], "scalar",
                               lambda e, sems, bank, g=g, ln2Tv=ln2Tv:
                               e.activation(
                                   ln2Tv[:, g, :], bank[:, :], AF.Copy),
                               deps=apply_toks, evac_deps=evac_deps)
                tr_toks.append(tok)
            last_tr_group = bank_state[(bank_rr[0] - 1) % 8]["group"]
            for k in range(4):
                xn_slot_tok[k] = last_tr_group
            btr[c] = tr_toks

            f1_toks = []
            for E in range(8):
                def mk_mm(g, E, ln2Tv=ln2Tv):
                    def m(e, sems, out):
                        return e.matmul(out[:, :],
                                        wfv[:, 0, g, 128 * E:128 * (E + 1)],
                                        ln2Tv[:, g, :], start=(g == 0),
                                        stop=(g == 7))
                    return m
                evac_deps = [front_gate]
                if h1t_tok[0] is not None:
                    evac_deps.append(h1t_tok[0])
                tok = mm_group([(mk_mm(g, E), tr_toks[g]) for g in range(8)],
                               "scalar",
                               lambda e, sems, bank, E=E, h1Tv=h1Tv:
                               e.activation(
                                   h1Tv[:, E, :], bank[:, :], AF.Copy),
                               deps=[tok_wf1], evac_deps=evac_deps)
                f1_toks.append(tok)
            ln2t_tok[0] = bank_state[(bank_rr[0] - 1) % 8]["group"]
            bf1[c] = f1_toks

        def b_f2(c):
            h1Tv = h1T0 if c == 0 else h1T
            h2Tv = h2T0 if c == 0 else h2T
            f2_toks = []
            for E in range(8):
                def mk_mm(g, E, h1Tv=h1Tv):
                    def m(e, sems, out):
                        return e.matmul(out[:, :],
                                        wf1v[:, g, 128 * E:128 * (E + 1)],
                                        h1Tv[:, g, :], start=(g == 0),
                                        stop=(g == 7))
                    return m
                if c == 0:
                    evac_deps = [ln2t_tok[0]]
                else:
                    evac_deps = [scan1_end]
                if h2t_tok[0] is not None:
                    evac_deps.append(h2t_tok[0])
                tok = mm_group([(mk_mm(g, E), bf1[c][g]) for g in range(8)],
                               "scalar",
                               lambda e, sems, bank, E=E, h2Tv=h2Tv:
                               e.activation(
                                   h2Tv[:, E, :], bank[:, :], AF.Copy),
                               deps=[("swf2", 16, "gpsimd")],
                               evac_deps=evac_deps)
                f2_toks.append(tok)
            h1t_tok[0] = bank_state[(bank_rr[0] - 1) % 8]["group"]
            bf2[c] = f2_toks

        def b_f3(c):
            j0 = c * 4
            h2Tv = h2T0 if c == 0 else h2T
            for k in range(4):
                j = j0 + k
                y_toks = []
                for ec in range(2):
                    def mk_mm(g, k, ec, h2Tv=h2Tv):
                        def m(e, sems, out):
                            return e.matmul(out[:, :],
                                            h2Tv[:, g, 128 * k:128 * (k + 1)],
                                            wfv[:, 2, g, 512 * ec:512 * (ec + 1)],
                                            start=(g == 0), stop=(g == 7))
                        return m

                    def evac(e, sems, bank, ec=ec):
                        return e.tensor_tensor(
                            btmp[:, 2 * ec, :], bank[:, :],
                            biasb[:, 512 * ec:512 * (ec + 1)], TT.add)
                    if btmp_tok[ec] is not None:
                        evac_deps_b = [btmp_tok[ec]]
                    else:
                        evac_deps_b = [proj_tok[0]]
                    tok_b = mm_group([(mk_mm(g, k, ec), bf2[c][g])
                                      for g in range(8)],
                                     "vector", evac,
                                     deps=[("swf3", 16, "gpsimd")],
                                     evac_deps=evac_deps_b)
                    p.wait("scalar", tok_b)
                    tok_tanh = sop(lambda e, sems, ec=ec: e.activation(
                        btmp[:, 2 * ec + 1, :], btmp[:, 2 * ec, :], AF.Tanh),
                        inc="s")
                    btmp_tok[ec] = tok_tanh
                    p.wait("vector", tok_tanh)
                    tok_y = vop(lambda e, sems, j=j, ec=ec: e.tensor_tensor(
                        x_sb[:, j, 512 * ec:512 * (ec + 1)],
                        x_sb[:, j, 512 * ec:512 * (ec + 1)],
                        btmp[:, 2 * ec + 1, :], TT.add), inc="v")
                    y_toks.append(tok_y)
                    p.wait("scalar", tok_y)
                for tk in y_toks:
                    p.wait("sync", tk)
                p.op("sync", lambda e, sems, j=j: e.dma_start(
                    out=y_d[128 * j:128 * (j + 1), :], in_=x_sb[:, j, :]),
                    inc="sy", amt=16)
            h2t_tok[0] = bank_state[(bank_rr[0] - 1) % 8]["group"]


        abcd_chain(0, [tok_fold],
                   mid=lambda: norm_half(0, p.cur("v", "vector")))
        norm_half(1, p.cur("v", "vector"))
        scan_core(0)
        r_apply(0)
        scan0_end = half_tok[0]
        tok_dc0 = scan0_end
        abcd_chain(1, [scan0_end], mid=lambda: norm_half(0, tok_dc0))
        tok_mm1 = p.cur("t", "tensor")
        tok_abcd1_s = p.cur("s", "scalar")
        norm_half(1, norm_mul_tok[0])
        r_apply(1)
        p.wait("gpsimd", tok_mm1)
        p.wait("gpsimd", tok_abcd1_s)
        tok_wf1 = p.op("gpsimd", lambda e, sems: e.dma_start(
            out=wfv[:, 0, :, :], in_=wf_d[:, 0, :, :]), inc="swf", amt=16)
        proj_x2(0, tok_dc0, 0, 4)
        b_stats(0)
        proj_x2(0, tok_dc0, 4, 8)
        scan1_end = None
        b_front(0)
        scan_core(1)
        scan1_end = half_tok[0]
        p.wait("gpsimd", pk1_free[0])
        p.op("gpsimd", lambda e, sems: e.dma_start(
            out=wf1v[:, :, :], in_=wf_d[:, 1, :, :]), inc="swf2", amt=16)
        p.wait("gpsimd", scan1_end)
        tok_wf = p.op("gpsimd", lambda e, sems: e.dma_start(
            out=wfv[:, 2, :, :], in_=wf_d[:, 2, :, :]), inc="swf3", amt=16)

        tok_dc1 = scan1_end
        proj_x2(1, tok_dc1)
        b_stats(1)
        b_stats(2)
        b_stats(3)
        b_f2(0)
        b_front(1)
        b_f3(0)
        b_f2(1)
        b_front(2)
        b_f3(1)
        b_f2(2)
        b_front(3)
        b_f3(2)
        b_f2(3)
        b_f3(3)

        p.op("sync", lambda e, sems: e.wait_ge(sems["sy"], p.val["sy"]))

        with nc.Block() as block:
            @block.sync
            def _(e):
                for f in p.ops["sync"]:
                    f(e, sems)

            @block.gpsimd
            def _(e):
                for f in p.ops["gpsimd"]:
                    f(e, sems)

            @block.tensor
            def _(e):
                for f in p.ops["tensor"]:
                    f(e, sems)

            @block.vector
            def _(e):
                for f in p.ops["vector"]:
                    f(e, sems)

            @block.scalar
            def _(e):
                for f in p.ops["scalar"]:
                    f(e, sems)

    return nc


def make_weights(L0_attn_W_m, L0_attn_v_vec, L0_attn_W_proj, L0_attn_out_scale,
                 L0_ffn_weight, L0_ffn_aux_weight1, L0_ffn_aux_weight2,
                 L0_ffn_bias):
    f32 = np.float32
    Wm = np.asarray(L0_attn_W_m, f32)
    Wp = np.asarray(L0_attn_W_proj, f32)
    vv = np.asarray(L0_attn_v_vec, f32)
    osc = np.asarray(L0_attn_out_scale, f32)
    Wf = [np.asarray(w, f32) for w in
          (L0_ffn_weight, L0_ffn_aux_weight1, L0_ffn_aux_weight2)]
    bias = np.asarray(L0_ffn_bias, f32)

    Wm4 = Wm.reshape(256, 4, 1024)
    ws = np.empty((128, 8, 8, 128), f32)
    for e in range(4):
        for h in range(2):
            blk = Wm4[128 * h:128 * (h + 1), e, :]
            ws[:, e * 2 + h, :, :] = blk.T.reshape(8, 128, 128).transpose(1, 0, 2)
    wp = np.empty((128, 4, 1024), f32)
    for h in range(2):
        for comp in range(2):
            cols = 2 * (128 * h + np.arange(128)) + comp
            wp[:, 2 * comp + h, :] = Wp[:, cols].T
    wf = np.stack([w.reshape(8, 128, 1024).transpose(1, 0, 2) for w in Wf], axis=1)
    sb = np.ascontiguousarray(np.broadcast_to(osc, (128, 1024)))
    bb = np.ascontiguousarray(np.broadcast_to(bias, (128, 1024)))
    vi = np.empty((128, 2, 2), f32)
    for h in range(2):
        vi[:, h, 0] = vv[128 * h:128 * (h + 1), 0, 0]
        vi[:, h, 1] = vv[128 * h:128 * (h + 1), 1, 0]
    idm = np.eye(128, dtype=ml_dtypes.bfloat16)
    return {"w_scan": np.ascontiguousarray(ws), "w_proj": np.ascontiguousarray(wp),
            "w_ffn": np.ascontiguousarray(wf), "scale_b": sb, "bias_b": bb,
            "v_init": vi, "ident": idm}


def kernel(L0_attn_x, L0_attn_W_m, L0_attn_v_vec, L0_attn_W_proj,
           L0_attn_out_scale, L0_ffn_weight, L0_ffn_aux_weight1,
           L0_ffn_aux_weight2, L0_ffn_bias):
    x = np.asarray(L0_attn_x, np.float32)
    wmap = make_weights(L0_attn_W_m, L0_attn_v_vec, L0_attn_W_proj,
                        L0_attn_out_scale, L0_ffn_weight, L0_ffn_aux_weight1,
                        L0_ffn_aux_weight2, L0_ffn_bias)
    nc = build_nc()
    in_maps = []
    for i in range(8):
        m = dict(wmap)
        m["x"] = np.ascontiguousarray(x[i])
        in_maps.append(m)
    res = run_bass_kernel_spmd(nc, in_maps, core_ids=list(range(8)))
    out = np.stack([np.asarray(r["y"], np.float32) for r in res.results], axis=0)
    return out


# version 20
# speedup vs baseline: 1.2133x; 1.0084x over previous
import numpy as np
import ml_dtypes

import concourse.bass as bass
import concourse.mybir as mybir
from concourse.bass_utils import run_bass_kernel_spmd

F32 = mybir.dt.float32
BF16 = mybir.dt.bfloat16
OP = mybir.AluOpType
AF = mybir.ActivationFunctionType

T = 2048
D = 1024
NT = 16
NSC = 2
EPS_LN = 1e-5
EPS_RMS = 1.1920929e-07

ENGINES = ("sync", "gpsimd", "tensor", "vector", "scalar")

A_XNT = 0
A_ABCD = 4096
A_PK1 = 12288
A_PK2 = 16384
A_PK3 = 18432
A_Q = 19456
A_R = 19968
A_SIG = 20992
A_SCALEB = 21760
A_END = 23168
A_WPX = 4096
A_TL1 = 16384
A_TL2 = 18432
A_TL3 = 19456
A_TL4 = 19968
A_TDB = 20992
B_WF = 0
B_LN2T = 16384
B_H1T = 18432
B_H2T = 20480
B_TMP = 12288

COMB = (("a", "a", "a", "b", "c"), ("b", "a", "b", "b", "d"),
        ("c", "c", "a", "d", "c"), ("d", "c", "b", "d", "d"))
NAMES = ("a", "b", "c", "d")


class Plan:
    def __init__(self):
        self.ops = {e: [] for e in ENGINES}
        self.val = {}
        self.last_wait = {}
        self.since_drain = {e: 0 for e in ENGINES}

    def op(self, eng, fn, inc=None, amt=1):
        self.since_drain[eng] += 1
        if inc is not None:
            self.val[inc] = self.val.get(inc, 0) + amt
            v = self.val[inc]

            def wrapped(e, sems, fn=fn, inc=inc, amt=amt):
                fn(e, sems).then_inc(sems[inc], amt)

            self.ops[eng].append(wrapped)
            return (inc, v, eng)
        self.ops[eng].append(fn)
        return None

    def drain(self, eng):
        if self.since_drain[eng] == 0:
            return
        self.since_drain[eng] = 0
        self.ops[eng].append(lambda e, sems: e.drain())

    def wait(self, eng, tok):
        if tok is None:
            return
        sem, v, producer = tok
        if producer == eng:
            self.drain(eng)
            return
        key = (eng, sem)
        if self.last_wait.get(key, -1) >= v:
            return
        self.last_wait[key] = v

        def w(e, sems, sem=sem, v=v):
            e.wait_ge(sems[sem], v)

        self.ops[eng].append(w)

    def cur(self, sem, eng="?"):
        return (sem, self.val.get(sem, 0), eng)


def build_nc():
    nc = bass.Bass("TRN2", target_bir_lowering=False)

    x_d = nc.dram_tensor("x", [T, D], F32, kind="ExternalInput").ap()
    y_d = nc.dram_tensor("y", [T, D], F32, kind="ExternalOutput").ap()
    ws_d = nc.dram_tensor("w_scan", [128, 8, 8, 128], F32, kind="ExternalInput").ap()
    wp_d = nc.dram_tensor("w_proj", [128, 4, 1024], F32, kind="ExternalInput").ap()
    wf_d = nc.dram_tensor("w_ffn", [128, 3, 8, 1024], F32, kind="ExternalInput").ap()
    sb_d = nc.dram_tensor("scale_b", [128, 1024], F32, kind="ExternalInput").ap()
    bb_d = nc.dram_tensor("bias_b", [128, 1024], F32, kind="ExternalInput").ap()
    vi_d = nc.dram_tensor("v_init", [128, 2, 2], F32, kind="ExternalInput").ap()
    id_d = nc.dram_tensor("ident", [128, 128], BF16, kind="ExternalInput").ap()

    p = Plan()
    import contextlib
    stack = contextlib.ExitStack()
    with stack:
        arena_base = (nc._sbuf_addr_for_side("left") + 31) // 32 * 32
        arena = stack.enter_context(nc.sbuf_tensor("arena", [128, A_END], F32))
        x_sb = stack.enter_context(nc.sbuf_tensor("x_sb", [128, NT, D], F32))
        xn_sb = stack.enter_context(nc.sbuf_tensor("xn_sb", [128, 4, D], BF16))
        weh_base = (nc._sbuf_addr_for_side("left") + 31) // 32 * 32
        weh = stack.enter_context(nc.sbuf_tensor("weh", [128, 8, 8, 128], BF16))
        ln2T0 = nc.alloc_sbuf_tensor_at(
            "ln2T0", [128, 8, 512], BF16, offset=weh_base).ap()
        h1T0 = nc.alloc_sbuf_tensor_at(
            "h1T0", [128, 8, 512], BF16, offset=weh_base + 8192).ap()
        h2T0 = nc.alloc_sbuf_tensor_at(
            "h2T0", [128, 8, 512], BF16, offset=weh_base).ap()
        wpb = stack.enter_context(nc.sbuf_tensor("wpb", [128, 4, D], BF16))
        biasb = stack.enter_context(nc.sbuf_tensor("biasb", [128, D], F32))
        vinit = stack.enter_context(nc.sbuf_tensor("vinit", [128, 2, 2], F32))
        id_sb = stack.enter_context(nc.sbuf_tensor("id_sb", [128, 128], BF16))
        stats = stack.enter_context(nc.sbuf_tensor("stats", [128, NT, 2], F32))
        bn6 = stack.enter_context(nc.sbuf_tensor("bn6", [128, 8, 12], F32))
        st_t0 = stack.enter_context(nc.sbuf_tensor("st_t0", [128, NT], F32))
        st_s1 = stack.enter_context(nc.sbuf_tensor("st_s1", [128, NT], F32))
        st_s2 = stack.enter_context(nc.sbuf_tensor("st_s2", [128, NT], F32))
        st_r1 = stack.enter_context(nc.sbuf_tensor("st_r1", [128, NT], F32))
        st_rn = stack.enter_context(nc.sbuf_tensor("st_rn", [128, NT], F32))
        st_bxn = stack.enter_context(nc.sbuf_tensor("st_bxn", [128, NT], F32))
        st_sr = stack.enter_context(nc.sbuf_tensor("st_sr", [128, NT], F32))
        st_br = stack.enter_context(nc.sbuf_tensor("st_br", [128, NT], F32))
        hxybf_base = (nc._sbuf_addr_for_side("left") + 31) // 32 * 32
        hxybf = stack.enter_context(nc.sbuf_tensor("hxybf", [128, 2, 4, D], BF16))
        btmpv = nc.alloc_sbuf_tensor_at(
            "btmpv", [128, 4, 512], F32, offset=hxybf_base).ap()
        wf1v = None
        psum = [stack.enter_context(nc.psum_tensor(f"ps{i}", [128, 512], F32))
                for i in range(8)]
        sems = {}
        for s in ("sx", "sx2", "sx3", "sx4", "sw", "swf", "swf2", "swf3", "sy",
                  "t", "v", "s", "g"):
            sems[s] = stack.enter_context(nc.semaphore(f"sem_{s}"))

        def av(off, n):
            return arena[:, off:off + n]

        xnT = nc.alloc_sbuf_tensor_at(
            "xnT", [128, 8, 1024], BF16, offset=arena_base + A_XNT * 4).ap()
        abcd = av(A_ABCD, 8192).rearrange("p (i t) -> p i t", i=8)
        ntmp = av(A_PK1, 4096).rearrange("p (i t) -> p i t", i=4)
        PK1 = av(A_PK1, 4096).rearrange("p (i h t) -> p i h t", i=4, h=2)
        PK2 = av(A_PK2, 2048).rearrange("p (i h t) -> p i h t", i=4, h=2)
        PK3 = av(A_PK3, 1024).rearrange("p (i h t) -> p i h t", i=4, h=2)
        Qv = av(A_Q, 512).rearrange("p (i h t) -> p i h t", i=4, h=2)
        Rv = av(A_R, 1024).rearrange("p (b i h t) -> p b i h t", b=2, i=4, h=2)
        SIG = av(A_SIG, 768).rearrange("p (i h t) -> p i h t", i=6, h=2)
        TL1 = av(A_TL1, 4096).rearrange("p (i h t) -> p i h t", i=4, h=2)
        TL2 = av(A_TL2, 2048).rearrange("p (i h t) -> p i h t", i=4, h=2)
        TL3 = av(A_TL3, 1024).rearrange("p (i h t) -> p i h t", i=4, h=2)
        TL4 = av(A_TL4, 512).rearrange("p (i h t) -> p i h t", i=4, h=2)
        TDB = av(A_TDB, 504).rearrange("p (i t) -> p i t", i=4)
        hxy = av(A_PK1, 4096).rearrange("p (i t) -> p i t", i=4)
        wpx = av(A_WPX, 4096).rearrange("p (i t) -> p i t", i=4)
        scaleb = av(A_SCALEB, 1024)
        wf1v = nc.alloc_sbuf_tensor_at(
            "wf1v", [128, 8, 1024], BF16, offset=arena_base + A_PK1 * 4).ap()
        wfv = nc.alloc_sbuf_tensor_at(
            "wfv", [128, 3, 8, 1024], BF16, offset=arena_base + B_WF * 4).ap()
        xn2_sb = nc.alloc_sbuf_tensor_at(
            "xn2_sb", [128, 8, D], BF16, offset=arena_base + A_ABCD * 4).ap()
        ln2T = nc.alloc_sbuf_tensor_at(
            "ln2T", [128, 8, 512], BF16, offset=arena_base + B_LN2T * 4).ap()
        h1T = nc.alloc_sbuf_tensor_at(
            "h1T", [128, 8, 512], BF16, offset=arena_base + B_H1T * 4).ap()
        h2T = nc.alloc_sbuf_tensor_at(
            "h2T", [128, 8, 512], BF16, offset=arena_base + B_H2T * 4).ap()
        btmp = btmpv

        def vop(fn, inc=None):
            return p.op("vector", fn, inc=inc)

        def sop(fn, inc=None):
            return p.op("scalar", fn, inc=inc)

        TT = OP

        bank_state = [{"evac": None, "group": None} for _ in range(8)]
        bank_rr = [0]

        def mm_group(mms, evac_eng, evac_fn, deps=(), evac_deps=()):
            b = bank_rr[0] % 8
            bank_rr[0] += 1
            st = bank_state[b]
            if st["evac"] is not None:
                p.wait("tensor", st["evac"])
            for d in deps:
                p.wait("tensor", d)
            n = len(mms)
            for i, m in enumerate(mms):
                if isinstance(m, tuple):
                    m, mdep = m
                    p.wait("tensor", mdep)
                if i == n - 1:
                    tok = p.op("tensor", lambda e, sems, m=m, b=b: m(e, sems, psum[b]),
                               inc="t")
                else:
                    p.op("tensor", lambda e, sems, m=m, b=b: m(e, sems, psum[b]))
            st["group"] = tok
            p.wait(evac_eng, tok)
            for d in evac_deps:
                p.wait(evac_eng, d)
            esem = {"scalar": "s", "vector": "v"}[evac_eng]
            if not isinstance(evac_fn, (list, tuple)):
                evac_fn = [evac_fn]
            for fi, ef in enumerate(evac_fn):
                if fi == len(evac_fn) - 1:
                    etok = p.op(evac_eng, lambda e, sems, b=b, ef=ef:
                                ef(e, sems, psum[b]), inc=esem)
                else:
                    p.op(evac_eng, lambda e, sems, b=b, ef=ef: ef(e, sems, psum[b]))
                    p.drain(evac_eng)
            st["evac"] = etok
            return etok

        x_src = x_d.rearrange("(n p) d -> p n d", p=128)
        tok_x_q = []
        for q in range(4):
            tok_x_q.append(p.op("sync", lambda e, sems, q=q: e.dma_start(
                out=x_sb[:, 4 * q:4 * (q + 1), :], in_=x_src[:, 4 * q:4 * (q + 1), :]),
                inc=("sx", "sx2", "sx3", "sx4")[q], amt=16))

        def wdma(out_ap, in_ap):
            return p.op("gpsimd", lambda e, sems: e.dma_start(out=out_ap, in_=in_ap),
                        inc="sw", amt=16)

        wdma(weh[:, :, :, :], ws_d)
        wdma(wpx[:, :, :], wp_d)
        wdma(scaleb, sb_d)
        wdma(biasb[:, :], bb_d)
        wdma(vinit[:, :, :], vi_d)
        wdma(id_sb[:, :], id_d)
        tok_w = p.cur("sw", "gpsimd")

        p.wait("vector", tok_w)
        for i in range(4):
            tok_fold = vop(lambda e, sems, i=i: e.tensor_tensor(
                wpb[:, i, :], wpx[:, i, :], scaleb, TT.mult), inc="v")

        xn_slot_tok = [None] * 4
        half_tok = [None]
        pk1_free = [None]
        proj_tok = [None, None]
        ln2t_tok = [None]
        h1t_tok = [None]
        h2t_tok = [None]

        def combine(dst, L, R, tmps, extra_layer1=()):
            for (od, f1, f2, g1, g2) in COMB:
                vop(lambda e, sems, od=od, f1=f1, f2=f2: e.tensor_tensor(
                    dst[od], L[f1], R[f2], TT.mult))
                vop(lambda e, sems, od=od, g1=g1, g2=g2: e.tensor_tensor(
                    tmps[od], L[g1], R[g2], TT.mult))
            for fn in extra_layer1:
                vop(fn)
            p.drain("vector")
            for od in NAMES:
                vop(lambda e, sems, od=od: e.tensor_tensor(
                    dst[od], dst[od], tmps[od], TT.add))
            p.drain("vector")

        def pairs(viewmap):
            return {nm: viewmap[nm].rearrange("p (u two) -> p u two", two=2)
                    for nm in NAMES}

        stat_toks = []
        stat_toks2 = []
        apply_tok_j = [None] * NT
        for sc in range(NSC):
            j0 = sc * 8
            p.drain("vector")
            for jj in range(8):
                j = j0 + jj
                p.wait("vector", tok_x_q[j // 4])
                vop(lambda e, sems, j=j, jj=jj: e.bn_stats(
                    bn6[:, jj, 0:6], x_sb[:, j, 0:512]))
                vop(lambda e, sems, j=j, jj=jj: e.bn_stats(
                    bn6[:, jj, 6:12], x_sb[:, j, 512:1024]))
            p.drain("vector")
            for jj in range(8):
                vop(lambda e, sems, j=j0 + jj, jj=jj: e.bn_aggr(
                    stats[:, j, :], bn6[:, jj, 0:12]))
            p.drain("vector")
            sl = slice(j0, j0 + 8)
            mean = stats[:, sl, 0]
            var = stats[:, sl, 1]
            vop(lambda e, sems, sl=sl, var=var: e.tensor_scalar(
                st_t0[:, sl], var, EPS_LN, None, TT.add))
            tok_v0 = vop(lambda e, sems, sl=sl, var=var: e.tensor_scalar(
                st_s2[:, sl], var, 1.0 + EPS_RMS, EPS_LN * EPS_RMS, TT.mult, TT.add),
                inc="v")
            p.wait("scalar", tok_v0)
            sop(lambda e, sems, sl=sl: e.activation(
                st_s1[:, sl], st_t0[:, sl], AF.Ln))
            sop(lambda e, sems, sl=sl: e.activation(
                st_s2[:, sl], st_s2[:, sl], AF.Ln))
            p.drain("scalar")
            sop(lambda e, sems, sl=sl: e.activation(
                st_r1[:, sl], st_s1[:, sl], AF.Exp, scale=-0.5))
            tok_e = sop(lambda e, sems, sl=sl: e.activation(
                st_rn[:, sl], st_s2[:, sl], AF.Exp, scale=-0.5), inc="s")
            p.wait("vector", tok_e)
            tok_stat = vop(lambda e, sems, sl=sl, mean=mean: e.scalar_tensor_tensor(
                st_bxn[:, sl], mean, -1.0, st_rn[:, sl], TT.mult, TT.mult), inc="v")
            vop(lambda e, sems, sl=sl: e.tensor_scalar(
                st_sr[:, sl], st_r1[:, sl], 1.0, None, TT.add))
            tok_stat2 = vop(lambda e, sems, sl=sl, mean=mean: e.scalar_tensor_tensor(
                st_br[:, sl], mean, -1.0, st_r1[:, sl], TT.mult, TT.mult), inc="v")
            stat_toks.append(tok_stat)
            stat_toks2.append(tok_stat2)

        def abcd_chain(sc, extra_evac_deps, mid=None):
            j0 = sc * 8
            ee = "vector" if sc == 0 else "scalar"
            tr_tok = {}
            for cc in range(2):
                for k in range(4):
                    j = j0 + cc * 4 + k
                    p.wait("scalar", stat_toks[sc])
                    if xn_slot_tok[k] is not None:
                        p.wait("scalar", xn_slot_tok[k])
                    atok = sop(lambda e, sems, j=j, k=k: e.activation(
                        xn_sb[:, k, :], x_sb[:, j, :], AF.Identity,
                        bias=st_bxn[:, j:j + 1], scale=st_rn[:, j:j + 1]), inc="s")
                    apply_tok_j[j] = atok
                    for gh in range(2):
                        def mk_mm(k, g):
                            def m(e, sems, out):
                                return e.matmul(
                                    out[:, 128 * (g % 4):128 * (g % 4 + 1)],
                                    xn_sb[:, k, 128 * g:128 * (g + 1)],
                                    id_sb[:, :],
                                    start=(g % 4 == 0), stop=(g % 4 == 3))
                            return m
                        def tr_evac(e, sems, bank, k=k, cc=cc, gh=gh):
                            dst = xnT[:, 4 * gh:4 * gh + 4,
                                      128 * (4 * cc + k):128 * (4 * cc + k + 1)]
                            if sc == 0:
                                return e.tensor_copy(dst, bank[:, :])
                            return e.activation(dst, bank[:, :], AF.Copy)
                        tok = mm_group(
                            [mk_mm(k, g) for g in range(4 * gh, 4 * gh + 4)],
                            ee, tr_evac, deps=[atok, tok_w])
                        tr_tok[(cc, gh)] = tok
                    xn_slot_tok[k] = bank_state[(bank_rr[0] - 1) % 8]["group"]

            def mm_i(i):
                for cc in range(2):
                    def mk_mm(g, i, cc):
                        def m(e, sems, out):
                            return e.matmul(out[:, :], weh[:, i, g, :],
                                            xnT[:, g, 512 * cc:512 * (cc + 1)],
                                            start=(g == 0), stop=(g == 7))
                        return m
                    def ab_evac(e, sems, bank, i=i, cc=cc):
                        dst = abcd[:, i, 512 * cc:512 * (cc + 1)]
                        if sc == 0:
                            return e.tensor_copy(dst, bank[:, :])
                        return e.activation(dst, bank[:, :], AF.Copy)
                    mm_group(
                        [(mk_mm(g, i, cc), tr_tok[(cc, g // 4)]) for g in range(8)],
                        ee, ab_evac,
                        deps=[tok_w], evac_deps=list(extra_evac_deps))

            for i in (0, 2, 4, 6):
                mm_i(i)
            if mid is not None:
                mid()
            for i in (1, 3, 5, 7):
                mm_i(i)

        norm_mul_tok = [None]

        def norm_half(h, gate):
            A, B, C, Dt = (abcd[:, 0 + h, :], abcd[:, 2 + h, :],
                           abcd[:, 4 + h, :], abcd[:, 6 + h, :])
            p.drain("scalar")
            if gate is not None:
                p.wait("scalar", gate)
            for ti, M_ in enumerate((A, B, C, Dt)):
                tok_sq = sop(lambda e, sems, M_=M_, ti=ti: e.activation(
                    ntmp[:, ti, :], M_, AF.Square), inc="s")
            p.wait("vector", tok_sq)
            vop(lambda e, sems: e.tensor_tensor(
                ntmp[:, 0, :], ntmp[:, 0, :], ntmp[:, 1, :], TT.add))
            vop(lambda e, sems: e.tensor_tensor(
                ntmp[:, 2, :], ntmp[:, 2, :], ntmp[:, 3, :], TT.add))
            p.drain("vector")
            vop(lambda e, sems: e.tensor_tensor(
                ntmp[:, 0, :], ntmp[:, 0, :], ntmp[:, 2, :], TT.add))
            p.drain("vector")
            tok_clip = vop(lambda e, sems: e.tensor_scalar(
                ntmp[:, 0, :], ntmp[:, 0, :], 1e-12, None, TT.max), inc="v")
            p.wait("scalar", tok_clip)
            sop(lambda e, sems: e.activation(
                ntmp[:, 1, :], ntmp[:, 0, :], AF.Ln))
            p.drain("scalar")
            tok_s = sop(lambda e, sems: e.activation(
                ntmp[:, 0, :], ntmp[:, 1, :], AF.Exp, scale=-0.5), inc="s")
            p.wait("vector", tok_s)
            for mi, M_ in enumerate((A, B, C, Dt)):
                if mi == 3:
                    norm_mul_tok[0] = vop(lambda e, sems, M_=M_: e.tensor_tensor(
                        M_, M_, ntmp[:, 0, :], TT.mult), inc="v")
                else:
                    vop(lambda e, sems, M_=M_: e.tensor_tensor(
                        M_, M_, ntmp[:, 0, :], TT.mult))
            p.drain("vector")

        def scan_core(sc):
            def epack(n):
                out = {}
                for ei, nm in enumerate(NAMES):
                    out[nm] = abcd[:, 2 * ei:2 * ei + 2, :].rearrange(
                        "p h (u two) -> p h u two", two=2)
                return out

            pr = epack(512)
            combine({nm: PK1[:, ei, :, :] for ei, nm in enumerate(NAMES)},
                    {nm: pr[nm][:, :, :, 1] for nm in NAMES},
                    {nm: pr[nm][:, :, :, 0] for nm in NAMES},
                    {nm: TL1[:, ei, :, :] for ei, nm in enumerate(NAMES)})
            pr = {nm: PK1[:, ei, :, :].rearrange("p h (u two) -> p h u two", two=2)
                  for ei, nm in enumerate(NAMES)}
            combine({nm: PK2[:, ei, :, :] for ei, nm in enumerate(NAMES)},
                    {nm: pr[nm][:, :, :, 1] for nm in NAMES},
                    {nm: pr[nm][:, :, :, 0] for nm in NAMES},
                    {nm: TL2[:, ei, :, :] for ei, nm in enumerate(NAMES)})
            pr = {nm: PK2[:, ei, :, :].rearrange("p h (u two) -> p h u two", two=2)
                  for ei, nm in enumerate(NAMES)}
            combine({nm: PK3[:, ei, :, :] for ei, nm in enumerate(NAMES)},
                    {nm: pr[nm][:, :, :, 1] for nm in NAMES},
                    {nm: pr[nm][:, :, :, 0] for nm in NAMES},
                    {nm: TL3[:, ei, :, :] for ei, nm in enumerate(NAMES)})
            pr = {nm: PK3[:, ei, :, :].rearrange("p h (u two) -> p h u two", two=2)
                  for ei, nm in enumerate(NAMES)}
            combine({nm: Qv[:, ei, :, :] for ei, nm in enumerate(NAMES)},
                    {nm: pr[nm][:, :, :, 1] for nm in NAMES},
                    {nm: pr[nm][:, :, :, 0] for nm in NAMES},
                    {nm: TL4[:, ei, :, :] for ei, nm in enumerate(NAMES)})

            cur, nxt = 0, 1
            first = True
            s = 1
            while s < 64:
                srcv = {nm: (Qv[:, ei, :, :] if first else Rv[:, cur, ei, :, :])
                        for ei, nm in enumerate(NAMES)}
                dstv = {nm: Rv[:, nxt, ei, :, :] for ei, nm in enumerate(NAMES)}
                heads = [
                    (lambda e, sems, d=dstv[nm], sv=srcv[nm], s=s:
                     e.tensor_copy(d[:, :, 0:s], sv[:, :, 0:s])) for nm in NAMES]
                combine({nm: dstv[nm][:, :, s:64] for nm in NAMES},
                        {nm: srcv[nm][:, :, s:64] for nm in NAMES},
                        {nm: srcv[nm][:, :, 0:64 - s] for nm in NAMES},
                        {nm: TDB[:, ei, 0:2 * (64 - s)].rearrange(
                            "p (h t) -> p h t", h=2)
                         for ei, nm in enumerate(NAMES)},
                        extra_layer1=heads)
                cur, nxt = nxt, cur
                first = False
                s *= 2
            Rf = {nm: Rv[:, cur, ei, :, :] for ei, nm in enumerate(NAMES)}

            cxb = vinit[:, :, 0:1].broadcast_to([128, 2, 64])
            cyb = vinit[:, :, 1:2].broadcast_to([128, 2, 64])
            vop(lambda e, sems, Rf=Rf, cxb=cxb: e.tensor_tensor(
                SIG[:, 0, :, :], Rf["a"], cxb, TT.mult))
            vop(lambda e, sems, Rf=Rf, cyb=cyb: e.tensor_tensor(
                SIG[:, 4, :, :], Rf["b"], cyb, TT.mult))
            vop(lambda e, sems, Rf=Rf, cxb=cxb: e.tensor_tensor(
                SIG[:, 1, :, :], Rf["c"], cxb, TT.mult))
            vop(lambda e, sems, Rf=Rf, cyb=cyb: e.tensor_tensor(
                SIG[:, 5, :, :], Rf["d"], cyb, TT.mult))
            p.drain("vector")
            vop(lambda e, sems: e.tensor_tensor(
                SIG[:, 0, :, :], SIG[:, 0, :, :], SIG[:, 4, :, :], TT.add))
            vop(lambda e, sems: e.tensor_tensor(
                SIG[:, 1, :, :], SIG[:, 1, :, :], SIG[:, 5, :, :], TT.add))
            p.drain("vector")
            vop(lambda e, sems: e.tensor_copy(SIG[:, 2, :, 1:64], SIG[:, 0, :, 0:63]))
            vop(lambda e, sems: e.tensor_copy(SIG[:, 2, :, 0:1], vinit[:, :, 0:1]))
            vop(lambda e, sems: e.tensor_copy(SIG[:, 3, :, 1:64], SIG[:, 1, :, 0:63]))
            vop(lambda e, sems: e.tensor_copy(SIG[:, 3, :, 0:1], vinit[:, :, 1:2]))
            p.drain("vector")
            vop(lambda e, sems: e.tensor_copy(vinit[:, :, 0:1], SIG[:, 0, :, 63:64]))
            vop(lambda e, sems: e.tensor_copy(vinit[:, :, 1:2], SIG[:, 1, :, 63:64]))
            p.drain("vector")

            archx = av(A_PK2, 1152).rearrange("p (j h u) -> p j h u", j=9, h=2)
            archy = av(A_PK2 + 1152, 1152).rearrange("p (j h u) -> p j h u",
                                                     j=9, h=2)
            etmp1 = av(A_R, 1024)
            etmp2 = av(18688, 1024)
            PRV = {nm: PK1[:, ei, :, :].rearrange("p h (u j) -> p h u j", j=8)
                   for ei, nm in enumerate(NAMES)}
            vop(lambda e, sems: e.tensor_copy(archx[:, 0, :, :], SIG[:, 2, :, :]))
            vop(lambda e, sems: e.tensor_copy(archy[:, 0, :, :], SIG[:, 3, :, :]))
            p.drain("vector")
            for j in range(8):
                px = archx[:, j, :, :]
                py = archy[:, j, :, :]
                vop(lambda e, sems, j=j, px=px: e.tensor_tensor(
                    SIG[:, 4, :, :], PRV["a"][:, :, :, j], px, TT.mult))
                vop(lambda e, sems, j=j, px=px: e.tensor_tensor(
                    SIG[:, 5, :, :], PRV["c"][:, :, :, j], px, TT.mult))
                vop(lambda e, sems, j=j, py=py: e.tensor_tensor(
                    archx[:, j + 1, :, :], PRV["b"][:, :, :, j], py, TT.mult))
                vop(lambda e, sems, j=j, py=py: e.tensor_tensor(
                    archy[:, j + 1, :, :], PRV["d"][:, :, :, j], py, TT.mult))
                p.drain("vector")
                vop(lambda e, sems, j=j: e.tensor_tensor(
                    archx[:, j + 1, :, :], archx[:, j + 1, :, :],
                    SIG[:, 4, :, :], TT.add))
                if j == 7:
                    pk1_free[0] = vop(lambda e, sems, j=j: e.tensor_tensor(
                        archy[:, j + 1, :, :], archy[:, j + 1, :, :],
                        SIG[:, 5, :, :], TT.add), inc="v")
                else:
                    vop(lambda e, sems, j=j: e.tensor_tensor(
                        archy[:, j + 1, :, :], archy[:, j + 1, :, :],
                        SIG[:, 5, :, :], TT.add))
                p.drain("vector")
            bk = sc % 2
            HBx = hxybf[:, bk, 0:2, :].rearrange("p h (u r) -> p h u r", r=16)
            HBy = hxybf[:, bk, 2:4, :].rearrange("p h (u r) -> p h u r", r=16)
            arx = archx[:, 1:9, :, :].rearrange("p j h u -> p h u j")
            ary = archy[:, 1:9, :, :].rearrange("p j h u -> p h u j")
            vop(lambda e, sems: e.tensor_copy(HBx[:, :, :, 1:16:2], arx))
            vop(lambda e, sems: e.tensor_copy(HBy[:, :, :, 1:16:2], ary))
            Aev = {nm: abcd[:, 2 * ei:2 * ei + 2, :].rearrange(
                "p h (u r) -> p h u r", r=16)[:, :, :, 0:16:2]
                for ei, nm in enumerate(NAMES)}
            hpx = archx[:, 0:8, :, :].rearrange("p j h u -> p h u j")
            hpy = archy[:, 0:8, :, :].rearrange("p j h u -> p h u j")
            e1v = etmp1.rearrange("p (h u j) -> p h u j", h=2, u=64)
            e2v = etmp2.rearrange("p (h u j) -> p h u j", h=2, u=64)
            vop(lambda e, sems: e.tensor_tensor(e1v, Aev["a"], hpx, TT.mult))
            vop(lambda e, sems: e.tensor_tensor(e2v, Aev["b"], hpy, TT.mult))
            p.drain("vector")
            vop(lambda e, sems: e.tensor_tensor(
                HBx[:, :, :, 0:16:2], e1v, e2v, TT.add))
            p.drain("vector")
            vop(lambda e, sems: e.tensor_tensor(e1v, Aev["c"], hpx, TT.mult))
            vop(lambda e, sems: e.tensor_tensor(e2v, Aev["d"], hpy, TT.mult))
            p.drain("vector")
            half_tok[0] = vop(lambda e, sems: e.tensor_tensor(
                HBy[:, :, :, 0:16:2], e1v, e2v, TT.add), inc="v")
            p.drain("vector")

        def r_apply(sc):
            for jj in range(8):
                j = sc * 8 + jj
                p.wait("scalar", stat_toks2[sc])
                p.wait("scalar", apply_tok_j[j])
                apply_tok_j[j] = sop(lambda e, sems, j=j: e.activation(
                    x_sb[:, j, :], x_sb[:, j, :], AF.Identity,
                    bias=st_br[:, j:j + 1], scale=st_sr[:, j:j + 1]), inc="s")

        def proj_x2(sc, tok_dc, jlo=0, jhi=8):
            j0 = sc * 8
            for jj in range(jlo, jhi):
                j = j0 + jj
                for ec in range(2):
                    def mk_mm(i, jj, ec, bk=sc % 2):
                        def m(e, sems, out):
                            return e.matmul(out[:, :],
                                            hxybf[:, bk, i, 128 * jj:128 * (jj + 1)],
                                            wpb[:, i, 512 * ec:512 * (ec + 1)],
                                            start=(i == 0), stop=(i == 3))
                        return m

                    def ev2(e, sems, bank, j=j, ec=ec):
                        return e.tensor_tensor(
                            x_sb[:, j, 512 * ec:512 * (ec + 1)],
                            x_sb[:, j, 512 * ec:512 * (ec + 1)],
                            bank[:, :], TT.add)
                    mm_group([mk_mm(i, jj, ec) for i in range(2 * 2)],
                             "vector", ev2,
                             deps=[tok_dc, tok_fold],
                             evac_deps=[apply_tok_j[j]])
            proj_tok[sc] = bank_state[(bank_rr[0] - 1) % 8]["group"]

        btmp_tok = [None, None]
        bstat_toks = [None] * 4

        def b_stats(c):
            j0 = c * 4
            sl = slice(j0, j0 + 4)
            p.drain("vector")
            for k in range(4):
                j = j0 + k
                vop(lambda e, sems, j=j, k=k: e.bn_stats(
                    bn6[:, k, 0:6], x_sb[:, j, 0:512]))
                vop(lambda e, sems, j=j, k=k: e.bn_stats(
                    bn6[:, k, 6:12], x_sb[:, j, 512:1024]))
            p.drain("vector")
            for k in range(4):
                vop(lambda e, sems, j=j0 + k, k=k: e.bn_aggr(
                    stats[:, j, :], bn6[:, k, 0:12]))
            p.drain("vector")
            mean = stats[:, sl, 0]
            var = stats[:, sl, 1]
            tok_v0 = vop(lambda e, sems, sl=sl, var=var: e.tensor_scalar(
                st_t0[:, sl], var, EPS_LN, None, TT.add), inc="v")
            p.wait("scalar", tok_v0)
            sop(lambda e, sems, sl=sl: e.activation(
                st_s1[:, sl], st_t0[:, sl], AF.Ln))
            p.drain("scalar")
            tok_s0 = sop(lambda e, sems, sl=sl: e.activation(
                st_r1[:, sl], st_s1[:, sl], AF.Exp, scale=-0.5), inc="s")
            p.wait("vector", tok_s0)
            bstat_toks[c] = vop(lambda e, sems, sl=sl, mean=mean:
                                e.scalar_tensor_tensor(
                st_bxn[:, sl], mean, -1.0, st_r1[:, sl], TT.mult, TT.mult), inc="v")

        btr = {}
        bf1 = {}
        bf2 = {}

        b_apply = {}

        def b_applies(c):
            j0 = c * 4
            if c in (1, 2):
                xnt, base = xn2_sb, (c - 1) * 4
            else:
                xnt, base = xn_sb, 0
            apply_toks = []
            for k in range(4):
                j = j0 + k
                p.wait("scalar", bstat_toks[c])
                if c in (1, 2):
                    p.wait("scalar", scan1_end)
                elif xn_slot_tok[k] is not None:
                    p.wait("scalar", xn_slot_tok[k])
                tok = sop(lambda e, sems, j=j, k=k, xnt=xnt, base=base:
                          e.activation(
                    xnt[:, base + k, :], x_sb[:, j, :], AF.Identity,
                    bias=st_bxn[:, j:j + 1], scale=st_r1[:, j:j + 1]), inc="s")
                apply_toks.append(tok)
            b_apply[c] = (xnt, base, apply_toks)

        def b_front(c):
            ln2Tv = ln2T0 if c == 0 else ln2T
            h1Tv = h1T0 if c == 0 else h1T
            front_gate = tok_mm1 if c == 0 else scan1_end
            if c not in b_apply:
                b_applies(c)
            xnt, base, apply_toks = b_apply[c]
            tr_toks = []
            for g in range(8):
                def mk_mm(k, g, xnt=xnt, base=base):
                    def m(e, sems, out):
                        return e.matmul(out[:, 128 * k:128 * (k + 1)],
                                        xnt[:, base + k, 128 * g:128 * (g + 1)],
                                        id_sb[:, :], start=(k == 0), stop=(k == 3))
                    return m
                evac_deps = [front_gate]
                if ln2t_tok[0] is not None:
                    evac_deps.append(ln2t_tok[0])
                tok = mm_group([mk_mm(k, g) for k in range(4)], "scalar",
                               lambda e, sems, bank, g=g, ln2Tv=ln2Tv:
                               e.activation(
                                   ln2Tv[:, g, :], bank[:, :], AF.Copy),
                               deps=apply_toks, evac_deps=evac_deps)
                tr_toks.append(tok)
            if xnt is xn_sb:
                last_tr_group = bank_state[(bank_rr[0] - 1) % 8]["group"]
                for k in range(4):
                    xn_slot_tok[k] = last_tr_group
            btr[c] = tr_toks

            f1_toks = []
            for E in range(8):
                def mk_mm(g, E, ln2Tv=ln2Tv):
                    def m(e, sems, out):
                        return e.matmul(out[:, :],
                                        wfv[:, 0, g, 128 * E:128 * (E + 1)],
                                        ln2Tv[:, g, :], start=(g == 0),
                                        stop=(g == 7))
                    return m
                evac_deps = [front_gate]
                if h1t_tok[0] is not None:
                    evac_deps.append(h1t_tok[0])
                tok = mm_group([(mk_mm(g, E), tr_toks[g]) for g in range(8)],
                               "scalar",
                               lambda e, sems, bank, E=E, h1Tv=h1Tv:
                               e.activation(
                                   h1Tv[:, E, :], bank[:, :], AF.Copy),
                               deps=[tok_wf1], evac_deps=evac_deps)
                f1_toks.append(tok)
            ln2t_tok[0] = bank_state[(bank_rr[0] - 1) % 8]["group"]
            bf1[c] = f1_toks

        def b_f2(c):
            h1Tv = h1T0 if c == 0 else h1T
            h2Tv = h2T0 if c == 0 else h2T
            f2_toks = []
            for E in range(8):
                def mk_mm(g, E, h1Tv=h1Tv):
                    def m(e, sems, out):
                        return e.matmul(out[:, :],
                                        wf1v[:, g, 128 * E:128 * (E + 1)],
                                        h1Tv[:, g, :], start=(g == 0),
                                        stop=(g == 7))
                    return m
                if c == 0:
                    evac_deps = [ln2t_tok[0]]
                else:
                    evac_deps = [scan1_end]
                if h2t_tok[0] is not None:
                    evac_deps.append(h2t_tok[0])
                tok = mm_group([(mk_mm(g, E), bf1[c][g]) for g in range(8)],
                               "scalar",
                               lambda e, sems, bank, E=E, h2Tv=h2Tv:
                               e.activation(
                                   h2Tv[:, E, :], bank[:, :], AF.Copy),
                               deps=[("swf2", 16, "gpsimd")],
                               evac_deps=evac_deps)
                f2_toks.append(tok)
            h1t_tok[0] = bank_state[(bank_rr[0] - 1) % 8]["group"]
            bf2[c] = f2_toks

        def b_f3(c):
            j0 = c * 4
            h2Tv = h2T0 if c == 0 else h2T
            for k in range(4):
                j = j0 + k
                y_toks = []
                for ec in range(2):
                    def mk_mm(g, k, ec, h2Tv=h2Tv):
                        def m(e, sems, out):
                            return e.matmul(out[:, :],
                                            h2Tv[:, g, 128 * k:128 * (k + 1)],
                                            wfv[:, 2, g, 512 * ec:512 * (ec + 1)],
                                            start=(g == 0), stop=(g == 7))
                        return m

                    def evac(e, sems, bank, ec=ec):
                        return e.tensor_tensor(
                            btmp[:, 2 * ec, :], bank[:, :],
                            biasb[:, 512 * ec:512 * (ec + 1)], TT.add)
                    if btmp_tok[ec] is not None:
                        evac_deps_b = [btmp_tok[ec]]
                    else:
                        evac_deps_b = [proj_tok[0]]
                    tok_b = mm_group([(mk_mm(g, k, ec), bf2[c][g])
                                      for g in range(8)],
                                     "vector", evac,
                                     deps=[("swf3", 16, "gpsimd")],
                                     evac_deps=evac_deps_b)
                    p.wait("scalar", tok_b)
                    tok_tanh = sop(lambda e, sems, ec=ec: e.activation(
                        btmp[:, 2 * ec + 1, :], btmp[:, 2 * ec, :], AF.Tanh),
                        inc="s")
                    btmp_tok[ec] = tok_tanh
                    p.wait("vector", tok_tanh)
                    tok_y = vop(lambda e, sems, j=j, ec=ec: e.tensor_tensor(
                        x_sb[:, j, 512 * ec:512 * (ec + 1)],
                        x_sb[:, j, 512 * ec:512 * (ec + 1)],
                        btmp[:, 2 * ec + 1, :], TT.add), inc="v")
                    y_toks.append(tok_y)
                    p.wait("scalar", tok_y)
                for tk in y_toks:
                    p.wait("sync", tk)
                p.op("sync", lambda e, sems, j=j: e.dma_start(
                    out=y_d[128 * j:128 * (j + 1), :], in_=x_sb[:, j, :]),
                    inc="sy", amt=16)
            h2t_tok[0] = bank_state[(bank_rr[0] - 1) % 8]["group"]


        abcd_chain(0, [tok_fold],
                   mid=lambda: norm_half(0, p.cur("v", "vector")))
        norm_half(1, p.cur("v", "vector"))
        scan_core(0)
        r_apply(0)
        scan0_end = half_tok[0]
        tok_dc0 = scan0_end
        abcd_chain(1, [scan0_end], mid=lambda: norm_half(0, tok_dc0))
        tok_mm1 = p.cur("t", "tensor")
        tok_abcd1_s = p.cur("s", "scalar")
        norm_half(1, norm_mul_tok[0])
        r_apply(1)
        p.wait("gpsimd", tok_mm1)
        p.wait("gpsimd", tok_abcd1_s)
        tok_wf1 = p.op("gpsimd", lambda e, sems: e.dma_start(
            out=wfv[:, 0, :, :], in_=wf_d[:, 0, :, :]), inc="swf", amt=16)
        proj_x2(0, tok_dc0, 0, 4)
        b_stats(0)
        proj_x2(0, tok_dc0, 4, 8)
        scan1_end = None
        b_front(0)
        scan_core(1)
        scan1_end = half_tok[0]
        p.wait("gpsimd", pk1_free[0])
        p.op("gpsimd", lambda e, sems: e.dma_start(
            out=wf1v[:, :, :], in_=wf_d[:, 1, :, :]), inc="swf2", amt=16)
        p.wait("gpsimd", scan1_end)
        tok_wf = p.op("gpsimd", lambda e, sems: e.dma_start(
            out=wfv[:, 2, :, :], in_=wf_d[:, 2, :, :]), inc="swf3", amt=16)

        tok_dc1 = scan1_end
        proj_x2(1, tok_dc1)
        b_stats(1)
        b_stats(2)
        b_stats(3)
        b_applies(1)
        b_applies(2)
        b_applies(3)
        b_f2(0)
        b_front(1)
        b_f3(0)
        b_f2(1)
        b_front(2)
        b_f3(1)
        b_f2(2)
        b_front(3)
        b_f3(2)
        b_f2(3)
        b_f3(3)

        p.op("sync", lambda e, sems: e.wait_ge(sems["sy"], p.val["sy"]))

        with nc.Block() as block:
            @block.sync
            def _(e):
                for f in p.ops["sync"]:
                    f(e, sems)

            @block.gpsimd
            def _(e):
                for f in p.ops["gpsimd"]:
                    f(e, sems)

            @block.tensor
            def _(e):
                for f in p.ops["tensor"]:
                    f(e, sems)

            @block.vector
            def _(e):
                for f in p.ops["vector"]:
                    f(e, sems)

            @block.scalar
            def _(e):
                for f in p.ops["scalar"]:
                    f(e, sems)

    return nc


def make_weights(L0_attn_W_m, L0_attn_v_vec, L0_attn_W_proj, L0_attn_out_scale,
                 L0_ffn_weight, L0_ffn_aux_weight1, L0_ffn_aux_weight2,
                 L0_ffn_bias):
    f32 = np.float32
    Wm = np.asarray(L0_attn_W_m, f32)
    Wp = np.asarray(L0_attn_W_proj, f32)
    vv = np.asarray(L0_attn_v_vec, f32)
    osc = np.asarray(L0_attn_out_scale, f32)
    Wf = [np.asarray(w, f32) for w in
          (L0_ffn_weight, L0_ffn_aux_weight1, L0_ffn_aux_weight2)]
    bias = np.asarray(L0_ffn_bias, f32)

    Wm4 = Wm.reshape(256, 4, 1024)
    ws = np.empty((128, 8, 8, 128), f32)
    for e in range(4):
        for h in range(2):
            blk = Wm4[128 * h:128 * (h + 1), e, :]
            ws[:, e * 2 + h, :, :] = blk.T.reshape(8, 128, 128).transpose(1, 0, 2)
    wp = np.empty((128, 4, 1024), f32)
    for h in range(2):
        for comp in range(2):
            cols = 2 * (128 * h + np.arange(128)) + comp
            wp[:, 2 * comp + h, :] = Wp[:, cols].T
    wf = np.stack([w.reshape(8, 128, 1024).transpose(1, 0, 2) for w in Wf], axis=1)
    sb = np.ascontiguousarray(np.broadcast_to(osc, (128, 1024)))
    bb = np.ascontiguousarray(np.broadcast_to(bias, (128, 1024)))
    vi = np.empty((128, 2, 2), f32)
    for h in range(2):
        vi[:, h, 0] = vv[128 * h:128 * (h + 1), 0, 0]
        vi[:, h, 1] = vv[128 * h:128 * (h + 1), 1, 0]
    idm = np.eye(128, dtype=ml_dtypes.bfloat16)
    return {"w_scan": np.ascontiguousarray(ws), "w_proj": np.ascontiguousarray(wp),
            "w_ffn": np.ascontiguousarray(wf), "scale_b": sb, "bias_b": bb,
            "v_init": vi, "ident": idm}


def kernel(L0_attn_x, L0_attn_W_m, L0_attn_v_vec, L0_attn_W_proj,
           L0_attn_out_scale, L0_ffn_weight, L0_ffn_aux_weight1,
           L0_ffn_aux_weight2, L0_ffn_bias):
    x = np.asarray(L0_attn_x, np.float32)
    wmap = make_weights(L0_attn_W_m, L0_attn_v_vec, L0_attn_W_proj,
                        L0_attn_out_scale, L0_ffn_weight, L0_ffn_aux_weight1,
                        L0_ffn_aux_weight2, L0_ffn_bias)
    nc = build_nc()
    in_maps = []
    for i in range(8):
        m = dict(wmap)
        m["x"] = np.ascontiguousarray(x[i])
        in_maps.append(m)
    res = run_bass_kernel_spmd(nc, in_maps, core_ids=list(range(8)))
    out = np.stack([np.asarray(r["y"], np.float32) for r in res.results], axis=0)
    return out


# version 21
# speedup vs baseline: 1.2374x; 1.0199x over previous
import numpy as np
import ml_dtypes

import concourse.bass as bass
import concourse.mybir as mybir
from concourse.bass_utils import run_bass_kernel_spmd

F32 = mybir.dt.float32
BF16 = mybir.dt.bfloat16
OP = mybir.AluOpType
AF = mybir.ActivationFunctionType

T = 2048
D = 1024
NT = 16
NSC = 2
EPS_LN = 1e-5
EPS_RMS = 1.1920929e-07

ENGINES = ("sync", "gpsimd", "tensor", "vector", "scalar")

A_XNT = 0
A_ABCD = 4096
A_PK1 = 12288
A_PK2 = 16384
A_PK3 = 18432
A_Q = 19456
A_R = 19968
A_SIG = 20992
A_SCALEB = 21760
A_END = 23168
A_WPX = 4096
A_TL1 = 16384
A_TL2 = 18432
A_TL3 = 19456
A_TL4 = 19968
A_TDB = 20992
B_WF = 0
B_LN2T = 16384
B_H1T = 18432
B_H2T = 20480
B_TMP = 12288

COMB = (("a", "a", "a", "b", "c"), ("b", "a", "b", "b", "d"),
        ("c", "c", "a", "d", "c"), ("d", "c", "b", "d", "d"))
NAMES = ("a", "b", "c", "d")


class Plan:
    def __init__(self):
        self.ops = {e: [] for e in ENGINES}
        self.val = {}
        self.last_wait = {}
        self.since_drain = {e: 0 for e in ENGINES}

    def op(self, eng, fn, inc=None, amt=1):
        self.since_drain[eng] += 1
        if inc is not None:
            self.val[inc] = self.val.get(inc, 0) + amt
            v = self.val[inc]

            def wrapped(e, sems, fn=fn, inc=inc, amt=amt):
                fn(e, sems).then_inc(sems[inc], amt)

            self.ops[eng].append(wrapped)
            return (inc, v, eng)
        self.ops[eng].append(fn)
        return None

    def drain(self, eng):
        if self.since_drain[eng] == 0:
            return
        self.since_drain[eng] = 0
        self.ops[eng].append(lambda e, sems: e.drain())

    def wait(self, eng, tok):
        if tok is None:
            return
        sem, v, producer = tok
        if producer == eng:
            self.drain(eng)
            return
        key = (eng, sem)
        if self.last_wait.get(key, -1) >= v:
            return
        self.last_wait[key] = v

        def w(e, sems, sem=sem, v=v):
            e.wait_ge(sems[sem], v)

        self.ops[eng].append(w)

    def cur(self, sem, eng="?"):
        return (sem, self.val.get(sem, 0), eng)


def build_nc():
    nc = bass.Bass("TRN2", target_bir_lowering=False)

    x_d = nc.dram_tensor("x", [T, D], F32, kind="ExternalInput").ap()
    y_d = nc.dram_tensor("y", [T, D], F32, kind="ExternalOutput").ap()
    ws_d = nc.dram_tensor("w_scan", [128, 8, 8, 128], F32, kind="ExternalInput").ap()
    wp_d = nc.dram_tensor("w_proj", [128, 4, 1024], F32, kind="ExternalInput").ap()
    wf_d = nc.dram_tensor("w_ffn", [128, 3, 8, 1024], F32, kind="ExternalInput").ap()
    sb_d = nc.dram_tensor("scale_b", [128, 1024], F32, kind="ExternalInput").ap()
    bb_d = nc.dram_tensor("bias_b", [128, 1024], F32, kind="ExternalInput").ap()
    vi_d = nc.dram_tensor("v_init", [128, 2, 2], F32, kind="ExternalInput").ap()
    id_d = nc.dram_tensor("ident", [128, 128], BF16, kind="ExternalInput").ap()

    p = Plan()
    import contextlib
    stack = contextlib.ExitStack()
    with stack:
        arena_base = (nc._sbuf_addr_for_side("left") + 31) // 32 * 32
        arena = stack.enter_context(nc.sbuf_tensor("arena", [128, A_END], F32))
        x_sb = stack.enter_context(nc.sbuf_tensor("x_sb", [128, NT, D], F32))
        xn_sb = stack.enter_context(nc.sbuf_tensor("xn_sb", [128, 4, D], BF16))
        weh_base = (nc._sbuf_addr_for_side("left") + 31) // 32 * 32
        weh = stack.enter_context(nc.sbuf_tensor("weh", [128, 8, 8, 128], BF16))
        ln2T0 = nc.alloc_sbuf_tensor_at(
            "ln2T0", [128, 8, 512], BF16, offset=weh_base).ap()
        h1T0 = nc.alloc_sbuf_tensor_at(
            "h1T0", [128, 8, 512], BF16, offset=weh_base + 8192).ap()
        h2T0 = nc.alloc_sbuf_tensor_at(
            "h2T0", [128, 8, 512], BF16, offset=weh_base).ap()
        wpb = stack.enter_context(nc.sbuf_tensor("wpb", [128, 4, D], BF16))
        biasb = stack.enter_context(nc.sbuf_tensor("biasb", [128, D], F32))
        vinit = stack.enter_context(nc.sbuf_tensor("vinit", [128, 2, 2], F32))
        id_sb = stack.enter_context(nc.sbuf_tensor("id_sb", [128, 128], BF16))
        stats = stack.enter_context(nc.sbuf_tensor("stats", [128, NT, 2], F32))
        bn6 = stack.enter_context(nc.sbuf_tensor("bn6", [128, 8, 12], F32))
        st_t0 = stack.enter_context(nc.sbuf_tensor("st_t0", [128, NT], F32))
        st_s1 = stack.enter_context(nc.sbuf_tensor("st_s1", [128, NT], F32))
        st_s2 = stack.enter_context(nc.sbuf_tensor("st_s2", [128, NT], F32))
        st_r1 = stack.enter_context(nc.sbuf_tensor("st_r1", [128, NT], F32))
        st_rn = stack.enter_context(nc.sbuf_tensor("st_rn", [128, NT], F32))
        st_bxn = stack.enter_context(nc.sbuf_tensor("st_bxn", [128, NT], F32))
        st_sr = stack.enter_context(nc.sbuf_tensor("st_sr", [128, NT], F32))
        st_br = stack.enter_context(nc.sbuf_tensor("st_br", [128, NT], F32))
        hxybf_base = (nc._sbuf_addr_for_side("left") + 31) // 32 * 32
        hxybf = stack.enter_context(nc.sbuf_tensor("hxybf", [128, 2, 4, D], BF16))
        btmpv = nc.alloc_sbuf_tensor_at(
            "btmpv", [128, 4, 512], F32, offset=hxybf_base).ap()
        wf1v = None
        psum = [stack.enter_context(nc.psum_tensor(f"ps{i}", [128, 512], F32))
                for i in range(8)]
        sems = {}
        for s in ("sx", "sx2", "sx3", "sx4", "sw", "swf", "swf2", "swf3", "sy",
                  "t", "v", "s", "g"):
            sems[s] = stack.enter_context(nc.semaphore(f"sem_{s}"))

        def av(off, n):
            return arena[:, off:off + n]

        xnT = nc.alloc_sbuf_tensor_at(
            "xnT", [128, 8, 1024], BF16, offset=arena_base + A_XNT * 4).ap()
        abcd = av(A_ABCD, 8192).rearrange("p (i t) -> p i t", i=8)
        ntmp = av(A_PK1, 4096).rearrange("p (i t) -> p i t", i=4)
        PK1 = av(A_PK1, 4096).rearrange("p (i h t) -> p i h t", i=4, h=2)
        PK2 = av(A_PK2, 2048).rearrange("p (i h t) -> p i h t", i=4, h=2)
        PK3 = av(A_PK3, 1024).rearrange("p (i h t) -> p i h t", i=4, h=2)
        Qv = av(A_Q, 512).rearrange("p (i h t) -> p i h t", i=4, h=2)
        Rv = av(A_R, 1024).rearrange("p (b i h t) -> p b i h t", b=2, i=4, h=2)
        SIG = av(A_SIG, 768).rearrange("p (i h t) -> p i h t", i=6, h=2)
        TL1 = av(A_TL1, 4096).rearrange("p (i h t) -> p i h t", i=4, h=2)
        TL2 = av(A_TL2, 2048).rearrange("p (i h t) -> p i h t", i=4, h=2)
        TL3 = av(A_TL3, 1024).rearrange("p (i h t) -> p i h t", i=4, h=2)
        TL4 = av(A_TL4, 512).rearrange("p (i h t) -> p i h t", i=4, h=2)
        TDB = av(A_TDB, 504).rearrange("p (i t) -> p i t", i=4)
        hxy = av(A_PK1, 4096).rearrange("p (i t) -> p i t", i=4)
        wpx = av(A_WPX, 4096).rearrange("p (i t) -> p i t", i=4)
        scaleb = av(A_SCALEB, 1024)
        wf1v = nc.alloc_sbuf_tensor_at(
            "wf1v", [128, 8, 1024], BF16, offset=arena_base + A_PK1 * 4).ap()
        wfv = nc.alloc_sbuf_tensor_at(
            "wfv", [128, 3, 8, 1024], BF16, offset=arena_base + B_WF * 4).ap()
        xn2_sb = nc.alloc_sbuf_tensor_at(
            "xn2_sb", [128, 8, D], BF16, offset=arena_base + A_ABCD * 4).ap()
        ln2T = nc.alloc_sbuf_tensor_at(
            "ln2T", [128, 8, 512], BF16, offset=arena_base + B_LN2T * 4).ap()
        h1T = nc.alloc_sbuf_tensor_at(
            "h1T", [128, 8, 512], BF16, offset=arena_base + B_H1T * 4).ap()
        h2T = nc.alloc_sbuf_tensor_at(
            "h2T", [128, 8, 512], BF16, offset=arena_base + B_H2T * 4).ap()
        btmp = btmpv

        def vop(fn, inc=None):
            return p.op("vector", fn, inc=inc)

        def sop(fn, inc=None):
            return p.op("scalar", fn, inc=inc)

        TT = OP

        bank_state = [{"evac": None, "group": None} for _ in range(8)]
        bank_rr = [0]

        def mm_group(mms, evac_eng, evac_fn, deps=(), evac_deps=()):
            b = bank_rr[0] % 8
            bank_rr[0] += 1
            st = bank_state[b]
            if st["evac"] is not None:
                p.wait("tensor", st["evac"])
            for d in deps:
                p.wait("tensor", d)
            n = len(mms)
            for i, m in enumerate(mms):
                if isinstance(m, tuple):
                    m, mdep = m
                    p.wait("tensor", mdep)
                if i == n - 1:
                    tok = p.op("tensor", lambda e, sems, m=m, b=b: m(e, sems, psum[b]),
                               inc="t")
                else:
                    p.op("tensor", lambda e, sems, m=m, b=b: m(e, sems, psum[b]))
            st["group"] = tok
            p.wait(evac_eng, tok)
            for d in evac_deps:
                p.wait(evac_eng, d)
            esem = {"scalar": "s", "vector": "v"}[evac_eng]
            if not isinstance(evac_fn, (list, tuple)):
                evac_fn = [evac_fn]
            for fi, ef in enumerate(evac_fn):
                if fi == len(evac_fn) - 1:
                    etok = p.op(evac_eng, lambda e, sems, b=b, ef=ef:
                                ef(e, sems, psum[b]), inc=esem)
                else:
                    p.op(evac_eng, lambda e, sems, b=b, ef=ef: ef(e, sems, psum[b]))
                    p.drain(evac_eng)
            st["evac"] = etok
            return etok

        x_src = x_d.rearrange("(n p) d -> p n d", p=128)
        tok_x_q = []
        for q in range(4):
            tok_x_q.append(p.op("sync", lambda e, sems, q=q: e.dma_start(
                out=x_sb[:, 4 * q:4 * (q + 1), :], in_=x_src[:, 4 * q:4 * (q + 1), :]),
                inc=("sx", "sx2", "sx3", "sx4")[q], amt=16))

        def wdma(out_ap, in_ap):
            return p.op("gpsimd", lambda e, sems: e.dma_start(out=out_ap, in_=in_ap),
                        inc="sw", amt=16)

        wdma(weh[:, :, :, :], ws_d)
        wdma(wpx[:, :, :], wp_d)
        wdma(scaleb, sb_d)
        wdma(biasb[:, :], bb_d)
        wdma(vinit[:, :, :], vi_d)
        wdma(id_sb[:, :], id_d)
        tok_w = p.cur("sw", "gpsimd")

        p.wait("vector", tok_w)
        for i in range(4):
            tok_fold = vop(lambda e, sems, i=i: e.tensor_tensor(
                wpb[:, i, :], wpx[:, i, :], scaleb, TT.mult), inc="v")

        xn_slot_tok = [None] * 4
        half_tok = [None]
        pk1_free = [None]
        proj_tok = [None, None]
        ln2t_tok = [None]
        h1t_tok = [None]
        h2t_tok = [None]

        def combine(dst, L, R, tmps, extra_layer1=()):
            for (od, f1, f2, g1, g2) in COMB:
                vop(lambda e, sems, od=od, f1=f1, f2=f2: e.tensor_tensor(
                    dst[od], L[f1], R[f2], TT.mult))
                vop(lambda e, sems, od=od, g1=g1, g2=g2: e.tensor_tensor(
                    tmps[od], L[g1], R[g2], TT.mult))
            for fn in extra_layer1:
                vop(fn)
            p.drain("vector")
            for od in NAMES:
                vop(lambda e, sems, od=od: e.tensor_tensor(
                    dst[od], dst[od], tmps[od], TT.add))
            p.drain("vector")

        def pairs(viewmap):
            return {nm: viewmap[nm].rearrange("p (u two) -> p u two", two=2)
                    for nm in NAMES}

        stat_toks = []
        stat_toks2 = []
        apply_tok_j = [None] * NT
        for sc in range(NSC):
            j0 = sc * 8
            p.drain("vector")
            for jj in range(8):
                j = j0 + jj
                p.wait("vector", tok_x_q[j // 4])
                vop(lambda e, sems, j=j, jj=jj: e.bn_stats(
                    bn6[:, jj, 0:6], x_sb[:, j, 0:512]))
                vop(lambda e, sems, j=j, jj=jj: e.bn_stats(
                    bn6[:, jj, 6:12], x_sb[:, j, 512:1024]))
            p.drain("vector")
            for jj in range(8):
                vop(lambda e, sems, j=j0 + jj, jj=jj: e.bn_aggr(
                    stats[:, j, :], bn6[:, jj, 0:12]))
            p.drain("vector")
            sl = slice(j0, j0 + 8)
            mean = stats[:, sl, 0]
            var = stats[:, sl, 1]
            vop(lambda e, sems, sl=sl, var=var: e.tensor_scalar(
                st_t0[:, sl], var, EPS_LN, None, TT.add))
            tok_v0 = vop(lambda e, sems, sl=sl, var=var: e.tensor_scalar(
                st_s2[:, sl], var, 1.0 + EPS_RMS, EPS_LN * EPS_RMS, TT.mult, TT.add),
                inc="v")
            p.wait("scalar", tok_v0)
            sop(lambda e, sems, sl=sl: e.activation(
                st_s1[:, sl], st_t0[:, sl], AF.Ln))
            sop(lambda e, sems, sl=sl: e.activation(
                st_s2[:, sl], st_s2[:, sl], AF.Ln))
            p.drain("scalar")
            sop(lambda e, sems, sl=sl: e.activation(
                st_r1[:, sl], st_s1[:, sl], AF.Exp, scale=-0.5))
            tok_e = sop(lambda e, sems, sl=sl: e.activation(
                st_rn[:, sl], st_s2[:, sl], AF.Exp, scale=-0.5), inc="s")
            p.wait("vector", tok_e)
            tok_stat = vop(lambda e, sems, sl=sl, mean=mean: e.scalar_tensor_tensor(
                st_bxn[:, sl], mean, -1.0, st_rn[:, sl], TT.mult, TT.mult), inc="v")
            vop(lambda e, sems, sl=sl: e.tensor_scalar(
                st_sr[:, sl], st_r1[:, sl], 1.0, None, TT.add))
            tok_stat2 = vop(lambda e, sems, sl=sl, mean=mean: e.scalar_tensor_tensor(
                st_br[:, sl], mean, -1.0, st_r1[:, sl], TT.mult, TT.mult), inc="v")
            stat_toks.append(tok_stat)
            stat_toks2.append(tok_stat2)

        def abcd_chain(sc, extra_evac_deps, mid=None):
            j0 = sc * 8
            ee = "vector" if sc == 0 else "scalar"
            tr_tok = {}
            for cc in range(2):
                for k in range(4):
                    j = j0 + cc * 4 + k
                    p.wait("scalar", stat_toks[sc])
                    if xn_slot_tok[k] is not None:
                        p.wait("scalar", xn_slot_tok[k])
                    atok = sop(lambda e, sems, j=j, k=k: e.activation(
                        xn_sb[:, k, :], x_sb[:, j, :], AF.Identity,
                        bias=st_bxn[:, j:j + 1], scale=st_rn[:, j:j + 1]), inc="s")
                    apply_tok_j[j] = atok
                    for gh in range(2):
                        def mk_mm(k, g):
                            def m(e, sems, out):
                                return e.matmul(
                                    out[:, 128 * (g % 4):128 * (g % 4 + 1)],
                                    xn_sb[:, k, 128 * g:128 * (g + 1)],
                                    id_sb[:, :],
                                    start=(g % 4 == 0), stop=(g % 4 == 3))
                            return m
                        def tr_evac(e, sems, bank, k=k, cc=cc, gh=gh):
                            dst = xnT[:, 4 * gh:4 * gh + 4,
                                      128 * (4 * cc + k):128 * (4 * cc + k + 1)]
                            if sc == 0:
                                return e.tensor_copy(dst, bank[:, :])
                            return e.activation(dst, bank[:, :], AF.Copy)
                        tok = mm_group(
                            [mk_mm(k, g) for g in range(4 * gh, 4 * gh + 4)],
                            ee, tr_evac, deps=[atok, tok_w])
                        tr_tok[(cc, gh)] = tok
                    xn_slot_tok[k] = bank_state[(bank_rr[0] - 1) % 8]["group"]

            def mm_i(i):
                for cc in range(2):
                    def mk_mm(g, i, cc):
                        def m(e, sems, out):
                            return e.matmul(out[:, :], weh[:, i, g, :],
                                            xnT[:, g, 512 * cc:512 * (cc + 1)],
                                            start=(g == 0), stop=(g == 7))
                        return m
                    def ab_evac(e, sems, bank, i=i, cc=cc):
                        dst = abcd[:, i, 512 * cc:512 * (cc + 1)]
                        if sc == 0:
                            return e.tensor_copy(dst, bank[:, :])
                        return e.activation(dst, bank[:, :], AF.Copy)
                    mm_group(
                        [(mk_mm(g, i, cc), tr_tok[(cc, g // 4)]) for g in range(8)],
                        ee, ab_evac,
                        deps=[tok_w], evac_deps=list(extra_evac_deps))

            for i in (0, 2, 4, 6):
                mm_i(i)
            if mid is not None:
                mid()
            for i in (1, 3, 5, 7):
                mm_i(i)

        norm_mul_tok = [None]

        def norm_half(h, gate):
            A, B, C, Dt = (abcd[:, 0 + h, :], abcd[:, 2 + h, :],
                           abcd[:, 4 + h, :], abcd[:, 6 + h, :])
            p.drain("scalar")
            if gate is not None:
                p.wait("scalar", gate)
            for ti, M_ in enumerate((A, B, C, Dt)):
                tok_sq = sop(lambda e, sems, M_=M_, ti=ti: e.activation(
                    ntmp[:, ti, :], M_, AF.Square), inc="s")
            p.wait("vector", tok_sq)
            vop(lambda e, sems: e.tensor_tensor(
                ntmp[:, 0, :], ntmp[:, 0, :], ntmp[:, 1, :], TT.add))
            vop(lambda e, sems: e.tensor_tensor(
                ntmp[:, 2, :], ntmp[:, 2, :], ntmp[:, 3, :], TT.add))
            p.drain("vector")
            vop(lambda e, sems: e.tensor_tensor(
                ntmp[:, 0, :], ntmp[:, 0, :], ntmp[:, 2, :], TT.add))
            p.drain("vector")
            tok_clip = vop(lambda e, sems: e.tensor_scalar(
                ntmp[:, 0, :], ntmp[:, 0, :], 1e-12, None, TT.max), inc="v")
            p.wait("scalar", tok_clip)
            sop(lambda e, sems: e.activation(
                ntmp[:, 1, :], ntmp[:, 0, :], AF.Ln))
            p.drain("scalar")
            tok_s = sop(lambda e, sems: e.activation(
                ntmp[:, 0, :], ntmp[:, 1, :], AF.Exp, scale=-0.5), inc="s")
            p.wait("vector", tok_s)
            for mi, M_ in enumerate((A, B, C, Dt)):
                if mi == 3:
                    norm_mul_tok[0] = vop(lambda e, sems, M_=M_: e.tensor_tensor(
                        M_, M_, ntmp[:, 0, :], TT.mult), inc="v")
                else:
                    vop(lambda e, sems, M_=M_: e.tensor_tensor(
                        M_, M_, ntmp[:, 0, :], TT.mult))
            p.drain("vector")

        def scan_core(sc):
            def epack(n):
                out = {}
                for ei, nm in enumerate(NAMES):
                    out[nm] = abcd[:, 2 * ei:2 * ei + 2, :].rearrange(
                        "p h (u two) -> p h u two", two=2)
                return out

            pr = epack(512)
            combine({nm: PK1[:, ei, :, :] for ei, nm in enumerate(NAMES)},
                    {nm: pr[nm][:, :, :, 1] for nm in NAMES},
                    {nm: pr[nm][:, :, :, 0] for nm in NAMES},
                    {nm: TL1[:, ei, :, :] for ei, nm in enumerate(NAMES)})
            pr = {nm: PK1[:, ei, :, :].rearrange("p h (u two) -> p h u two", two=2)
                  for ei, nm in enumerate(NAMES)}
            combine({nm: PK2[:, ei, :, :] for ei, nm in enumerate(NAMES)},
                    {nm: pr[nm][:, :, :, 1] for nm in NAMES},
                    {nm: pr[nm][:, :, :, 0] for nm in NAMES},
                    {nm: TL2[:, ei, :, :] for ei, nm in enumerate(NAMES)})
            pr = {nm: PK2[:, ei, :, :].rearrange("p h (u two) -> p h u two", two=2)
                  for ei, nm in enumerate(NAMES)}
            combine({nm: PK3[:, ei, :, :] for ei, nm in enumerate(NAMES)},
                    {nm: pr[nm][:, :, :, 1] for nm in NAMES},
                    {nm: pr[nm][:, :, :, 0] for nm in NAMES},
                    {nm: TL3[:, ei, :, :] for ei, nm in enumerate(NAMES)})
            pr = {nm: PK3[:, ei, :, :].rearrange("p h (u two) -> p h u two", two=2)
                  for ei, nm in enumerate(NAMES)}
            combine({nm: Qv[:, ei, :, :] for ei, nm in enumerate(NAMES)},
                    {nm: pr[nm][:, :, :, 1] for nm in NAMES},
                    {nm: pr[nm][:, :, :, 0] for nm in NAMES},
                    {nm: TL4[:, ei, :, :] for ei, nm in enumerate(NAMES)})

            cur, nxt = 0, 1
            first = True
            s = 1
            while s < 64:
                srcv = {nm: (Qv[:, ei, :, :] if first else Rv[:, cur, ei, :, :])
                        for ei, nm in enumerate(NAMES)}
                dstv = {nm: Rv[:, nxt, ei, :, :] for ei, nm in enumerate(NAMES)}
                heads = [
                    (lambda e, sems, d=dstv[nm], sv=srcv[nm], s=s:
                     e.tensor_copy(d[:, :, 0:s], sv[:, :, 0:s])) for nm in NAMES]
                combine({nm: dstv[nm][:, :, s:64] for nm in NAMES},
                        {nm: srcv[nm][:, :, s:64] for nm in NAMES},
                        {nm: srcv[nm][:, :, 0:64 - s] for nm in NAMES},
                        {nm: TDB[:, ei, 0:2 * (64 - s)].rearrange(
                            "p (h t) -> p h t", h=2)
                         for ei, nm in enumerate(NAMES)},
                        extra_layer1=heads)
                cur, nxt = nxt, cur
                first = False
                s *= 2
            Rf = {nm: Rv[:, cur, ei, :, :] for ei, nm in enumerate(NAMES)}

            cxb = vinit[:, :, 0:1].broadcast_to([128, 2, 64])
            cyb = vinit[:, :, 1:2].broadcast_to([128, 2, 64])
            vop(lambda e, sems, Rf=Rf, cxb=cxb: e.tensor_tensor(
                SIG[:, 0, :, :], Rf["a"], cxb, TT.mult))
            vop(lambda e, sems, Rf=Rf, cyb=cyb: e.tensor_tensor(
                SIG[:, 4, :, :], Rf["b"], cyb, TT.mult))
            vop(lambda e, sems, Rf=Rf, cxb=cxb: e.tensor_tensor(
                SIG[:, 1, :, :], Rf["c"], cxb, TT.mult))
            vop(lambda e, sems, Rf=Rf, cyb=cyb: e.tensor_tensor(
                SIG[:, 5, :, :], Rf["d"], cyb, TT.mult))
            p.drain("vector")
            vop(lambda e, sems: e.tensor_tensor(
                SIG[:, 0, :, :], SIG[:, 0, :, :], SIG[:, 4, :, :], TT.add))
            vop(lambda e, sems: e.tensor_tensor(
                SIG[:, 1, :, :], SIG[:, 1, :, :], SIG[:, 5, :, :], TT.add))
            p.drain("vector")
            vop(lambda e, sems: e.tensor_copy(SIG[:, 2, :, 1:64], SIG[:, 0, :, 0:63]))
            vop(lambda e, sems: e.tensor_copy(SIG[:, 2, :, 0:1], vinit[:, :, 0:1]))
            vop(lambda e, sems: e.tensor_copy(SIG[:, 3, :, 1:64], SIG[:, 1, :, 0:63]))
            vop(lambda e, sems: e.tensor_copy(SIG[:, 3, :, 0:1], vinit[:, :, 1:2]))
            p.drain("vector")
            vop(lambda e, sems: e.tensor_copy(vinit[:, :, 0:1], SIG[:, 0, :, 63:64]))
            vop(lambda e, sems: e.tensor_copy(vinit[:, :, 1:2], SIG[:, 1, :, 63:64]))
            p.drain("vector")

            archx = av(A_PK2, 1152).rearrange("p (j h u) -> p j h u", j=9, h=2)
            archy = av(A_PK2 + 1152, 1152).rearrange("p (j h u) -> p j h u",
                                                     j=9, h=2)
            etmp1 = av(A_R, 1024)
            etmp2 = av(18688, 1024)
            PRV = {nm: PK1[:, ei, :, :].rearrange("p h (u j) -> p h u j", j=8)
                   for ei, nm in enumerate(NAMES)}
            vop(lambda e, sems: e.tensor_copy(archx[:, 0, :, :], SIG[:, 2, :, :]))
            vop(lambda e, sems: e.tensor_copy(archy[:, 0, :, :], SIG[:, 3, :, :]))
            p.drain("vector")
            for j in range(8):
                px = archx[:, j, :, :]
                py = archy[:, j, :, :]
                vop(lambda e, sems, j=j, px=px: e.tensor_tensor(
                    SIG[:, 4, :, :], PRV["a"][:, :, :, j], px, TT.mult))
                vop(lambda e, sems, j=j, px=px: e.tensor_tensor(
                    SIG[:, 5, :, :], PRV["c"][:, :, :, j], px, TT.mult))
                vop(lambda e, sems, j=j, py=py: e.tensor_tensor(
                    archx[:, j + 1, :, :], PRV["b"][:, :, :, j], py, TT.mult))
                vop(lambda e, sems, j=j, py=py: e.tensor_tensor(
                    archy[:, j + 1, :, :], PRV["d"][:, :, :, j], py, TT.mult))
                p.drain("vector")
                vop(lambda e, sems, j=j: e.tensor_tensor(
                    archx[:, j + 1, :, :], archx[:, j + 1, :, :],
                    SIG[:, 4, :, :], TT.add))
                if j == 7:
                    pk1_free[0] = vop(lambda e, sems, j=j: e.tensor_tensor(
                        archy[:, j + 1, :, :], archy[:, j + 1, :, :],
                        SIG[:, 5, :, :], TT.add), inc="v")
                else:
                    vop(lambda e, sems, j=j: e.tensor_tensor(
                        archy[:, j + 1, :, :], archy[:, j + 1, :, :],
                        SIG[:, 5, :, :], TT.add))
                p.drain("vector")
            bk = sc % 2
            HBx = hxybf[:, bk, 0:2, :].rearrange("p h (u r) -> p h u r", r=16)
            HBy = hxybf[:, bk, 2:4, :].rearrange("p h (u r) -> p h u r", r=16)
            arx = archx[:, 1:9, :, :].rearrange("p j h u -> p h u j")
            ary = archy[:, 1:9, :, :].rearrange("p j h u -> p h u j")
            vop(lambda e, sems: e.tensor_copy(HBx[:, :, :, 1:16:2], arx))
            vop(lambda e, sems: e.tensor_copy(HBy[:, :, :, 1:16:2], ary))
            Aev = {nm: abcd[:, 2 * ei:2 * ei + 2, :].rearrange(
                "p h (u r) -> p h u r", r=16)[:, :, :, 0:16:2]
                for ei, nm in enumerate(NAMES)}
            hpx = archx[:, 0:8, :, :].rearrange("p j h u -> p h u j")
            hpy = archy[:, 0:8, :, :].rearrange("p j h u -> p h u j")
            e1v = etmp1.rearrange("p (h u j) -> p h u j", h=2, u=64)
            e2v = etmp2.rearrange("p (h u j) -> p h u j", h=2, u=64)
            vop(lambda e, sems: e.tensor_tensor(e1v, Aev["a"], hpx, TT.mult))
            vop(lambda e, sems: e.tensor_tensor(e2v, Aev["b"], hpy, TT.mult))
            p.drain("vector")
            vop(lambda e, sems: e.tensor_tensor(
                HBx[:, :, :, 0:16:2], e1v, e2v, TT.add))
            p.drain("vector")
            vop(lambda e, sems: e.tensor_tensor(e1v, Aev["c"], hpx, TT.mult))
            vop(lambda e, sems: e.tensor_tensor(e2v, Aev["d"], hpy, TT.mult))
            p.drain("vector")
            half_tok[0] = vop(lambda e, sems: e.tensor_tensor(
                HBy[:, :, :, 0:16:2], e1v, e2v, TT.add), inc="v")
            p.drain("vector")

        def r_apply(sc):
            for jj in range(8):
                j = sc * 8 + jj
                p.wait("scalar", stat_toks2[sc])
                p.wait("scalar", apply_tok_j[j])
                apply_tok_j[j] = sop(lambda e, sems, j=j: e.activation(
                    x_sb[:, j, :], x_sb[:, j, :], AF.Identity,
                    bias=st_br[:, j:j + 1], scale=st_sr[:, j:j + 1]), inc="s")

        def proj_x2(sc, tok_dc, jlo=0, jhi=8):
            j0 = sc * 8
            for jj in range(jlo, jhi):
                j = j0 + jj
                for ec in range(2):
                    def mk_mm(i, jj, ec, bk=sc % 2):
                        def m(e, sems, out):
                            return e.matmul(out[:, :],
                                            hxybf[:, bk, i, 128 * jj:128 * (jj + 1)],
                                            wpb[:, i, 512 * ec:512 * (ec + 1)],
                                            start=(i == 0), stop=(i == 3))
                        return m

                    def ev2(e, sems, bank, j=j, ec=ec):
                        return e.tensor_tensor(
                            x_sb[:, j, 512 * ec:512 * (ec + 1)],
                            x_sb[:, j, 512 * ec:512 * (ec + 1)],
                            bank[:, :], TT.add)
                    mm_group([mk_mm(i, jj, ec) for i in range(2 * 2)],
                             "vector", ev2,
                             deps=[tok_dc, tok_fold],
                             evac_deps=[apply_tok_j[j]])
            proj_tok[sc] = bank_state[(bank_rr[0] - 1) % 8]["group"]

        btmp_tok = [None, None]
        bstat_toks = [None] * 4

        def b_stats(c):
            j0 = c * 4
            sl = slice(j0, j0 + 4)
            p.drain("vector")
            for k in range(4):
                j = j0 + k
                vop(lambda e, sems, j=j, k=k: e.bn_stats(
                    bn6[:, k, 0:6], x_sb[:, j, 0:512]))
                vop(lambda e, sems, j=j, k=k: e.bn_stats(
                    bn6[:, k, 6:12], x_sb[:, j, 512:1024]))
            p.drain("vector")
            for k in range(4):
                vop(lambda e, sems, j=j0 + k, k=k: e.bn_aggr(
                    stats[:, j, :], bn6[:, k, 0:12]))
            p.drain("vector")
            mean = stats[:, sl, 0]
            var = stats[:, sl, 1]
            tok_v0 = vop(lambda e, sems, sl=sl, var=var: e.tensor_scalar(
                st_t0[:, sl], var, EPS_LN, None, TT.add), inc="v")
            p.wait("scalar", tok_v0)
            sop(lambda e, sems, sl=sl: e.activation(
                st_s1[:, sl], st_t0[:, sl], AF.Ln))
            p.drain("scalar")
            tok_s0 = sop(lambda e, sems, sl=sl: e.activation(
                st_r1[:, sl], st_s1[:, sl], AF.Exp, scale=-0.5), inc="s")
            p.wait("vector", tok_s0)
            bstat_toks[c] = vop(lambda e, sems, sl=sl, mean=mean:
                                e.scalar_tensor_tensor(
                st_bxn[:, sl], mean, -1.0, st_r1[:, sl], TT.mult, TT.mult), inc="v")

        btr = {}
        bf1 = {}
        bf2 = {}

        b_apply = {}

        def b_applies(c):
            j0 = c * 4
            if c in (1, 2):
                xnt, base = xn2_sb, (c - 1) * 4
            else:
                xnt, base = xn_sb, 0
            apply_toks = []
            for k in range(4):
                j = j0 + k
                p.wait("scalar", bstat_toks[c])
                if c in (1, 2):
                    p.wait("scalar", scan1_end)
                elif xn_slot_tok[k] is not None:
                    p.wait("scalar", xn_slot_tok[k])
                tok = sop(lambda e, sems, j=j, k=k, xnt=xnt, base=base:
                          e.activation(
                    xnt[:, base + k, :], x_sb[:, j, :], AF.Identity,
                    bias=st_bxn[:, j:j + 1], scale=st_r1[:, j:j + 1]), inc="s")
                apply_toks.append(tok)
            b_apply[c] = (xnt, base, apply_toks)

        def b_front(c):
            ln2Tv = ln2T0 if c == 0 else ln2T
            h1Tv = h1T0 if c == 0 else h1T
            front_gate = tok_mm1 if c == 0 else scan1_end
            if c not in b_apply:
                b_applies(c)
            xnt, base, apply_toks = b_apply[c]
            tr_toks = []
            for g in range(8):
                def mk_mm(k, g, xnt=xnt, base=base):
                    def m(e, sems, out):
                        return e.matmul(out[:, 128 * k:128 * (k + 1)],
                                        xnt[:, base + k, 128 * g:128 * (g + 1)],
                                        id_sb[:, :], start=(k == 0), stop=(k == 3))
                    return m
                evac_deps = [front_gate]
                if ln2t_tok[0] is not None:
                    evac_deps.append(ln2t_tok[0])
                tok = mm_group([mk_mm(k, g) for k in range(4)], "scalar",
                               lambda e, sems, bank, g=g, ln2Tv=ln2Tv:
                               e.activation(
                                   ln2Tv[:, g, :], bank[:, :], AF.Copy),
                               deps=apply_toks, evac_deps=evac_deps)
                tr_toks.append(tok)
            if xnt is xn_sb:
                last_tr_group = bank_state[(bank_rr[0] - 1) % 8]["group"]
                for k in range(4):
                    xn_slot_tok[k] = last_tr_group
            btr[c] = tr_toks

            f1_toks = []
            for E in range(8):
                def mk_mm(g, E, ln2Tv=ln2Tv):
                    def m(e, sems, out):
                        return e.matmul(out[:, :],
                                        wfv[:, 0, g, 128 * E:128 * (E + 1)],
                                        ln2Tv[:, g, :], start=(g == 0),
                                        stop=(g == 7))
                    return m
                evac_deps = [front_gate]
                if h1t_tok[0] is not None:
                    evac_deps.append(h1t_tok[0])
                tok = mm_group([(mk_mm(g, E), tr_toks[g]) for g in range(8)],
                               "scalar",
                               lambda e, sems, bank, E=E, h1Tv=h1Tv:
                               e.activation(
                                   h1Tv[:, E, :], bank[:, :], AF.Copy),
                               deps=[tok_wf1], evac_deps=evac_deps)
                f1_toks.append(tok)
            ln2t_tok[0] = bank_state[(bank_rr[0] - 1) % 8]["group"]
            bf1[c] = f1_toks

        def b_f2(c):
            h1Tv = h1T0 if c == 0 else h1T
            h2Tv = h2T0 if c == 0 else h2T
            f2_toks = []
            for E in range(8):
                def mk_mm(g, E, h1Tv=h1Tv):
                    def m(e, sems, out):
                        return e.matmul(out[:, :],
                                        wf1v[:, g, 128 * E:128 * (E + 1)],
                                        h1Tv[:, g, :], start=(g == 0),
                                        stop=(g == 7))
                    return m
                if c == 0:
                    evac_deps = [ln2t_tok[0]]
                else:
                    evac_deps = [scan1_end]
                if h2t_tok[0] is not None:
                    evac_deps.append(h2t_tok[0])
                tok = mm_group([(mk_mm(g, E), bf1[c][g]) for g in range(8)],
                               "scalar",
                               lambda e, sems, bank, E=E, h2Tv=h2Tv:
                               e.activation(
                                   h2Tv[:, E, :], bank[:, :], AF.Copy),
                               deps=[("swf2", 16, "gpsimd")],
                               evac_deps=evac_deps)
                f2_toks.append(tok)
            h1t_tok[0] = bank_state[(bank_rr[0] - 1) % 8]["group"]
            bf2[c] = f2_toks

        def b_f3(c):
            j0 = c * 4
            h2Tv = h2T0 if c == 0 else h2T
            for k in range(4):
                j = j0 + k
                y_toks = []
                for ec in range(2):
                    def mk_mm(g, k, ec, h2Tv=h2Tv):
                        def m(e, sems, out):
                            return e.matmul(out[:, :],
                                            h2Tv[:, g, 128 * k:128 * (k + 1)],
                                            wfv[:, 2, g, 512 * ec:512 * (ec + 1)],
                                            start=(g == 0), stop=(g == 7))
                        return m

                    def evac(e, sems, bank, ec=ec):
                        return e.tensor_tensor(
                            btmp[:, 2 * ec, :], bank[:, :],
                            biasb[:, 512 * ec:512 * (ec + 1)], TT.add)
                    if btmp_tok[ec] is not None:
                        evac_deps_b = [btmp_tok[ec]]
                    else:
                        evac_deps_b = [proj_tok[0]]
                    tok_b = mm_group([(mk_mm(g, k, ec), bf2[c][g])
                                      for g in range(8)],
                                     "vector", evac,
                                     deps=[("swf3", 16, "gpsimd")],
                                     evac_deps=evac_deps_b)
                    p.wait("scalar", tok_b)
                    tok_tanh = sop(lambda e, sems, ec=ec: e.activation(
                        btmp[:, 2 * ec + 1, :], btmp[:, 2 * ec, :], AF.Tanh),
                        inc="s")
                    btmp_tok[ec] = tok_tanh
                    p.wait("vector", tok_tanh)
                    tok_y = vop(lambda e, sems, j=j, ec=ec: e.tensor_tensor(
                        x_sb[:, j, 512 * ec:512 * (ec + 1)],
                        x_sb[:, j, 512 * ec:512 * (ec + 1)],
                        btmp[:, 2 * ec + 1, :], TT.add), inc="v")
                    y_toks.append(tok_y)
                    p.wait("scalar", tok_y)
                for tk in y_toks:
                    p.wait("sync", tk)
                p.op("sync", lambda e, sems, j=j: e.dma_start(
                    out=y_d[128 * j:128 * (j + 1), :], in_=x_sb[:, j, :]),
                    inc="sy", amt=16)
            h2t_tok[0] = bank_state[(bank_rr[0] - 1) % 8]["group"]


        abcd_chain(0, [tok_fold],
                   mid=lambda: norm_half(0, p.cur("v", "vector")))
        norm_half(1, p.cur("v", "vector"))
        scan_core(0)
        r_apply(0)
        scan0_end = half_tok[0]
        tok_dc0 = scan0_end
        abcd_chain(1, [scan0_end], mid=lambda: norm_half(0, tok_dc0))
        tok_mm1 = p.cur("t", "tensor")
        tok_abcd1_s = p.cur("s", "scalar")
        norm_half(1, norm_mul_tok[0])
        r_apply(1)
        p.wait("gpsimd", tok_mm1)
        p.wait("gpsimd", tok_abcd1_s)
        tok_wf1 = p.op("gpsimd", lambda e, sems: e.dma_start(
            out=wfv[:, 0, :, :], in_=wf_d[:, 0, :, :]), inc="swf", amt=16)
        proj_x2(0, tok_dc0, 0, 4)
        b_stats(0)
        proj_x2(0, tok_dc0, 4, 8)
        scan1_end = None
        b_front(0)
        scan_core(1)
        scan1_end = half_tok[0]
        p.wait("gpsimd", pk1_free[0])
        p.op("gpsimd", lambda e, sems: e.dma_start(
            out=wf1v[:, :, :], in_=wf_d[:, 1, :, :]), inc="swf2", amt=16)
        p.wait("gpsimd", scan1_end)
        tok_wf = p.op("gpsimd", lambda e, sems: e.dma_start(
            out=wfv[:, 2, :, :], in_=wf_d[:, 2, :, :]), inc="swf3", amt=16)

        tok_dc1 = scan1_end
        b_stats(1)
        b_applies(1)
        proj_x2(1, tok_dc1)
        b_stats(2)
        b_stats(3)
        b_applies(2)
        b_applies(3)
        b_f2(0)
        b_front(1)
        b_f3(0)
        b_f2(1)
        b_front(2)
        b_f3(1)
        b_f2(2)
        b_front(3)
        b_f3(2)
        b_f2(3)
        b_f3(3)

        p.op("sync", lambda e, sems: e.wait_ge(sems["sy"], p.val["sy"]))

        with nc.Block() as block:
            @block.sync
            def _(e):
                for f in p.ops["sync"]:
                    f(e, sems)

            @block.gpsimd
            def _(e):
                for f in p.ops["gpsimd"]:
                    f(e, sems)

            @block.tensor
            def _(e):
                for f in p.ops["tensor"]:
                    f(e, sems)

            @block.vector
            def _(e):
                for f in p.ops["vector"]:
                    f(e, sems)

            @block.scalar
            def _(e):
                for f in p.ops["scalar"]:
                    f(e, sems)

    return nc


def make_weights(L0_attn_W_m, L0_attn_v_vec, L0_attn_W_proj, L0_attn_out_scale,
                 L0_ffn_weight, L0_ffn_aux_weight1, L0_ffn_aux_weight2,
                 L0_ffn_bias):
    f32 = np.float32
    Wm = np.asarray(L0_attn_W_m, f32)
    Wp = np.asarray(L0_attn_W_proj, f32)
    vv = np.asarray(L0_attn_v_vec, f32)
    osc = np.asarray(L0_attn_out_scale, f32)
    Wf = [np.asarray(w, f32) for w in
          (L0_ffn_weight, L0_ffn_aux_weight1, L0_ffn_aux_weight2)]
    bias = np.asarray(L0_ffn_bias, f32)

    Wm4 = Wm.reshape(256, 4, 1024)
    ws = np.empty((128, 8, 8, 128), f32)
    for e in range(4):
        for h in range(2):
            blk = Wm4[128 * h:128 * (h + 1), e, :]
            ws[:, e * 2 + h, :, :] = blk.T.reshape(8, 128, 128).transpose(1, 0, 2)
    wp = np.empty((128, 4, 1024), f32)
    for h in range(2):
        for comp in range(2):
            cols = 2 * (128 * h + np.arange(128)) + comp
            wp[:, 2 * comp + h, :] = Wp[:, cols].T
    wf = np.stack([w.reshape(8, 128, 1024).transpose(1, 0, 2) for w in Wf], axis=1)
    sb = np.ascontiguousarray(np.broadcast_to(osc, (128, 1024)))
    bb = np.ascontiguousarray(np.broadcast_to(bias, (128, 1024)))
    vi = np.empty((128, 2, 2), f32)
    for h in range(2):
        vi[:, h, 0] = vv[128 * h:128 * (h + 1), 0, 0]
        vi[:, h, 1] = vv[128 * h:128 * (h + 1), 1, 0]
    idm = np.eye(128, dtype=ml_dtypes.bfloat16)
    return {"w_scan": np.ascontiguousarray(ws), "w_proj": np.ascontiguousarray(wp),
            "w_ffn": np.ascontiguousarray(wf), "scale_b": sb, "bias_b": bb,
            "v_init": vi, "ident": idm}


def kernel(L0_attn_x, L0_attn_W_m, L0_attn_v_vec, L0_attn_W_proj,
           L0_attn_out_scale, L0_ffn_weight, L0_ffn_aux_weight1,
           L0_ffn_aux_weight2, L0_ffn_bias):
    x = np.asarray(L0_attn_x, np.float32)
    wmap = make_weights(L0_attn_W_m, L0_attn_v_vec, L0_attn_W_proj,
                        L0_attn_out_scale, L0_ffn_weight, L0_ffn_aux_weight1,
                        L0_ffn_aux_weight2, L0_ffn_bias)
    nc = build_nc()
    in_maps = []
    for i in range(8):
        m = dict(wmap)
        m["x"] = np.ascontiguousarray(x[i])
        in_maps.append(m)
    res = run_bass_kernel_spmd(nc, in_maps, core_ids=list(range(8)))
    out = np.stack([np.asarray(r["y"], np.float32) for r in res.results], axis=0)
    return out


# version 22
# speedup vs baseline: 1.2592x; 1.0176x over previous
import numpy as np
import ml_dtypes

import concourse.bass as bass
import concourse.mybir as mybir
from concourse.bass_utils import run_bass_kernel_spmd

F32 = mybir.dt.float32
BF16 = mybir.dt.bfloat16
OP = mybir.AluOpType
AF = mybir.ActivationFunctionType

T = 2048
D = 1024
NT = 16
NSC = 2
EPS_LN = 1e-5
EPS_RMS = 1.1920929e-07

ENGINES = ("sync", "gpsimd", "tensor", "vector", "scalar")

A_XNT = 0
A_ABCD = 4096
A_PK1 = 12288
A_PK2 = 16384
A_PK3 = 18432
A_Q = 19456
A_R = 19968
A_SIG = 20992
A_SCALEB = 21760
A_END = 23168
A_WPX = 4096
A_TL1 = 16384
A_TL2 = 18432
A_TL3 = 19456
A_TL4 = 19968
A_TDB = 20992
B_WF = 0
B_LN2T = 16384
B_H1T = 18432
B_H2T = 20480
B_TMP = 12288

COMB = (("a", "a", "a", "b", "c"), ("b", "a", "b", "b", "d"),
        ("c", "c", "a", "d", "c"), ("d", "c", "b", "d", "d"))
NAMES = ("a", "b", "c", "d")


class Plan:
    def __init__(self):
        self.ops = {e: [] for e in ENGINES}
        self.val = {}
        self.last_wait = {}
        self.since_drain = {e: 0 for e in ENGINES}

    def op(self, eng, fn, inc=None, amt=1):
        self.since_drain[eng] += 1
        if inc is not None:
            self.val[inc] = self.val.get(inc, 0) + amt
            v = self.val[inc]

            def wrapped(e, sems, fn=fn, inc=inc, amt=amt):
                fn(e, sems).then_inc(sems[inc], amt)

            self.ops[eng].append(wrapped)
            return (inc, v, eng)
        self.ops[eng].append(fn)
        return None

    def drain(self, eng):
        if self.since_drain[eng] == 0:
            return
        self.since_drain[eng] = 0
        self.ops[eng].append(lambda e, sems: e.drain())

    def wait(self, eng, tok):
        if tok is None:
            return
        sem, v, producer = tok
        if producer == eng:
            self.drain(eng)
            return
        key = (eng, sem)
        if self.last_wait.get(key, -1) >= v:
            return
        self.last_wait[key] = v

        def w(e, sems, sem=sem, v=v):
            e.wait_ge(sems[sem], v)

        self.ops[eng].append(w)

    def cur(self, sem, eng="?"):
        return (sem, self.val.get(sem, 0), eng)


def build_nc():
    nc = bass.Bass("TRN2", target_bir_lowering=False)

    x_d = nc.dram_tensor("x", [T, D], F32, kind="ExternalInput").ap()
    y_d = nc.dram_tensor("y", [T, D], F32, kind="ExternalOutput").ap()
    ws_d = nc.dram_tensor("w_scan", [128, 8, 8, 128], F32, kind="ExternalInput").ap()
    wp_d = nc.dram_tensor("w_proj", [128, 4, 1024], F32, kind="ExternalInput").ap()
    wf_d = nc.dram_tensor("w_ffn", [128, 3, 8, 1024], F32, kind="ExternalInput").ap()
    sb_d = nc.dram_tensor("scale_b", [128, 1024], F32, kind="ExternalInput").ap()
    bb_d = nc.dram_tensor("bias_b", [128, 1024], F32, kind="ExternalInput").ap()
    vi_d = nc.dram_tensor("v_init", [128, 2, 2], F32, kind="ExternalInput").ap()
    id_d = nc.dram_tensor("ident", [128, 128], BF16, kind="ExternalInput").ap()

    p = Plan()
    import contextlib
    stack = contextlib.ExitStack()
    with stack:
        arena_base = (nc._sbuf_addr_for_side("left") + 31) // 32 * 32
        arena = stack.enter_context(nc.sbuf_tensor("arena", [128, A_END], F32))
        x_sb = stack.enter_context(nc.sbuf_tensor("x_sb", [128, NT, D], F32))
        xn_sb = stack.enter_context(nc.sbuf_tensor("xn_sb", [128, 4, D], BF16))
        weh_base = (nc._sbuf_addr_for_side("left") + 31) // 32 * 32
        weh = stack.enter_context(nc.sbuf_tensor("weh", [128, 8, 8, 128], BF16))
        ln2T0 = nc.alloc_sbuf_tensor_at(
            "ln2T0", [128, 8, 512], BF16, offset=weh_base).ap()
        h1T0 = nc.alloc_sbuf_tensor_at(
            "h1T0", [128, 8, 512], BF16, offset=weh_base + 8192).ap()
        h2T0 = nc.alloc_sbuf_tensor_at(
            "h2T0", [128, 8, 512], BF16, offset=weh_base).ap()
        wpb = stack.enter_context(nc.sbuf_tensor("wpb", [128, 4, D], BF16))
        biasb = stack.enter_context(nc.sbuf_tensor("biasb", [128, D], F32))
        vinit = stack.enter_context(nc.sbuf_tensor("vinit", [128, 2, 2], F32))
        id_sb = stack.enter_context(nc.sbuf_tensor("id_sb", [128, 128], BF16))
        stats = stack.enter_context(nc.sbuf_tensor("stats", [128, NT, 2], F32))
        bn6 = stack.enter_context(nc.sbuf_tensor("bn6", [128, 8, 12], F32))
        st_t0 = stack.enter_context(nc.sbuf_tensor("st_t0", [128, NT], F32))
        st_s1 = stack.enter_context(nc.sbuf_tensor("st_s1", [128, NT], F32))
        st_s2 = stack.enter_context(nc.sbuf_tensor("st_s2", [128, NT], F32))
        st_r1 = stack.enter_context(nc.sbuf_tensor("st_r1", [128, NT], F32))
        st_rn = stack.enter_context(nc.sbuf_tensor("st_rn", [128, NT], F32))
        st_bxn = stack.enter_context(nc.sbuf_tensor("st_bxn", [128, NT], F32))
        st_sr = stack.enter_context(nc.sbuf_tensor("st_sr", [128, NT], F32))
        st_br = stack.enter_context(nc.sbuf_tensor("st_br", [128, NT], F32))
        hxybf_base = (nc._sbuf_addr_for_side("left") + 31) // 32 * 32
        hxybf = stack.enter_context(nc.sbuf_tensor("hxybf", [128, 2, 4, D], BF16))
        btmpv = nc.alloc_sbuf_tensor_at(
            "btmpv", [128, 4, 512], F32, offset=hxybf_base).ap()
        wf1v = None
        psum = [stack.enter_context(nc.psum_tensor(f"ps{i}", [128, 512], F32))
                for i in range(8)]
        sems = {}
        for s in ("sx", "sx2", "sx3", "sx4", "sw", "sws", "swl", "swf", "swf2",
                  "swf3", "sy",
                  "t", "v", "s", "g"):
            sems[s] = stack.enter_context(nc.semaphore(f"sem_{s}"))

        def av(off, n):
            return arena[:, off:off + n]

        xnT = nc.alloc_sbuf_tensor_at(
            "xnT", [128, 8, 1024], BF16, offset=arena_base + A_XNT * 4).ap()
        abcd = av(A_ABCD, 8192).rearrange("p (i t) -> p i t", i=8)
        ntmp = av(A_PK1, 4096).rearrange("p (i t) -> p i t", i=4)
        PK1 = av(A_PK1, 4096).rearrange("p (i h t) -> p i h t", i=4, h=2)
        PK2 = av(A_PK2, 2048).rearrange("p (i h t) -> p i h t", i=4, h=2)
        PK3 = av(A_PK3, 1024).rearrange("p (i h t) -> p i h t", i=4, h=2)
        Qv = av(A_Q, 512).rearrange("p (i h t) -> p i h t", i=4, h=2)
        Rv = av(A_R, 1024).rearrange("p (b i h t) -> p b i h t", b=2, i=4, h=2)
        SIG = av(A_SIG, 768).rearrange("p (i h t) -> p i h t", i=6, h=2)
        TL1 = av(A_TL1, 4096).rearrange("p (i h t) -> p i h t", i=4, h=2)
        TL2 = av(A_TL2, 2048).rearrange("p (i h t) -> p i h t", i=4, h=2)
        TL3 = av(A_TL3, 1024).rearrange("p (i h t) -> p i h t", i=4, h=2)
        TL4 = av(A_TL4, 512).rearrange("p (i h t) -> p i h t", i=4, h=2)
        TDB = av(A_TDB, 504).rearrange("p (i t) -> p i t", i=4)
        hxy = av(A_PK1, 4096).rearrange("p (i t) -> p i t", i=4)
        wpx = av(A_WPX, 4096).rearrange("p (i t) -> p i t", i=4)
        scaleb = av(A_SCALEB, 1024)
        wf1v = nc.alloc_sbuf_tensor_at(
            "wf1v", [128, 8, 1024], BF16, offset=arena_base + A_PK1 * 4).ap()
        wfv = nc.alloc_sbuf_tensor_at(
            "wfv", [128, 3, 8, 1024], BF16, offset=arena_base + B_WF * 4).ap()
        xn2_sb = nc.alloc_sbuf_tensor_at(
            "xn2_sb", [128, 8, D], BF16, offset=arena_base + A_ABCD * 4).ap()
        ln2T = nc.alloc_sbuf_tensor_at(
            "ln2T", [128, 8, 512], BF16, offset=arena_base + B_LN2T * 4).ap()
        h1T = nc.alloc_sbuf_tensor_at(
            "h1T", [128, 8, 512], BF16, offset=arena_base + B_H1T * 4).ap()
        h2T = nc.alloc_sbuf_tensor_at(
            "h2T", [128, 8, 512], BF16, offset=arena_base + B_H2T * 4).ap()
        btmp = btmpv

        def vop(fn, inc=None):
            return p.op("vector", fn, inc=inc)

        def sop(fn, inc=None):
            return p.op("scalar", fn, inc=inc)

        TT = OP

        bank_state = [{"evac": None, "group": None} for _ in range(8)]
        bank_rr = [0]

        def mm_group(mms, evac_eng, evac_fn, deps=(), evac_deps=()):
            b = bank_rr[0] % 8
            bank_rr[0] += 1
            st = bank_state[b]
            if st["evac"] is not None:
                p.wait("tensor", st["evac"])
            for d in deps:
                p.wait("tensor", d)
            n = len(mms)
            for i, m in enumerate(mms):
                if isinstance(m, tuple):
                    m, mdep = m
                    p.wait("tensor", mdep)
                if i == n - 1:
                    tok = p.op("tensor", lambda e, sems, m=m, b=b: m(e, sems, psum[b]),
                               inc="t")
                else:
                    p.op("tensor", lambda e, sems, m=m, b=b: m(e, sems, psum[b]))
            st["group"] = tok
            p.wait(evac_eng, tok)
            for d in evac_deps:
                p.wait(evac_eng, d)
            esem = {"scalar": "s", "vector": "v"}[evac_eng]
            if not isinstance(evac_fn, (list, tuple)):
                evac_fn = [evac_fn]
            for fi, ef in enumerate(evac_fn):
                if fi == len(evac_fn) - 1:
                    etok = p.op(evac_eng, lambda e, sems, b=b, ef=ef:
                                ef(e, sems, psum[b]), inc=esem)
                else:
                    p.op(evac_eng, lambda e, sems, b=b, ef=ef: ef(e, sems, psum[b]))
                    p.drain(evac_eng)
            st["evac"] = etok
            return etok

        x_src = x_d.rearrange("(n p) d -> p n d", p=128)
        tok_x_q = []
        for q in range(4):
            if q:
                p.wait("sync", tok_x_q[q - 1])
            tok_x_q.append(p.op("sync", lambda e, sems, q=q: e.dma_start(
                out=x_sb[:, 4 * q:4 * (q + 1), :], in_=x_src[:, 4 * q:4 * (q + 1), :]),
                inc=("sx", "sx2", "sx3", "sx4")[q], amt=16))

        def wdma(out_ap, in_ap, sem="sw"):
            return p.op("gpsimd", lambda e, sems: e.dma_start(out=out_ap, in_=in_ap),
                        inc=sem, amt=16)

        p.wait("gpsimd", tok_x_q[0])
        wdma(id_sb[:, :], id_d)
        wdma(weh[:, :, :, :], ws_d)
        tok_w = p.cur("sw", "gpsimd")
        wdma(wpx[:, :, :], wp_d, sem="sws")
        wdma(scaleb, sb_d, sem="sws")
        tok_ws = p.cur("sws", "gpsimd")
        wdma(biasb[:, :], bb_d, sem="swl")
        wdma(vinit[:, :, :], vi_d, sem="swl")
        tok_wl = p.cur("swl", "gpsimd")
        tok_fold = None

        xn_slot_tok = [None] * 4
        half_tok = [None]
        pk1_free = [None]
        proj_tok = [None, None]
        ln2t_tok = [None]
        h1t_tok = [None]
        h2t_tok = [None]

        def combine(dst, L, R, tmps, extra_layer1=()):
            for (od, f1, f2, g1, g2) in COMB:
                vop(lambda e, sems, od=od, f1=f1, f2=f2: e.tensor_tensor(
                    dst[od], L[f1], R[f2], TT.mult))
                vop(lambda e, sems, od=od, g1=g1, g2=g2: e.tensor_tensor(
                    tmps[od], L[g1], R[g2], TT.mult))
            for fn in extra_layer1:
                vop(fn)
            p.drain("vector")
            for od in NAMES:
                vop(lambda e, sems, od=od: e.tensor_tensor(
                    dst[od], dst[od], tmps[od], TT.add))
            p.drain("vector")

        def pairs(viewmap):
            return {nm: viewmap[nm].rearrange("p (u two) -> p u two", two=2)
                    for nm in NAMES}

        stat_toks = [None, None]
        stat_toks2 = [None, None]
        apply_tok_j = [None] * NT

        def stats_bn(sc):
            j0 = sc * 8
            p.drain("vector")
            for jj in range(8):
                j = j0 + jj
                p.wait("vector", tok_x_q[j // 4])
                vop(lambda e, sems, j=j, jj=jj: e.bn_stats(
                    bn6[:, jj, 0:6], x_sb[:, j, 0:512]))
                vop(lambda e, sems, j=j, jj=jj: e.bn_stats(
                    bn6[:, jj, 6:12], x_sb[:, j, 512:1024]))
            p.drain("vector")
            for jj in range(8):
                vop(lambda e, sems, j=j0 + jj, jj=jj: e.bn_aggr(
                    stats[:, j, :], bn6[:, jj, 0:12]))
            p.drain("vector")
            sl = slice(j0, j0 + 8)
            var = stats[:, sl, 1]
            vop(lambda e, sems, sl=sl, var=var: e.tensor_scalar(
                st_t0[:, sl], var, EPS_LN, None, TT.add))
            tok_v0 = vop(lambda e, sems, sl=sl, var=var: e.tensor_scalar(
                st_s2[:, sl], var, 1.0 + EPS_RMS, EPS_LN * EPS_RMS, TT.mult, TT.add),
                inc="v")
            return tok_v0

        def stats_math(sc, tok_v0):
            j0 = sc * 8
            sl = slice(j0, j0 + 8)
            mean = stats[:, sl, 0]
            p.wait("scalar", tok_v0)
            sop(lambda e, sems, sl=sl: e.activation(
                st_s1[:, sl], st_t0[:, sl], AF.Ln))
            sop(lambda e, sems, sl=sl: e.activation(
                st_s2[:, sl], st_s2[:, sl], AF.Ln))
            p.drain("scalar")
            sop(lambda e, sems, sl=sl: e.activation(
                st_r1[:, sl], st_s1[:, sl], AF.Exp, scale=-0.5))
            tok_e = sop(lambda e, sems, sl=sl: e.activation(
                st_rn[:, sl], st_s2[:, sl], AF.Exp, scale=-0.5), inc="s")
            p.wait("vector", tok_e)
            tok_stat = vop(lambda e, sems, sl=sl, mean=mean: e.scalar_tensor_tensor(
                st_bxn[:, sl], mean, -1.0, st_rn[:, sl], TT.mult, TT.mult), inc="v")
            vop(lambda e, sems, sl=sl: e.tensor_scalar(
                st_sr[:, sl], st_r1[:, sl], 1.0, None, TT.add))
            tok_stat2 = vop(lambda e, sems, sl=sl, mean=mean: e.scalar_tensor_tensor(
                st_br[:, sl], mean, -1.0, st_r1[:, sl], TT.mult, TT.mult), inc="v")
            stat_toks[sc] = tok_stat
            stat_toks2[sc] = tok_stat2

        stats_math(0, stats_bn(0))
        p.wait("vector", tok_ws)
        for i in range(4):
            tok_fold = vop(lambda e, sems, i=i: e.tensor_tensor(
                wpb[:, i, :], wpx[:, i, :], scaleb, TT.mult), inc="v")
        tok_v0_sc1 = stats_bn(1)

        def abcd_chain(sc, extra_evac_deps, mid=None):
            j0 = sc * 8
            ee = "vector" if sc == 0 else "scalar"
            tr_tok = {}
            for cc in range(2):
                for k in range(4):
                    j = j0 + cc * 4 + k
                    p.wait("scalar", stat_toks[sc])
                    if xn_slot_tok[k] is not None:
                        p.wait("scalar", xn_slot_tok[k])
                    atok = sop(lambda e, sems, j=j, k=k: e.activation(
                        xn_sb[:, k, :], x_sb[:, j, :], AF.Identity,
                        bias=st_bxn[:, j:j + 1], scale=st_rn[:, j:j + 1]), inc="s")
                    apply_tok_j[j] = atok
                    for gh in range(2):
                        def mk_mm(k, g):
                            def m(e, sems, out):
                                return e.matmul(
                                    out[:, 128 * (g % 4):128 * (g % 4 + 1)],
                                    xn_sb[:, k, 128 * g:128 * (g + 1)],
                                    id_sb[:, :],
                                    start=(g % 4 == 0), stop=(g % 4 == 3))
                            return m
                        def tr_evac(e, sems, bank, k=k, cc=cc, gh=gh):
                            dst = xnT[:, 4 * gh:4 * gh + 4,
                                      128 * (4 * cc + k):128 * (4 * cc + k + 1)]
                            if sc == 0:
                                return e.tensor_copy(dst, bank[:, :])
                            return e.activation(dst, bank[:, :], AF.Copy)
                        tok = mm_group(
                            [mk_mm(k, g) for g in range(4 * gh, 4 * gh + 4)],
                            ee, tr_evac, deps=[atok, tok_w])
                        tr_tok[(cc, gh)] = tok
                    xn_slot_tok[k] = bank_state[(bank_rr[0] - 1) % 8]["group"]

            def mm_i(i):
                for cc in range(2):
                    def mk_mm(g, i, cc):
                        def m(e, sems, out):
                            return e.matmul(out[:, :], weh[:, i, g, :],
                                            xnT[:, g, 512 * cc:512 * (cc + 1)],
                                            start=(g == 0), stop=(g == 7))
                        return m
                    def ab_evac(e, sems, bank, i=i, cc=cc):
                        dst = abcd[:, i, 512 * cc:512 * (cc + 1)]
                        if sc == 0:
                            return e.tensor_copy(dst, bank[:, :])
                        return e.activation(dst, bank[:, :], AF.Copy)
                    mm_group(
                        [(mk_mm(g, i, cc), tr_tok[(cc, g // 4)]) for g in range(8)],
                        ee, ab_evac,
                        deps=[tok_w], evac_deps=list(extra_evac_deps))

            for i in (0, 2, 4, 6):
                mm_i(i)
            if mid is not None:
                mid()
            for i in (1, 3, 5, 7):
                mm_i(i)

        norm_mul_tok = [None]

        def norm_half(h, gate):
            A, B, C, Dt = (abcd[:, 0 + h, :], abcd[:, 2 + h, :],
                           abcd[:, 4 + h, :], abcd[:, 6 + h, :])
            p.drain("scalar")
            if gate is not None:
                p.wait("scalar", gate)
            for ti, M_ in enumerate((A, B, C, Dt)):
                tok_sq = sop(lambda e, sems, M_=M_, ti=ti: e.activation(
                    ntmp[:, ti, :], M_, AF.Square), inc="s")
            p.wait("vector", tok_sq)
            vop(lambda e, sems: e.tensor_tensor(
                ntmp[:, 0, :], ntmp[:, 0, :], ntmp[:, 1, :], TT.add))
            vop(lambda e, sems: e.tensor_tensor(
                ntmp[:, 2, :], ntmp[:, 2, :], ntmp[:, 3, :], TT.add))
            p.drain("vector")
            vop(lambda e, sems: e.tensor_tensor(
                ntmp[:, 0, :], ntmp[:, 0, :], ntmp[:, 2, :], TT.add))
            p.drain("vector")
            tok_clip = vop(lambda e, sems: e.tensor_scalar(
                ntmp[:, 0, :], ntmp[:, 0, :], 1e-12, None, TT.max), inc="v")
            p.wait("scalar", tok_clip)
            sop(lambda e, sems: e.activation(
                ntmp[:, 1, :], ntmp[:, 0, :], AF.Ln))
            p.drain("scalar")
            tok_s = sop(lambda e, sems: e.activation(
                ntmp[:, 0, :], ntmp[:, 1, :], AF.Exp, scale=-0.5), inc="s")
            p.wait("vector", tok_s)
            for mi, M_ in enumerate((A, B, C, Dt)):
                if mi == 3:
                    norm_mul_tok[0] = vop(lambda e, sems, M_=M_: e.tensor_tensor(
                        M_, M_, ntmp[:, 0, :], TT.mult), inc="v")
                else:
                    vop(lambda e, sems, M_=M_: e.tensor_tensor(
                        M_, M_, ntmp[:, 0, :], TT.mult))
            p.drain("vector")

        def scan_core(sc):
            def epack(n):
                out = {}
                for ei, nm in enumerate(NAMES):
                    out[nm] = abcd[:, 2 * ei:2 * ei + 2, :].rearrange(
                        "p h (u two) -> p h u two", two=2)
                return out

            pr = epack(512)
            combine({nm: PK1[:, ei, :, :] for ei, nm in enumerate(NAMES)},
                    {nm: pr[nm][:, :, :, 1] for nm in NAMES},
                    {nm: pr[nm][:, :, :, 0] for nm in NAMES},
                    {nm: TL1[:, ei, :, :] for ei, nm in enumerate(NAMES)})
            pr = {nm: PK1[:, ei, :, :].rearrange("p h (u two) -> p h u two", two=2)
                  for ei, nm in enumerate(NAMES)}
            combine({nm: PK2[:, ei, :, :] for ei, nm in enumerate(NAMES)},
                    {nm: pr[nm][:, :, :, 1] for nm in NAMES},
                    {nm: pr[nm][:, :, :, 0] for nm in NAMES},
                    {nm: TL2[:, ei, :, :] for ei, nm in enumerate(NAMES)})
            pr = {nm: PK2[:, ei, :, :].rearrange("p h (u two) -> p h u two", two=2)
                  for ei, nm in enumerate(NAMES)}
            combine({nm: PK3[:, ei, :, :] for ei, nm in enumerate(NAMES)},
                    {nm: pr[nm][:, :, :, 1] for nm in NAMES},
                    {nm: pr[nm][:, :, :, 0] for nm in NAMES},
                    {nm: TL3[:, ei, :, :] for ei, nm in enumerate(NAMES)})
            pr = {nm: PK3[:, ei, :, :].rearrange("p h (u two) -> p h u two", two=2)
                  for ei, nm in enumerate(NAMES)}
            combine({nm: Qv[:, ei, :, :] for ei, nm in enumerate(NAMES)},
                    {nm: pr[nm][:, :, :, 1] for nm in NAMES},
                    {nm: pr[nm][:, :, :, 0] for nm in NAMES},
                    {nm: TL4[:, ei, :, :] for ei, nm in enumerate(NAMES)})

            cur, nxt = 0, 1
            first = True
            s = 1
            while s < 64:
                srcv = {nm: (Qv[:, ei, :, :] if first else Rv[:, cur, ei, :, :])
                        for ei, nm in enumerate(NAMES)}
                dstv = {nm: Rv[:, nxt, ei, :, :] for ei, nm in enumerate(NAMES)}
                heads = [
                    (lambda e, sems, d=dstv[nm], sv=srcv[nm], s=s:
                     e.tensor_copy(d[:, :, 0:s], sv[:, :, 0:s])) for nm in NAMES]
                combine({nm: dstv[nm][:, :, s:64] for nm in NAMES},
                        {nm: srcv[nm][:, :, s:64] for nm in NAMES},
                        {nm: srcv[nm][:, :, 0:64 - s] for nm in NAMES},
                        {nm: TDB[:, ei, 0:2 * (64 - s)].rearrange(
                            "p (h t) -> p h t", h=2)
                         for ei, nm in enumerate(NAMES)},
                        extra_layer1=heads)
                cur, nxt = nxt, cur
                first = False
                s *= 2
            Rf = {nm: Rv[:, cur, ei, :, :] for ei, nm in enumerate(NAMES)}

            p.wait("vector", tok_wl)
            cxb = vinit[:, :, 0:1].broadcast_to([128, 2, 64])
            cyb = vinit[:, :, 1:2].broadcast_to([128, 2, 64])
            vop(lambda e, sems, Rf=Rf, cxb=cxb: e.tensor_tensor(
                SIG[:, 0, :, :], Rf["a"], cxb, TT.mult))
            vop(lambda e, sems, Rf=Rf, cyb=cyb: e.tensor_tensor(
                SIG[:, 4, :, :], Rf["b"], cyb, TT.mult))
            vop(lambda e, sems, Rf=Rf, cxb=cxb: e.tensor_tensor(
                SIG[:, 1, :, :], Rf["c"], cxb, TT.mult))
            vop(lambda e, sems, Rf=Rf, cyb=cyb: e.tensor_tensor(
                SIG[:, 5, :, :], Rf["d"], cyb, TT.mult))
            p.drain("vector")
            vop(lambda e, sems: e.tensor_tensor(
                SIG[:, 0, :, :], SIG[:, 0, :, :], SIG[:, 4, :, :], TT.add))
            vop(lambda e, sems: e.tensor_tensor(
                SIG[:, 1, :, :], SIG[:, 1, :, :], SIG[:, 5, :, :], TT.add))
            p.drain("vector")
            vop(lambda e, sems: e.tensor_copy(SIG[:, 2, :, 1:64], SIG[:, 0, :, 0:63]))
            vop(lambda e, sems: e.tensor_copy(SIG[:, 2, :, 0:1], vinit[:, :, 0:1]))
            vop(lambda e, sems: e.tensor_copy(SIG[:, 3, :, 1:64], SIG[:, 1, :, 0:63]))
            vop(lambda e, sems: e.tensor_copy(SIG[:, 3, :, 0:1], vinit[:, :, 1:2]))
            p.drain("vector")
            vop(lambda e, sems: e.tensor_copy(vinit[:, :, 0:1], SIG[:, 0, :, 63:64]))
            vop(lambda e, sems: e.tensor_copy(vinit[:, :, 1:2], SIG[:, 1, :, 63:64]))
            p.drain("vector")

            archx = av(A_PK2, 1152).rearrange("p (j h u) -> p j h u", j=9, h=2)
            archy = av(A_PK2 + 1152, 1152).rearrange("p (j h u) -> p j h u",
                                                     j=9, h=2)
            etmp1 = av(A_R, 1024)
            etmp2 = av(18688, 1024)
            PRV = {nm: PK1[:, ei, :, :].rearrange("p h (u j) -> p h u j", j=8)
                   for ei, nm in enumerate(NAMES)}
            vop(lambda e, sems: e.tensor_copy(archx[:, 0, :, :], SIG[:, 2, :, :]))
            vop(lambda e, sems: e.tensor_copy(archy[:, 0, :, :], SIG[:, 3, :, :]))
            p.drain("vector")
            for j in range(8):
                px = archx[:, j, :, :]
                py = archy[:, j, :, :]
                vop(lambda e, sems, j=j, px=px: e.tensor_tensor(
                    SIG[:, 4, :, :], PRV["a"][:, :, :, j], px, TT.mult))
                vop(lambda e, sems, j=j, px=px: e.tensor_tensor(
                    SIG[:, 5, :, :], PRV["c"][:, :, :, j], px, TT.mult))
                vop(lambda e, sems, j=j, py=py: e.tensor_tensor(
                    archx[:, j + 1, :, :], PRV["b"][:, :, :, j], py, TT.mult))
                vop(lambda e, sems, j=j, py=py: e.tensor_tensor(
                    archy[:, j + 1, :, :], PRV["d"][:, :, :, j], py, TT.mult))
                p.drain("vector")
                vop(lambda e, sems, j=j: e.tensor_tensor(
                    archx[:, j + 1, :, :], archx[:, j + 1, :, :],
                    SIG[:, 4, :, :], TT.add))
                if j == 7:
                    pk1_free[0] = vop(lambda e, sems, j=j: e.tensor_tensor(
                        archy[:, j + 1, :, :], archy[:, j + 1, :, :],
                        SIG[:, 5, :, :], TT.add), inc="v")
                else:
                    vop(lambda e, sems, j=j: e.tensor_tensor(
                        archy[:, j + 1, :, :], archy[:, j + 1, :, :],
                        SIG[:, 5, :, :], TT.add))
                p.drain("vector")
            bk = sc % 2
            HBx = hxybf[:, bk, 0:2, :].rearrange("p h (u r) -> p h u r", r=16)
            HBy = hxybf[:, bk, 2:4, :].rearrange("p h (u r) -> p h u r", r=16)
            arx = archx[:, 1:9, :, :].rearrange("p j h u -> p h u j")
            ary = archy[:, 1:9, :, :].rearrange("p j h u -> p h u j")
            vop(lambda e, sems: e.tensor_copy(HBx[:, :, :, 1:16:2], arx))
            vop(lambda e, sems: e.tensor_copy(HBy[:, :, :, 1:16:2], ary))
            Aev = {nm: abcd[:, 2 * ei:2 * ei + 2, :].rearrange(
                "p h (u r) -> p h u r", r=16)[:, :, :, 0:16:2]
                for ei, nm in enumerate(NAMES)}
            hpx = archx[:, 0:8, :, :].rearrange("p j h u -> p h u j")
            hpy = archy[:, 0:8, :, :].rearrange("p j h u -> p h u j")
            e1v = etmp1.rearrange("p (h u j) -> p h u j", h=2, u=64)
            e2v = etmp2.rearrange("p (h u j) -> p h u j", h=2, u=64)
            vop(lambda e, sems: e.tensor_tensor(e1v, Aev["a"], hpx, TT.mult))
            vop(lambda e, sems: e.tensor_tensor(e2v, Aev["b"], hpy, TT.mult))
            p.drain("vector")
            vop(lambda e, sems: e.tensor_tensor(
                HBx[:, :, :, 0:16:2], e1v, e2v, TT.add))
            p.drain("vector")
            vop(lambda e, sems: e.tensor_tensor(e1v, Aev["c"], hpx, TT.mult))
            vop(lambda e, sems: e.tensor_tensor(e2v, Aev["d"], hpy, TT.mult))
            p.drain("vector")
            half_tok[0] = vop(lambda e, sems: e.tensor_tensor(
                HBy[:, :, :, 0:16:2], e1v, e2v, TT.add), inc="v")
            p.drain("vector")

        def r_apply(sc):
            for jj in range(8):
                j = sc * 8 + jj
                p.wait("scalar", stat_toks2[sc])
                p.wait("scalar", apply_tok_j[j])
                apply_tok_j[j] = sop(lambda e, sems, j=j: e.activation(
                    x_sb[:, j, :], x_sb[:, j, :], AF.Identity,
                    bias=st_br[:, j:j + 1], scale=st_sr[:, j:j + 1]), inc="s")

        def proj_x2(sc, tok_dc, jlo=0, jhi=8):
            j0 = sc * 8
            for jj in range(jlo, jhi):
                j = j0 + jj
                for ec in range(2):
                    def mk_mm(i, jj, ec, bk=sc % 2):
                        def m(e, sems, out):
                            return e.matmul(out[:, :],
                                            hxybf[:, bk, i, 128 * jj:128 * (jj + 1)],
                                            wpb[:, i, 512 * ec:512 * (ec + 1)],
                                            start=(i == 0), stop=(i == 3))
                        return m

                    def ev2(e, sems, bank, j=j, ec=ec):
                        return e.tensor_tensor(
                            x_sb[:, j, 512 * ec:512 * (ec + 1)],
                            x_sb[:, j, 512 * ec:512 * (ec + 1)],
                            bank[:, :], TT.add)
                    mm_group([mk_mm(i, jj, ec) for i in range(2 * 2)],
                             "vector", ev2,
                             deps=[tok_dc, tok_fold],
                             evac_deps=[apply_tok_j[j]])
            proj_tok[sc] = bank_state[(bank_rr[0] - 1) % 8]["group"]

        btmp_tok = [None, None]
        bstat_toks = [None] * 4

        def b_stats(c):
            j0 = c * 4
            sl = slice(j0, j0 + 4)
            p.drain("vector")
            for k in range(4):
                j = j0 + k
                vop(lambda e, sems, j=j, k=k: e.bn_stats(
                    bn6[:, k, 0:6], x_sb[:, j, 0:512]))
                vop(lambda e, sems, j=j, k=k: e.bn_stats(
                    bn6[:, k, 6:12], x_sb[:, j, 512:1024]))
            p.drain("vector")
            for k in range(4):
                vop(lambda e, sems, j=j0 + k, k=k: e.bn_aggr(
                    stats[:, j, :], bn6[:, k, 0:12]))
            p.drain("vector")
            mean = stats[:, sl, 0]
            var = stats[:, sl, 1]
            tok_v0 = vop(lambda e, sems, sl=sl, var=var: e.tensor_scalar(
                st_t0[:, sl], var, EPS_LN, None, TT.add), inc="v")
            p.wait("scalar", tok_v0)
            sop(lambda e, sems, sl=sl: e.activation(
                st_s1[:, sl], st_t0[:, sl], AF.Ln))
            p.drain("scalar")
            tok_s0 = sop(lambda e, sems, sl=sl: e.activation(
                st_r1[:, sl], st_s1[:, sl], AF.Exp, scale=-0.5), inc="s")
            p.wait("vector", tok_s0)
            bstat_toks[c] = vop(lambda e, sems, sl=sl, mean=mean:
                                e.scalar_tensor_tensor(
                st_bxn[:, sl], mean, -1.0, st_r1[:, sl], TT.mult, TT.mult), inc="v")

        btr = {}
        bf1 = {}
        bf2 = {}

        b_apply = {}

        def b_applies(c):
            j0 = c * 4
            if c in (1, 2):
                xnt, base = xn2_sb, (c - 1) * 4
            else:
                xnt, base = xn_sb, 0
            apply_toks = []
            for k in range(4):
                j = j0 + k
                p.wait("scalar", bstat_toks[c])
                if c in (1, 2):
                    p.wait("scalar", scan1_end)
                elif xn_slot_tok[k] is not None:
                    p.wait("scalar", xn_slot_tok[k])
                tok = sop(lambda e, sems, j=j, k=k, xnt=xnt, base=base:
                          e.activation(
                    xnt[:, base + k, :], x_sb[:, j, :], AF.Identity,
                    bias=st_bxn[:, j:j + 1], scale=st_r1[:, j:j + 1]), inc="s")
                apply_toks.append(tok)
            b_apply[c] = (xnt, base, apply_toks)

        def b_front(c):
            ln2Tv = ln2T0 if c == 0 else ln2T
            h1Tv = h1T0 if c == 0 else h1T
            front_gate = tok_mm1 if c == 0 else scan1_end
            if c not in b_apply:
                b_applies(c)
            xnt, base, apply_toks = b_apply[c]
            tr_toks = []
            for g in range(8):
                def mk_mm(k, g, xnt=xnt, base=base):
                    def m(e, sems, out):
                        return e.matmul(out[:, 128 * k:128 * (k + 1)],
                                        xnt[:, base + k, 128 * g:128 * (g + 1)],
                                        id_sb[:, :], start=(k == 0), stop=(k == 3))
                    return m
                evac_deps = [front_gate]
                if ln2t_tok[0] is not None:
                    evac_deps.append(ln2t_tok[0])
                tok = mm_group([mk_mm(k, g) for k in range(4)], "scalar",
                               lambda e, sems, bank, g=g, ln2Tv=ln2Tv:
                               e.activation(
                                   ln2Tv[:, g, :], bank[:, :], AF.Copy),
                               deps=apply_toks, evac_deps=evac_deps)
                tr_toks.append(tok)
            if xnt is xn_sb:
                last_tr_group = bank_state[(bank_rr[0] - 1) % 8]["group"]
                for k in range(4):
                    xn_slot_tok[k] = last_tr_group
            btr[c] = tr_toks

            f1_toks = []
            for E in range(8):
                def mk_mm(g, E, ln2Tv=ln2Tv):
                    def m(e, sems, out):
                        return e.matmul(out[:, :],
                                        wfv[:, 0, g, 128 * E:128 * (E + 1)],
                                        ln2Tv[:, g, :], start=(g == 0),
                                        stop=(g == 7))
                    return m
                evac_deps = [front_gate]
                if h1t_tok[0] is not None:
                    evac_deps.append(h1t_tok[0])
                tok = mm_group([(mk_mm(g, E), tr_toks[g]) for g in range(8)],
                               "scalar",
                               lambda e, sems, bank, E=E, h1Tv=h1Tv:
                               e.activation(
                                   h1Tv[:, E, :], bank[:, :], AF.Copy),
                               deps=[tok_wf1], evac_deps=evac_deps)
                f1_toks.append(tok)
            ln2t_tok[0] = bank_state[(bank_rr[0] - 1) % 8]["group"]
            bf1[c] = f1_toks

        def b_f2(c):
            h1Tv = h1T0 if c == 0 else h1T
            h2Tv = h2T0 if c == 0 else h2T
            f2_toks = []
            for E in range(8):
                def mk_mm(g, E, h1Tv=h1Tv):
                    def m(e, sems, out):
                        return e.matmul(out[:, :],
                                        wf1v[:, g, 128 * E:128 * (E + 1)],
                                        h1Tv[:, g, :], start=(g == 0),
                                        stop=(g == 7))
                    return m
                if c == 0:
                    evac_deps = [ln2t_tok[0]]
                else:
                    evac_deps = [scan1_end]
                if h2t_tok[0] is not None:
                    evac_deps.append(h2t_tok[0])
                tok = mm_group([(mk_mm(g, E), bf1[c][g]) for g in range(8)],
                               "scalar",
                               lambda e, sems, bank, E=E, h2Tv=h2Tv:
                               e.activation(
                                   h2Tv[:, E, :], bank[:, :], AF.Copy),
                               deps=[("swf2", 16, "gpsimd")],
                               evac_deps=evac_deps)
                f2_toks.append(tok)
            h1t_tok[0] = bank_state[(bank_rr[0] - 1) % 8]["group"]
            bf2[c] = f2_toks

        def b_f3(c):
            j0 = c * 4
            h2Tv = h2T0 if c == 0 else h2T
            for k in range(4):
                j = j0 + k
                y_toks = []
                for ec in range(2):
                    def mk_mm(g, k, ec, h2Tv=h2Tv):
                        def m(e, sems, out):
                            return e.matmul(out[:, :],
                                            h2Tv[:, g, 128 * k:128 * (k + 1)],
                                            wfv[:, 2, g, 512 * ec:512 * (ec + 1)],
                                            start=(g == 0), stop=(g == 7))
                        return m

                    def evac(e, sems, bank, ec=ec):
                        return e.tensor_tensor(
                            btmp[:, 2 * ec, :], bank[:, :],
                            biasb[:, 512 * ec:512 * (ec + 1)], TT.add)
                    if btmp_tok[ec] is not None:
                        evac_deps_b = [btmp_tok[ec]]
                    else:
                        evac_deps_b = [proj_tok[0], tok_wl]
                    tok_b = mm_group([(mk_mm(g, k, ec), bf2[c][g])
                                      for g in range(8)],
                                     "vector", evac,
                                     deps=[("swf3", 16, "gpsimd")],
                                     evac_deps=evac_deps_b)
                    p.wait("scalar", tok_b)
                    tok_tanh = sop(lambda e, sems, ec=ec: e.activation(
                        btmp[:, 2 * ec + 1, :], btmp[:, 2 * ec, :], AF.Tanh),
                        inc="s")
                    btmp_tok[ec] = tok_tanh
                    p.wait("vector", tok_tanh)
                    tok_y = vop(lambda e, sems, j=j, ec=ec: e.tensor_tensor(
                        x_sb[:, j, 512 * ec:512 * (ec + 1)],
                        x_sb[:, j, 512 * ec:512 * (ec + 1)],
                        btmp[:, 2 * ec + 1, :], TT.add), inc="v")
                    y_toks.append(tok_y)
                    p.wait("scalar", tok_y)
                for tk in y_toks:
                    p.wait("sync", tk)
                p.op("sync", lambda e, sems, j=j: e.dma_start(
                    out=y_d[128 * j:128 * (j + 1), :], in_=x_sb[:, j, :]),
                    inc="sy", amt=16)
            h2t_tok[0] = bank_state[(bank_rr[0] - 1) % 8]["group"]


        abcd_chain(0, [tok_fold],
                   mid=lambda: norm_half(0, p.cur("v", "vector")))
        stats_math(1, tok_v0_sc1)
        norm_half(1, p.cur("v", "vector"))
        scan_core(0)
        r_apply(0)
        scan0_end = half_tok[0]
        tok_dc0 = scan0_end
        abcd_chain(1, [scan0_end], mid=lambda: norm_half(0, tok_dc0))
        tok_mm1 = p.cur("t", "tensor")
        tok_abcd1_s = p.cur("s", "scalar")
        norm_half(1, norm_mul_tok[0])
        r_apply(1)
        p.wait("gpsimd", tok_mm1)
        p.wait("gpsimd", tok_abcd1_s)
        tok_wf1 = p.op("gpsimd", lambda e, sems: e.dma_start(
            out=wfv[:, 0, :, :], in_=wf_d[:, 0, :, :]), inc="swf", amt=16)
        proj_x2(0, tok_dc0, 0, 4)
        b_stats(0)
        proj_x2(0, tok_dc0, 4, 8)
        scan1_end = None
        b_front(0)
        scan_core(1)
        scan1_end = half_tok[0]
        p.wait("gpsimd", pk1_free[0])
        p.op("gpsimd", lambda e, sems: e.dma_start(
            out=wf1v[:, :, :], in_=wf_d[:, 1, :, :]), inc="swf2", amt=16)
        p.wait("gpsimd", scan1_end)
        tok_wf = p.op("gpsimd", lambda e, sems: e.dma_start(
            out=wfv[:, 2, :, :], in_=wf_d[:, 2, :, :]), inc="swf3", amt=16)

        tok_dc1 = scan1_end
        b_stats(1)
        b_applies(1)
        proj_x2(1, tok_dc1)
        b_stats(2)
        b_stats(3)
        b_applies(2)
        b_applies(3)
        b_f2(0)
        b_front(1)
        b_f3(0)
        b_f2(1)
        b_front(2)
        b_f3(1)
        b_f2(2)
        b_front(3)
        b_f3(2)
        b_f2(3)
        b_f3(3)

        p.op("sync", lambda e, sems: e.wait_ge(sems["sy"], p.val["sy"]))

        with nc.Block() as block:
            @block.sync
            def _(e):
                for f in p.ops["sync"]:
                    f(e, sems)

            @block.gpsimd
            def _(e):
                for f in p.ops["gpsimd"]:
                    f(e, sems)

            @block.tensor
            def _(e):
                for f in p.ops["tensor"]:
                    f(e, sems)

            @block.vector
            def _(e):
                for f in p.ops["vector"]:
                    f(e, sems)

            @block.scalar
            def _(e):
                for f in p.ops["scalar"]:
                    f(e, sems)

    return nc


def make_weights(L0_attn_W_m, L0_attn_v_vec, L0_attn_W_proj, L0_attn_out_scale,
                 L0_ffn_weight, L0_ffn_aux_weight1, L0_ffn_aux_weight2,
                 L0_ffn_bias):
    f32 = np.float32
    Wm = np.asarray(L0_attn_W_m, f32)
    Wp = np.asarray(L0_attn_W_proj, f32)
    vv = np.asarray(L0_attn_v_vec, f32)
    osc = np.asarray(L0_attn_out_scale, f32)
    Wf = [np.asarray(w, f32) for w in
          (L0_ffn_weight, L0_ffn_aux_weight1, L0_ffn_aux_weight2)]
    bias = np.asarray(L0_ffn_bias, f32)

    Wm4 = Wm.reshape(256, 4, 1024)
    ws = np.empty((128, 8, 8, 128), f32)
    for e in range(4):
        for h in range(2):
            blk = Wm4[128 * h:128 * (h + 1), e, :]
            ws[:, e * 2 + h, :, :] = blk.T.reshape(8, 128, 128).transpose(1, 0, 2)
    wp = np.empty((128, 4, 1024), f32)
    for h in range(2):
        for comp in range(2):
            cols = 2 * (128 * h + np.arange(128)) + comp
            wp[:, 2 * comp + h, :] = Wp[:, cols].T
    wf = np.stack([w.reshape(8, 128, 1024).transpose(1, 0, 2) for w in Wf], axis=1)
    sb = np.ascontiguousarray(np.broadcast_to(osc, (128, 1024)))
    bb = np.ascontiguousarray(np.broadcast_to(bias, (128, 1024)))
    vi = np.empty((128, 2, 2), f32)
    for h in range(2):
        vi[:, h, 0] = vv[128 * h:128 * (h + 1), 0, 0]
        vi[:, h, 1] = vv[128 * h:128 * (h + 1), 1, 0]
    idm = np.eye(128, dtype=ml_dtypes.bfloat16)
    return {"w_scan": np.ascontiguousarray(ws), "w_proj": np.ascontiguousarray(wp),
            "w_ffn": np.ascontiguousarray(wf), "scale_b": sb, "bias_b": bb,
            "v_init": vi, "ident": idm}


def kernel(L0_attn_x, L0_attn_W_m, L0_attn_v_vec, L0_attn_W_proj,
           L0_attn_out_scale, L0_ffn_weight, L0_ffn_aux_weight1,
           L0_ffn_aux_weight2, L0_ffn_bias):
    x = np.asarray(L0_attn_x, np.float32)
    wmap = make_weights(L0_attn_W_m, L0_attn_v_vec, L0_attn_W_proj,
                        L0_attn_out_scale, L0_ffn_weight, L0_ffn_aux_weight1,
                        L0_ffn_aux_weight2, L0_ffn_bias)
    nc = build_nc()
    in_maps = []
    for i in range(8):
        m = dict(wmap)
        m["x"] = np.ascontiguousarray(x[i])
        in_maps.append(m)
    res = run_bass_kernel_spmd(nc, in_maps, core_ids=list(range(8)))
    out = np.stack([np.asarray(r["y"], np.float32) for r in res.results], axis=0)
    return out
